# version 16
# baseline (speedup 1.0000x reference)
import numpy as np
from contextlib import ExitStack

import concourse.bass as bass
import concourse.mybir as mybir
from concourse.bass_utils import run_bass_kernel_spmd

F32 = mybir.dt.float32
BF16 = mybir.dt.bfloat16
I32 = mybir.dt.int32
I8 = mybir.dt.uint8
AF = mybir.ActivationFunctionType
ALU = mybir.AluOpType
AX = mybir.AxisListType

NCORES = 8
P = 128


class CFG:
    def __init__(self, S=2048, D=2048, NH=16, I0=5504, I1=5632):
        self.S, self.D, self.NH, self.I0, self.I1 = S, D, NH, I0, I1
        self.NT = S // P
        self.ND = D // P
        self.HD = D // NH
        assert self.HD == 128, "builders assume head_dim == 128"
        self.NO0 = I0 // P
        self.NO1 = I1 // P
        self.S2 = S // 2
        self.NT2 = self.NT // 2
        self.W = min(512, S)
        self.WD = min(512, D)
        self.SCALE = float(self.HD) ** -0.5


FULL = CFG()



class Sched:
    ENGINES = ("tensor", "vector", "scalar", "sync", "gpsimd")

    def __init__(self, nc):
        self.nc = nc
        self.items = {e: [] for e in self.ENGINES}
        self.cnt = {}
        self.default_waits = []

    def tokv(self, sem):
        return (sem, self.cnt.get(sem, 0))

    def op(self, eng, emit, waits=(), sem=None, inc=1):
        sem = sem or ("c_" + eng)
        val = self.cnt.get(sem, 0) + inc
        self.cnt[sem] = val
        ws = tuple((s, v) for (s, v) in merge(list(waits), self.default_waits) if v > 0)
        self.items[eng].append((ws, emit, sem, inc))
        return (sem, val)

    def dma(self, eng, out, in_, waits=(), sem=None):
        assert sem is not None
        return self.op(eng, lambda e, o=out, i=in_: e.dma_start(o, i), waits, sem, 16)

    def emit(self, stack):
        names = {}
        for items in self.items.values():
            for ws, _, sem, _ in items:
                for s, _ in ws:
                    names.setdefault(s, None)
                names.setdefault(sem, None)
        for n in list(names):
            names[n] = stack.enter_context(self.nc.semaphore(n))
        blk = stack.enter_context(self.nc.Block())
        for eng in self.ENGINES:
            if not self.items[eng]:
                continue

            def make(eng):
                def body(e):
                    maxw = {}
                    for ws, emit, sem, inc in self.items[eng]:
                        for (wsem, wval) in ws:
                            if wval > maxw.get(wsem, 0):
                                e.wait_ge(names[wsem], wval)
                                maxw[wsem] = wval
                        inst = emit(e)
                        if inst is not None:
                            inst.then_inc(names[sem], inc)
                return body
            getattr(blk, eng)(make(eng))


def merge(*toks):
    best = {}
    for t in toks:
        if not t:
            continue
        if isinstance(t, tuple) and len(t) == 2 and isinstance(t[0], str):
            t = [t]
        for (s, v) in t:
            if v > best.get(s, 0):
                best[s] = v
    return list(best.items())


class Core:
    def __init__(self, cfg):
        self.cfg = cfg
        self.nc = bass.Bass("TRN2", target_bir_lowering=False)
        self.S = Sched(self.nc)
        self.stack = ExitStack()
        self.thunks = []

    def pump(self, n=1):
        for _ in range(min(n, len(self.thunks))):
            self.thunks.pop(0)()

    def drain(self):
        self.pump(len(self.thunks))

    def sbuf(self, name, shape, dtype):
        return self.stack.enter_context(self.nc.sbuf_tensor(name, list(shape), dtype))

    def psum(self, name, shape, dtype):
        return self.stack.enter_context(self.nc.psum_tensor(name, list(shape), dtype))

    def din(self, name, shape, dtype):
        return self.nc.dram_tensor(name, list(shape), dtype, kind="ExternalInput").ap()

    def dout(self, name, shape, dtype):
        return self.nc.dram_tensor(name, list(shape), dtype, kind="ExternalOutput").ap()

    def dtmp(self, name, shape, dtype):
        return self.nc.dram_tensor(name, list(shape), dtype, kind="Internal").ap()

    CONSTS = ()

    def setup_common(self, need_ones=False, need_tri=False):
        cfg = self.cfg
        base = [1e-5, 1.0 / cfg.D, -0.5, cfg.SCALE, 128.0]
        for j in range(1, 11):
            base.append(-128.0 * (j - 0.5))
        self.CONSTS = tuple(base)
        toks = []
        n = len(self.CONSTS)
        cvals = self.din("cvals", (n,), F32)
        self.cpool = self.sbuf("cpool", (P, n), F32)
        toks.append(self.S.dma("sync", self.cpool[:],
                               cvals.unsqueeze(0).broadcast_to((P, n)), sem="d_c2"))
        for i, v in enumerate(self.CONSTS):
            self.nc.const_aps.register(F32, float(v), self.cpool[:, i:i + 1])
        self.ident_f = self.din("cident", (P, P), F32)
        self.ident = self.sbuf("identb", (P, P), BF16)
        toks.append(self.S.dma("gpsimd", self.ident[:], self.ident_f, sem="d_c"))
        if need_ones:
            self.ones_f = self.din("cones", (P, P), F32)
            self.ones = self.sbuf("onesb", (P, P), BF16)
            toks.append(self.S.dma("gpsimd", self.ones[:], self.ones_f, sem="d_c"))
        if need_tri:
            self.tri_f = self.din("ctri", (P, P), F32)
            self.tri = self.sbuf("trib", (P, P), BF16)
            toks.append(self.S.dma("gpsimd", self.tri[:], self.tri_f, sem="d_c"))
        self.const_tok = merge(*toks)
        self.S.default_waits = self.const_tok
        self.psA = self.psum("psA", (P, 2048), F32)
        self.psB = self.psum("psB", (P, 2048), F32)

    def ps512(self, i):
        W = self.cfg.W
        per = self.psA.shape[1] // W
        t = self.psA if i < per else self.psB
        return t[:, (i % per) * W:((i % per) + 1) * W]



def carve(t, col0, ncol_b, shape, dtype):
    v = t[:, col0:col0 + ncol_b].bitcast(dtype)
    assert v.shape[1] >= shape[1], (v.shape, shape)
    return v[:, 0:shape[1]]


def carve_or_alloc(c, t, col0, shape, dtype, name):
    need = shape[1] * (2 if dtype == F32 else 1)
    if t.shape[1] >= col0 + need:
        return carve(t[:, :] if hasattr(t, "ap") else t, col0, need, shape, dtype)
    return c.sbuf(name, shape, dtype)[:, :]


def transpose_in(c, src_hbm, arena, deps, tag, xb, ps_gate=None):
    cfg, S = c.cfg, c.S
    psT = c.psB[:, 0:512].bitcast(BF16)
    buf_tok = [None, None]
    last_ev = None
    all_ev = []
    rowtoks = deps if isinstance(deps, dict) else None
    for ti in range(cfg.NT):
        b = ti % 2
        d = merge(rowtoks[ti]) if rowtoks is not None else merge(deps)
        ld = S.dma("gpsimd", xb[b][:, 0:cfg.D], src_hbm[ti * P:(ti + 1) * P, :],
                   waits=merge(d, buf_tok[0], buf_tok[1]), sem=f"d_{tag}x{b}")
        nb = (cfg.ND + 7) // 8
        for g in range(nb):
            js = list(range(g * 8, min((g + 1) * 8, cfg.ND)))

            def em(e, js=js, b=b):
                inst = None
                for k, j in enumerate(js):
                    inst = e.transpose(psT[:, k * P:(k + 1) * P],
                                       xb[b][:, j * P:(j + 1) * P], c.ident[:])
                return inst
            mm = S.op("tensor", em, waits=merge([ld], c.const_tok, last_ev,
                                                ps_gate))
            buf_tok[b] = mm
            j0, n = js[0], len(js)
            dst = arena[:, :].rearrange("p (nd s) -> p nd s", nd=cfg.ND)[
                :, j0:j0 + n, ti * P:(ti + 1) * P]
            src = psT[:, 0:n * P].rearrange("p (n q) -> p n q", n=n)
            last_ev = S.op("vector", lambda e, d=dst, s=src: e.tensor_copy(d, s),
                           waits=[mm])
            all_ev.append(last_ev)
    return merge(last_ev, *all_ev)


def epilogue(c, part, nparts, midT, Cdim, wp_blk, bias_view, resid_hbm, z_hbm,
             deps, tag, w_eng, ps_free, eb, wready=None):
    cfg, S = c.cfg, c.S
    NTp = cfg.NT // nparts
    assert NTp <= 8
    NC = Cdim // P
    ne2 = cfg.D // cfg.WD
    wb, rb, zb = eb["wb"], eb["rb"], eb["zb"]
    wtok = eb.setdefault("wtok", [None, None])
    rtok = eb.setdefault("rtok", [None, None])
    zslot = eb.setdefault("zslot", [None] * 2)
    prev_ev = ps_free
    st_all, ev_all, mm_all = [], [], []
    EBG = 4
    nblk = (NC + EBG - 1) // EBG
    for e2c in range(ne2):
        mm = None
        for blk in range(nblk):
            b = blk % 2
            ct0 = blk * EBG
            ng = min(EBG, NC - ct0)
            wrd = wready(e2c) if wready else []
            lt = S.dma(w_eng,
                       wb[b][:, 0:ng * cfg.WD].rearrange("p (g w) -> p g w", g=ng),
                       wp_blk(ct0, ng, e2c),
                       waits=merge(deps, wtok[b], wrd), sem=f"d_epw_{w_eng}{b}")

            def em(e, ct0=ct0, ng=ng, b=b):
                inst = None
                for g in range(ng):
                    for tt in range(NTp):
                        inst = e.matmul(c.ps512(tt)[:, 0:cfg.WD],
                                        midT(ct0 + g, tt),
                                        wb[b][:, g * cfg.WD:(g + 1) * cfg.WD],
                                        start=(ct0 + g == 0),
                                        stop=(ct0 + g == NC - 1))
                return inst
            mm = S.op("tensor", em, waits=merge([lt], deps,
                                                prev_ev if blk == 0 else None))
            wtok[b] = mm
        mm_all.append(mm)
        evs = []
        for tt in range(NTp):
            trow = part * NTp + tt
            b = tt % 2
            zi = tt % 2
            lr = S.dma("sync", rb[b][:, 0:cfg.WD],
                       resid_hbm[trow * P:(trow + 1) * P,
                                 e2c * cfg.WD:(e2c + 1) * cfg.WD],
                       waits=merge(deps, rtok[b]), sem=f"d_epr{b}")
            ps = c.ps512(tt)
            bv = bias_view(e2c) if bias_view is not None else None
            if bv is not None:
                t1 = S.op("vector", lambda e, zi=zi, ps=ps, bv=bv:
                          e.tensor_tensor(zb[zi][:, 0:cfg.WD], ps[:, 0:cfg.WD],
                                          bv, ALU.add),
                          waits=merge([mm], zslot[zi], deps))
                t2 = S.op("vector", lambda e, zi=zi, b=b:
                          e.tensor_tensor(zb[zi][:, 0:cfg.WD], zb[zi][:, 0:cfg.WD],
                                          rb[b][:, 0:cfg.WD], ALU.add),
                          waits=merge([t1], [lr]))
            else:
                t2 = S.op("vector", lambda e, zi=zi, b=b, ps=ps:
                          e.tensor_tensor(zb[zi][:, 0:cfg.WD], ps[:, 0:cfg.WD],
                                          rb[b][:, 0:cfg.WD], ALU.add),
                          waits=merge([mm], [lr], zslot[zi]))
            rtok[b] = t2
            ev_all.append(t2)
            evs.append(t2)
            st = S.dma("scalar", z_hbm[trow * P:(trow + 1) * P,
                                       e2c * cfg.WD:(e2c + 1) * cfg.WD],
                       zb[zi][:, 0:cfg.WD], waits=[t2], sem=f"d_z{zi}")
            zslot[zi] = st
            st_all.append(st)
        prev_ev = merge(*evs)
    return dict(st=merge(*st_all), ev=merge(*ev_all), mm=merge(*mm_all))


def ln_phase(c, part, nparts, z_hbm, out_hbm, deps, lb, eps=1e-5):
    cfg, S = c.cfg, c.S
    NTp = cfg.NT // nparts
    zr, sq = lb["zr"], lb["sq"]
    cols = lb["cols"]
    rs, mneg, ssum, lnv, rstd, bb = (cols[:, i:i + 1] for i in range(6))
    zfree = lb.setdefault("zfree", [None, None])
    xfree = lb.setdefault("xfree", [None, None])
    state = lb.setdefault("state", {})
    sts = []
    for tt in range(NTp):
        trow = part * NTp + tt
        b = tt % 2
        ld = S.dma("sync", zr[b][:, 0:cfg.D], z_hbm[trow * P:(trow + 1) * P, :],
                   waits=merge(deps, zfree[b]), sem=f"d_lnl{b}")
        t1 = S.op("vector", lambda e, b=b: e.reduce_sum(rs, zr[b][:, 0:cfg.D], AX.X),
                  waits=merge([ld], state.get("chain")))
        t2 = S.op("vector", lambda e: e.tensor_scalar(mneg, rs, -1.0 / cfg.D,
                                                      None, ALU.mult),
                  waits=[t1])
        D2 = cfg.D // 2
        ss2 = cols[:, 6:7]
        t3a = S.op("scalar", lambda e, b=b: e.activation(sq[:, 0:D2],
                                                         zr[b][:, 0:D2], AF.Square,
                                                         bias=mneg, scale=1.0,
                                                         accum_out=ssum),
                   waits=merge([t2], state.get("sq")))
        t3b = S.op("scalar", lambda e, b=b: e.activation(sq[:, 0:D2],
                                                         zr[b][:, D2:cfg.D],
                                                         AF.Square,
                                                         bias=mneg, scale=1.0,
                                                         accum_out=ss2),
                   waits=[t3a])
        t3 = S.op("vector", lambda e: e.tensor_tensor(ssum, ssum, ss2, ALU.add),
                  waits=[t3b])
        t4 = S.op("scalar", lambda e: e.activation(lnv, ssum, AF.Ln,
                                                   bias=float(eps), scale=1.0 / cfg.D),
                  waits=[t3])
        t5 = S.op("scalar", lambda e: e.activation(rstd, lnv, AF.Exp,
                                                   bias=0.0, scale=-0.5),
                  waits=[t4])
        t6 = S.op("vector", lambda e: e.tensor_tensor(bb, mneg, rstd, ALU.mult),
                  waits=[t5])
        t7 = S.op("scalar", lambda e, b=b: e.activation(zr[b][:, 0:cfg.D],
                                                        zr[b][:, 0:cfg.D], AF.Identity,
                                                        bias=bb, scale=rstd),
                  waits=merge([t6], xfree[b]))
        st = S.dma("scalar", out_hbm[trow * P:(trow + 1) * P, :], zr[b][:, 0:cfg.D],
                   waits=[t7], sem=f"d_o{b}")
        zfree[b] = st
        xfree[b] = st
        state["chain"] = t6
        state["sq"] = t7
        sts.append(st)
        rowtoks = lb.setdefault("rowtoks", {})
        rowtoks[trow] = st
    return merge(*sts)



def l0_attn_half(c, H, arena, ctx, w_aps, bt, bufs, deps, psA_free, psB_free,
                 ctx_free):
    cfg, S = c.cfg, c.S
    ND, S2, SQ = cfg.ND, cfg.S2, cfg.S
    nW = S2 // min(cfg.W, S2)
    WH = min(cfg.W, S2)
    wA, wK = bufs["wA"], bufs["wK"]
    m0, t1b = bufs["m0"], bufs["t1b"]
    phq, phk, vv, kv = bufs["phq"], bufs["phk"], bufs["vv"], bufs["kv"]
    kvc = bufs["kvc"]
    zcol = bufs["zcol"]
    carr = bufs["carr"]
    ztok = bufs["ztok"]
    wa_tok = bufs.setdefault("wa_tok", [None, None])
    wk_tok = bufs.setdefault("wk_tok", [None, None])
    scr = bufs.setdefault("scr", None)
    kscr = bufs.setdefault("kscr", None)
    vv_free = bufs.setdefault("vv_free", None)
    kv_free = bufs.setdefault("kv_free", None)
    kvc_free = bufs.setdefault("kvc_free", None)
    ctx_tok = []

    def w_src(k, et):
        return w_aps[k].rearrange("(nj p) d -> p nj d", p=P)[:, :, et * P:(et + 1) * P]

    def mk_mm(wl, ps):
        def em(e):
            inst = None
            for j in range(ND):
                for cc in range(nW):
                    c0 = H * S2 + cc * WH
                    inst = e.matmul(ps[:, cc * WH:(cc + 1) * WH],
                                    wl[:, j * P:(j + 1) * P],
                                    arena[:, j * SQ + c0: j * SQ + c0 + WH],
                                    start=(j == 0), stop=(j == ND - 1))
            return inst
        return em

    for et in range(ND):
        c.pump(4)
        b = et % 2
        wAb, wKb = wA[b], wK[b]
        bq = bt["bq"][:, et:et + 1]
        bk = bt["bk"][:, et:et + 1]
        bvc = bt["bv"][:, et:et + 1]
        lq = c.S.dma("gpsimd", wAb.rearrange("p (nj d) -> p nj d", nj=ND),
                     w_src("wq", et), waits=merge(deps, bufs["wa_tok"][b]),
                     sem=f"d_wa{b}")
        mmq = c.S.op("tensor", mk_mm(wAb, c.psA), waits=merge([lq], deps, psA_free))
        nbq = bt["nbq"][:, et:et + 1]
        vq1 = c.S.op("scalar", lambda e, nbq=nbq: e.activation(
            m0[:, 0:S2], c.psA[:, 0:S2], AF.Relu, bias=nbq, scale=-1.0),
            waits=merge([mmq], scr))
        sq1 = c.S.op("scalar", lambda e: e.activation(m0[:, 0:S2], m0[:, 0:S2],
                                                      AF.Exp, bias=0.0, scale=-1.0),
                     waits=[vq1])
        vq2 = c.S.op("scalar", lambda e, bq=bq: e.activation(
            t1b[:, 0:S2], c.psA[:, 0:S2], AF.Relu, bias=bq, scale=1.0),
            waits=merge([mmq], scr, [vq1]))
        vq3 = c.S.op("vector", lambda e: e.tensor_tensor(
            phq[:, 0:S2], t1b[:, 0:S2], m0[:, 0:S2], ALU.add),
            waits=merge([sq1], [vq2], scr))
        lv = c.S.dma("gpsimd", wAb.rearrange("p (nj d) -> p nj d", nj=ND),
                     w_src("wv", et), waits=merge([mmq], deps), sem=f"d_wa{b}")
        mmv = c.S.op("tensor", mk_mm(wAb, c.psA), waits=merge([lv], [vq1], [vq2], deps))
        bufs["wa_tok"][b] = mmv
        sv1 = c.S.op("scalar", lambda e, bvc=bvc: e.activation(
            vv[:, 0:S2], c.psA[:, 0:S2], AF.Identity, bias=bvc, scale=1.0),
            waits=merge([mmv], vv_free))
        psA_free = sv1
        lk = c.S.dma("gpsimd", wKb.rearrange("p (nj d) -> p nj d", nj=ND),
                     w_src("wk", et), waits=merge(deps, bufs["wk_tok"][b]),
                     sem=f"d_wk{b}")
        mmk = c.S.op("tensor", mk_mm(wKb, c.psB), waits=merge([lk], deps, psB_free))
        bufs["wk_tok"][b] = mmk
        nbk = bt["nbk"][:, et:et + 1]
        vk1 = c.S.op("scalar", lambda e, nbk=nbk: e.activation(
            t1b[:, 0:S2], c.psB[:, 0:S2], AF.Relu, bias=nbk, scale=-1.0),
            waits=merge([mmk], [vq3], kscr))
        sk1 = c.S.op("scalar", lambda e: e.activation(t1b[:, 0:S2], t1b[:, 0:S2],
                                                      AF.Exp, bias=0.0, scale=-1.0),
                     waits=[vk1])
        vk2 = c.S.op("scalar", lambda e, bk=bk: e.activation(
            m0[:, 0:S2], c.psB[:, 0:S2], AF.Relu, bias=bk, scale=1.0),
            waits=merge([mmk], [vq3], [sq1], [vk1]))
        vk3 = c.S.op("vector", lambda e: e.tensor_tensor(
            phk[:, 0:S2], m0[:, 0:S2], t1b[:, 0:S2], ALU.add),
            waits=merge([sk1], [vk2], kscr))
        psB_free = vk2
        bufs["kscr"] = None
        v1 = c.S.op("vector", lambda e: e.tensor_tensor(
            kv[:, 0:S2], phk[:, 0:S2], vv[:, 0:S2], ALU.mult),
            waits=merge([vk3], [sv1], kv_free))
        bufs["vv_free"] = v1
        init = 0.0 if H == 0 else carr[:, et:et + 1]
        v2 = c.S.op("vector", lambda e, init=init: e.tensor_tensor_scan(
            kvc[:, 0:S2], kv[:, 0:S2],
            zcol[:, 0:1].broadcast_to((P, S2)), init, ALU.add, ALU.add),
            waits=merge([v1], [ztok], kvc_free,
                        bufs.get("carr_w%d" % et) if H == 1 else None))
        bufs["kv_free"] = v2
        cw = c.S.op("vector", lambda e, et=et: e.tensor_copy(
            carr[:, et:et + 1], kvc[:, S2 - 1:S2]), waits=[v2]) if H == 0 else None
        if H == 0:
            bufs["carr_w%d" % et] = cw
        dst = ctx[:, et * S2:(et + 1) * S2]
        v3 = c.S.op("vector", lambda e, dst=dst: e.scalar_tensor_tensor(
            dst, kvc[:, 0:S2], cfg.SCALE, phq[:, 0:S2], ALU.mult, ALU.mult),
            waits=merge([v2], [vq3], ctx_free, [cw] if cw else None))
        bufs["kvc_free"] = v3
        bufs["scr"] = v3
        bufs["kscr"] = vk3
        scr = v3
        kscr = vk3
        vv_free = v1
        kv_free = v2
        kvc_free = v3
        ctx_tok.append(v3)
    return dict(done=merge(*ctx_tok), psA_free=psA_free, psB_free=psB_free)



def l1_attn(c, arena, oT, w_aps, bt, bv_bcast, deps):
    cfg, S = c.cfg, c.S
    D, ND, NT, NH = cfg.D, cfg.ND, cfg.NT, cfg.NH
    SQ = cfg.S
    HPG = min(cfg.NH, 2)
    VW = HPG * cfg.HD
    nW = SQ // cfg.W
    NQ = SQ // cfg.W
    perA = c.psA.shape[1] // cfg.W
    vg = c.sbuf("l1_vg", (P, NT * VW), BF16)
    qh = c.sbuf("l1_qh", (P, SQ), BF16)
    kh = c.sbuf("l1_kh", (P, SQ), BF16)
    wqb = [c.sbuf(f"l1_wq{i}", (P, ND * P), BF16) for i in range(2)]
    wkb = [c.sbuf(f"l1_wk{i}", (P, ND * P), BF16) for i in range(2)]
    wvb = c.sbuf("l1_wv", (P, ND * VW), BF16)
    pT = [c.sbuf(f"l1_pT{i}", (P, cfg.W), BF16) for i in range(2)]
    rr = c.sbuf("l1_rr", (P, cfg.W), F32)
    wq_tok = [None, None]
    wk_tok = [None, None]
    wv_tok = None
    psA_free = None
    psB_free = None
    sslot_free = [None, None]
    rslot_free = [None, None]
    pT_free = [None, None]
    rr_free = None
    qh_free = None
    kh_free = None
    o_toks = []

    for g in range(NH // HPG):
        srcv = w_aps["wv"].rearrange("(nj p) d -> p nj d", p=P)[
            :, :, g * VW:(g + 1) * VW]
        lv = S.dma("gpsimd", wvb[:].rearrange("p (nj d) -> p nj d", nj=ND), srcv,
                   waits=merge(deps, wv_tok), sem="d_wv0")
        vg_evs = []
        for tt in range(NT):
            ps = c.ps512(tt % perA)

            def em(e, tt=tt, ps=ps):
                inst = None
                for j in range(ND):
                    inst = e.matmul(ps[:, 0:VW],
                                    arena[:, j * SQ + tt * P: j * SQ + (tt + 1) * P],
                                    wvb[:, j * VW:(j + 1) * VW],
                                    start=(j == 0), stop=(j == ND - 1))
                return inst
            mmv = S.op("tensor", em, waits=merge([lv], deps,
                                                 psA_free if tt < perA else None,
                                                 vg_evs[tt - perA] if tt >= perA else None))
            ev = S.op("vector", lambda e, tt=tt, ps=ps, g=g: e.tensor_tensor(
                vg[:, tt * VW:(tt + 1) * VW], ps[:, 0:VW],
                bv_bcast[:, g * VW:(g + 1) * VW], ALU.add),
                waits=merge([mmv], wv_tok))
            vg_evs.append(ev)
        wv_tok = merge(*vg_evs)
        psA_free = wv_tok
        head_evs = []
        for hh in range(HPG):
            h = g * HPG + hh
            b = h % 2
            srcq = w_aps["wq"].rearrange("(nj p) d -> p nj d", p=P)[
                :, :, h * P:(h + 1) * P]
            srck = w_aps["wk"].rearrange("(nj p) d -> p nj d", p=P)[
                :, :, h * P:(h + 1) * P]
            lq = S.dma("gpsimd", wqb[b][:].rearrange("p (nj d) -> p nj d", nj=ND),
                       srcq, waits=merge(deps, wq_tok[b]), sem=f"d_wq{b}")
            lk = S.dma("gpsimd", wkb[b][:].rearrange("p (nj d) -> p nj d", nj=ND),
                       srck, waits=merge(deps, wk_tok[b]), sem=f"d_wk{b}")

            def mkqk(wl, ps, b):
                def em(e):
                    inst = None
                    for j in range(ND):
                        for cc in range(nW):
                            inst = e.matmul(ps[:, cc * cfg.W:(cc + 1) * cfg.W],
                                            wl[b][:, j * P:(j + 1) * P],
                                            arena[:, j * SQ + cc * cfg.W:
                                                  j * SQ + (cc + 1) * cfg.W],
                                            start=(j == 0), stop=(j == ND - 1))
                    return inst
                return em
            mmq = S.op("tensor", mkqk(wqb, c.psA, b),
                       waits=merge([lq], deps, psA_free, qh_free))
            wq_tok[b] = mmq
            sq = S.op("scalar", lambda e, h=h: e.activation(
                qh[:], c.psA[:, 0:SQ], AF.Identity, bias=bt["bq"][:, h:h + 1],
                scale=1.0),
                waits=merge([mmq], qh_free))
            mmk = S.op("tensor", mkqk(wkb, c.psB, b),
                       waits=merge([lk], deps, psB_free, kh_free,
                                   sslot_free[0], sslot_free[1],
                                   rslot_free[0], rslot_free[1]))
            wk_tok[b] = mmk
            sk = S.op("scalar", lambda e, h=h: e.activation(
                kh[:], c.psB[:, 0:SQ], AF.Identity, bias=bt["bk"][:, h:h + 1],
                scale=1.0),
                waits=merge([mmk], kh_free))
            psB_free = sk
            evs = []
            for qg in range(NQ):
                q0 = qg * cfg.W
                jmax = min(NT - 1, (q0 + cfg.W - 1) // P)
                rs_i = 2 + (qg % 2)
                rs = c.psB[:, rs_i * cfg.W:(rs_i + 1) * cfg.W]
                last_p = None
                for j in range(jmax + 1):
                    sl = j % 2
                    sT = c.psB[:, sl * cfg.W:(sl + 1) * cfg.W]
                    mm1 = S.op("tensor", lambda e, j=j, q0=q0, sT=sT: e.matmul(
                        sT[:, 0:cfg.W], kh[:, j * P:(j + 1) * P], qh[:, q0:q0 + cfg.W],
                        start=True, stop=True),
                        waits=merge([sq], [sk], sslot_free[sl]))
                    pb = pT[sl]
                    se = S.op("scalar", lambda e, pb=pb, sT=sT: e.activation(
                        pb[:], sT[:, 0:cfg.W], AF.Exp, bias=0.0, scale=cfg.SCALE),
                        waits=merge([mm1], pT_free[sl]))
                    sslot_free[sl] = se
                    off = j * P - q0
                    mtoks = [se]
                    if off > 0:
                        z1 = S.op("vector", lambda e, pb=pb, off=off:
                                  e.memset(pb[:, 0:off], 0.0), waits=[se])
                        mtoks.append(z1)
                    if 0 <= off <= cfg.W - P:
                        z2 = S.op("vector", lambda e, pb=pb, off=off: e.tensor_tensor(
                            pb[:, off:off + P], pb[:, off:off + P], c.tri[:], ALU.mult),
                            waits=merge([se], c.const_tok))
                        mtoks.append(z2)
                    ptok = merge(*mtoks)
                    mm2 = S.op("tensor", lambda e, pb=pb, rs=rs, j=j, jmax=jmax:
                               e.matmul(rs[:, 0:cfg.W], c.ones[:], pb[:],
                                        start=(j == 0), stop=(j == jmax)),
                               waits=merge(ptok, c.const_tok,
                                           rslot_free[rs_i - 2] if j == 0 else None))
                    vsl = vg[:, j * VW + hh * cfg.HD: j * VW + hh * cfg.HD + P]
                    mm3 = S.op("tensor", lambda e, vsl=vsl, pb=pb, q0=q0, j=j, jmax=jmax:
                               e.matmul(c.psA[:, q0:q0 + cfg.W], vsl, pb[:],
                                        start=(j == 0), stop=(j == jmax)),
                               waits=merge(ptok, wv_tok, [sq] if j == 0 else None))
                    last_p = merge([mm2], [mm3])
                    pT_free[sl] = last_p
                v1 = S.op("vector", lambda e, rs=rs: e.reciprocal(rr[:], rs[:, 0:cfg.W]),
                          waits=merge(last_p, rr_free))
                rslot_free[rs_i - 2] = v1
                dst = oT[:, h * SQ + q0: h * SQ + q0 + cfg.W]
                ev = S.op("vector", lambda e, dst=dst, q0=q0:
                          e.scalar_tensor_tensor(dst, c.psA[:, q0:q0 + cfg.W], 1.0,
                                                 rr[:], ALU.mult, ALU.mult),
                          waits=merge([v1], last_p))
                rr_free = ev
                evs.append(ev)
            psA_free = merge(*evs)
            qh_free = psA_free
            kh_free = psA_free
            head_evs.extend(evs)
            o_toks.extend(evs)
        wv_tok = merge(wv_tok, *head_evs)
    return dict(done=merge(*o_toks), psA_free=psA_free, psB_free=merge(
        sslot_free[0], sslot_free[1], rslot_free[0], rslot_free[1], psB_free))



def ffn_core(c, arena, hT, wg_hbm, wu_hbm, wd_hbm, sg_t, su_t, NO,
             resid_hbm, z_hbm, out_hbm, deps, eb, lb):
    cfg, S = c.cfg, c.S
    D, ND = cfg.D, cfg.ND
    SQ = cfg.S
    NP = 4 if cfg.NT >= 8 else 2
    SP_ = SQ // NP
    WH = min(cfg.W, SP_)
    nW2 = SP_ // WH
    gw = [c.sbuf(f"ff_gw{i}", (P, ND * P), BF16) for i in range(2)]
    uw = [c.sbuf(f"ff_uw{i}", (P, ND * P), BF16) for i in range(2)]
    sgb = c.sbuf("ff_sg", (P, SP_), BF16)
    gtok = [None, None]
    utok = [None, None]
    out_toks = []
    ep_prev = None
    hT_readers = None
    for PI in range(NP):
        psg_free = ep_prev["ev"] if ep_prev else None
        psu_free = ep_prev["ev"] if ep_prev else None
        sg_free = None
        h_toks = []
        for ot in range(NO):
            b = ot % 2
            srcg = wg_hbm.rearrange("(nj p) o -> p nj o", p=P)[
                :, :, ot * P:(ot + 1) * P]
            srcu = wu_hbm.rearrange("(nj p) o -> p nj o", p=P)[
                :, :, ot * P:(ot + 1) * P]
            lg = S.dma("sync", gw[b][:].rearrange("p (nj o) -> p nj o", nj=ND),
                       srcg, waits=merge(deps, gtok[b]), sem=f"d_g{b}")
            lu = S.dma("sync", uw[b][:].rearrange("p (nj o) -> p nj o", nj=ND),
                       srcu, waits=merge(deps, utok[b]), sem=f"d_u{b}")

            def mk(wl, ps, b, PI=PI):
                def em(e):
                    inst = None
                    for j in range(ND):
                        for cc in range(nW2):
                            c0 = PI * SP_ + cc * WH
                            inst = e.matmul(ps[:, cc * WH:(cc + 1) * WH],
                                            wl[b][:, j * P:(j + 1) * P],
                                            arena[:, j * SQ + c0: j * SQ + c0 + WH],
                                            start=(j == 0), stop=(j == ND - 1))
                    return inst
                return em
            mmg = S.op("tensor", mk(gw, c.psA, b), waits=merge([lg], deps, psg_free))
            gtok[b] = mmg
            mmu = S.op("tensor", mk(uw, c.psB, b), waits=merge([lu], deps, psu_free))
            utok[b] = mmu
            sg_sc = sg_t[:, ot:ot + 1] if sg_t is not None else 1.0
            s1 = S.op("scalar", lambda e, sg_sc=sg_sc: e.activation(
                sgb[:, 0:SP_], c.psA[:, 0:SP_], AF.Silu, bias=0.0, scale=sg_sc),
                waits=merge([mmg], sg_free))
            psg_free = s1
            dsth = hT[:, ot * SP_:(ot + 1) * SP_]
            hw = merge([mmu], [s1], hT_readers if PI > 0 else None)
            if su_t is None:
                v1 = S.op("vector", lambda e, dsth=dsth: e.tensor_tensor(
                    dsth, c.psB[:, 0:SP_], sgb[:, 0:SP_], ALU.mult), waits=hw)
            else:
                su_sc = su_t[:, ot:ot + 1]
                v1 = S.op("vector", lambda e, dsth=dsth, su_sc=su_sc:
                          e.scalar_tensor_tensor(dsth, c.psB[:, 0:SP_], su_sc,
                                                 sgb[:, 0:SP_], ALU.mult, ALU.mult),
                          waits=hw)
            psu_free = v1
            sg_free = v1
            h_toks.append(v1)

        def midT(ct, tt):
            return hT[:, ct * SP_ + tt * P: ct * SP_ + (tt + 1) * P]

        def wp_blk(ct0, ng, e2c):
            return wd_hbm[ct0 * P:(ct0 + ng) * P,
                          e2c * cfg.WD:(e2c + 1) * cfg.WD].rearrange(
                "(g p) w -> p g w", p=P)
        ep = epilogue(c, PI, NP, midT, NO * P, wp_blk, None, resid_hbm, z_hbm,
                      merge(deps, *h_toks), "fe", "sync",
                      ps_free=merge(*h_toks, psg_free, psu_free), eb=eb)
        hT_readers = ep["mm"]
        ep_prev = ep
        out_toks.append(ln_phase(c, PI, NP, z_hbm, out_hbm, ep["st"], lb))
    return merge(*out_toks)



def _mk_eb(c, tag, wb=None):
    return {
        "wb": wb or [c.sbuf(f"{tag}_wb{i}", (P, 4 * c.cfg.WD), BF16)[:, :]
                     for i in range(2)],
        "rb": [c.sbuf(f"{tag}_rb{i}", (P, c.cfg.WD), F32) for i in range(2)],
        "zb": [c.sbuf(f"{tag}_zb{i}", (P, c.cfg.WD), F32) for i in range(2)],
    }


def _mk_lb_carve(c, t, col0):
    D = c.cfg.D
    f32w = 2 * D
    return {
        "zr": [carve(t, col0 + i * f32w, f32w, (P, D), F32) for i in range(2)],
        "xo": [carve(t, col0 + (2 + i) * f32w, f32w, (P, D), F32) for i in range(2)],
        "sq": carve(t, col0 + 4 * f32w, f32w, (P, D), F32),
        "cols": carve(t, col0 + 5 * f32w, 16, (P, 8), F32),
    }


def _mk_lb_alloc(c, tag):
    D = c.cfg.D
    return {
        "zr": [c.sbuf(f"{tag}_zr{i}", (P, D), F32) for i in range(2)],
        "xo": [c.sbuf(f"{tag}_xo{i}", (P, D), F32) for i in range(2)],
        "sq": c.sbuf(f"{tag}_sq", (P, D), F32),
        "cols": c.sbuf(f"{tag}_cols", (P, 8), F32),
    }


def build_l0a(cfg):
    c = Core(cfg)
    D, ND = cfg.D, cfg.ND
    xin = c.din("xin", (cfg.S, D), F32)
    wq = c.din("wq", (D, D), F32)
    wk = c.din("wk", (D, D), F32)
    wv = c.din("wv", (D, D), F32)
    wo = c.din("wo", (D, D), F32)
    bq = c.din("bq", (P, ND), F32)
    bk = c.din("bk", (P, ND), F32)
    bv = c.din("bv", (P, ND), F32)
    bo = c.din("bo", (D,), F32)
    xout = c.dout("xout", (cfg.S, D), F32)
    zscr = c.dtmp("zscr", (cfg.S, D), F32)
    with c.stack:
        c.setup_common()
        arena = c.sbuf("arena", (P, ND * cfg.S), BF16)
        ctx = c.sbuf("ctx", (P, ND * cfg.S2), BF16)
        pool = [c.sbuf(f"pool{i}", (P, D), F32) for i in range(2)]
        xbT = c.sbuf("xbT", (P, 2 * D), BF16)
        xb = [xbT[:, 0:D], xbT[:, D:2 * D]]
        lb = {"zr": [pool[0], pool[1]],
              "sq": c.sbuf("sqb", (P, D // 2), F32),
              "cols": c.sbuf("lncols", (P, 8), F32)}
        eb = _mk_eb(c, "ep")
        bufs = {
            "wA": [c.sbuf(f"wA{i}", (P, ND * P), BF16)[:, :] for i in range(2)],
            "wK": [c.sbuf(f"wK{i}", (P, ND * P), BF16)[:, :] for i in range(2)],
            "m0": c.sbuf("m0", (P, cfg.S2), F32)[:, :],
            "t1b": c.sbuf("t1b", (P, cfg.S2), F32)[:, :],
            "phq": c.sbuf("phq", (P, cfg.S2), BF16)[:, :],
            "phk": c.sbuf("phk", (P, cfg.S2), BF16)[:, :],
            "vv": c.sbuf("vvb", (P, cfg.S2), BF16)[:, :],
            "kv": c.sbuf("kvb", (P, cfg.S2), BF16)[:, :],
            "kvc": c.sbuf("kvcb", (P, cfg.S2), F32)[:, :],
            "zcol": c.sbuf("zcol", (P, 1), BF16),
            "carr": c.sbuf("carr", (P, ND), F32),
        }
        btile = c.sbuf("btile", (P, 3 * ND), F32)
        bo_b = c.sbuf("bo_b", (P, D), F32)
        c.S.dma("sync", btile[:, 0:ND], bq, sem="d_b")
        c.S.dma("sync", btile[:, ND:2 * ND], bk, sem="d_b")
        c.S.dma("sync", btile[:, 2 * ND:3 * ND], bv, sem="d_b")
        t4 = c.S.dma("sync", bo_b[:], bo.unsqueeze(0).broadcast_to((P, D)), sem="d_b")
        btok = merge(t4)
        nb = c.sbuf("nbtile", (P, 2 * ND), F32)
        nbt = c.S.op("vector", lambda e: e.tensor_scalar(
            nb[:, :], btile[:, 0:2 * ND], -1.0, None, ALU.mult),
            waits=merge(btok))
        bt = {"bq": btile[:, 0:ND], "bk": btile[:, ND:2 * ND],
              "bv": btile[:, 2 * ND:3 * ND],
              "nbq": nb[:, 0:ND], "nbk": nb[:, ND:2 * ND], "nbt": nbt}
        bufs["ztok"] = c.S.op("vector", lambda e: e.memset(bufs["zcol"][:], 0.0))
        ttok = transpose_in(c, xin, arena, btok, "t", xb)
        w_aps = {"wq": wq, "wk": wk, "wv": wv}

        def wp_blk(ct0, ng, e2c):
            return wo[ct0 * P:(ct0 + ng) * P,
                      e2c * cfg.WD:(e2c + 1) * cfg.WD].rearrange(
                "(g p) w -> p g w", p=P)

        def bias_view(e2c):
            return bo_b[:, e2c * cfg.WD:(e2c + 1) * cfg.WD]
        fins = []
        psA_free, psB_free = None, None
        ctx_free = None
        ln_dep0 = ttok
        for H in range(2):
            res = l0_attn_half(c, H, arena, ctx, w_aps, bt, bufs,
                               merge(ttok), psA_free, psB_free, ctx_free)

            def midT(ct, tt):
                return ctx[:, ct * cfg.S2 + tt * P: ct * cfg.S2 + (tt + 1) * P]
            ep = epilogue(c, H, 2, midT, D, wp_blk, bias_view, xin, zscr,
                          res["done"], "ep", "gpsimd",
                          merge(res["psA_free"], res["psB_free"]), eb)
            psA_free = ep["ev"]
            psB_free = ep["ev"]
            ctx_free = ep["mm"]
            fins.append(ln_phase(c, H, 2, zscr, xout, merge(ep["st"], ln_dep0), lb))
        c.S.op("sync", lambda e: None, waits=merge(*fins))
        c.S.emit(c.stack)
    return c.nc


def build_ffn(cfg, NO, with_scales):
    c = Core(cfg)
    D, ND = cfg.D, cfg.ND
    I = NO * P
    xin = c.din("xin", (cfg.S, D), F32)
    wg = c.din("wg", (D, I), BF16)
    wu = c.din("wu", (D, I), BF16)
    wd = c.din("wd", (I, D), BF16)
    if with_scales:
        sg = c.din("sg", (P, NO), F32)
        su = c.din("su", (P, NO), F32)
    xout = c.dout("xout", (cfg.S, D), F32)
    zscr = c.dtmp("zscr", (cfg.S, D), F32)
    with c.stack:
        c.setup_common()
        NP = 4 if cfg.NT >= 8 else 2
        arena = c.sbuf("arena", (P, ND * cfg.S), BF16)
        hT = c.sbuf("hT", (P, NO * (cfg.S // NP)), BF16)
        pool = [c.sbuf(f"pool{i}", (P, D), F32) for i in range(2)]
        xbT = c.sbuf("xbT", (P, 2 * D), BF16)
        xb = [xbT[:, 0:D], xbT[:, D:2 * D]]
        lb = {"zr": [pool[0], pool[1]],
              "sq": c.sbuf("sqb", (P, D // 2), F32),
              "cols": c.sbuf("lncols", (P, 8), F32)}
        eb = _mk_eb(c, "ep")
        if with_scales:
            sgt = c.sbuf("sgt", (P, NO), F32)
            sut = c.sbuf("sut", (P, NO), F32)
            c.S.dma("sync", sgt[:], sg, sem="d_b")
            s2 = c.S.dma("sync", sut[:], su, sem="d_b")
            stok = merge(s2)
            sg_t, su_t = sgt, sut
        else:
            stok = []
            sg_t, su_t = None, None
        ttok = transpose_in(c, xin, arena, stok, "t", xb)
        ftok = ffn_core(c, arena, hT, wg, wu, wd, sg_t, su_t, NO, xin, zscr, xout,
                        merge(ttok, stok), eb, lb)
        c.S.op("sync", lambda e: None, waits=ftok)
        c.S.emit(c.stack)
    return c.nc


def build_l1a(cfg):
    c = Core(cfg)
    D, ND = cfg.D, cfg.ND
    xin = c.din("xin", (cfg.S, D), F32)
    wq = c.din("wq", (D, D), F32)
    wk = c.din("wk", (D, D), F32)
    wv = c.din("wv", (D, D), F32)
    wo = c.din("wo", (D, D), F32)
    bq = c.din("bq", (P, cfg.NH), F32)
    bk = c.din("bk", (P, cfg.NH), F32)
    bv = c.din("bv", (D,), F32)
    bo = c.din("bo", (D,), F32)
    xout = c.dout("xout", (cfg.S, D), F32)
    zscr = c.dtmp("zscr", (cfg.S, D), F32)
    with c.stack:
        c.setup_common(need_ones=True, need_tri=True)
        arena = c.sbuf("arena", (P, ND * cfg.S), BF16)
        oT = c.sbuf("oT", (P, ND * cfg.S), BF16)
        xf = [carve_or_alloc(c, oT, i * 2 * D, (P, D), F32, f"xf{i}")
              for i in range(2)]
        xb = [carve_or_alloc(c, oT, 4 * D + i * D, (P, D), BF16, f"xb{i}")
              for i in range(2)]
        lb = {"zr": [carve_or_alloc(c, arena, i * 2 * D, (P, D), F32, f"zr{i}")
                     for i in range(2)],
              "xo": [carve_or_alloc(c, arena, (4 + 2 * i) * D, (P, D), F32, f"xo{i}")
                     for i in range(2)],
              "sq": carve_or_alloc(c, arena, 8 * D, (P, D), F32, "sqb"),
              "cols": c.sbuf("lncols", (P, 8), F32)}
        eb = {"wb": [carve_or_alloc(c, arena, 10 * D + i * cfg.WD, (P, cfg.WD),
                                    BF16, f"ewb{i}") for i in range(2)],
              "rb": [carve_or_alloc(c, arena, 11 * D + i * 2 * cfg.WD, (P, cfg.WD),
                                    F32, f"erb{i}") for i in range(2)],
              "zb": [carve_or_alloc(c, arena, 13 * D + i * 2 * cfg.WD, (P, cfg.WD),
                                    F32, f"ezb{i}") for i in range(4)]}
        bob = c.sbuf("bob", (P, D), F32)[:, :]
        btile = c.sbuf("btile", (P, 2 * cfg.NH), F32)
        bvb = c.sbuf("bvb", (P, D), F32)
        c.S.dma("sync", btile[:, 0:cfg.NH], bq, sem="d_b")
        c.S.dma("sync", btile[:, cfg.NH:2 * cfg.NH], bk, sem="d_b")
        btok0 = c.S.dma("sync", bvb[:], bv.unsqueeze(0).broadcast_to((P, D)),
                        sem="d_b")
        bt = {"bq": btile[:, 0:cfg.NH], "bk": btile[:, cfg.NH:2 * cfg.NH]}
        ttok = transpose_in(c, xin, arena, merge(btok0), "t", xb)
        res = l1_attn(c, arena, oT, {"wq": wq, "wk": wk, "wv": wv}, bt, bvb, ttok)
        btok1 = c.S.dma("sync", bob, bo.unsqueeze(0).broadcast_to((P, D)),
                        waits=res["done"], sem="d_b")

        def midT(ct, tt_local):
            pass
        def wp_blk(ct0, ng, e2c):
            return wo[ct0 * P:(ct0 + ng) * P,
                      e2c * cfg.WD:(e2c + 1) * cfg.WD].rearrange(
                "(g p) w -> p g w", p=P)

        def bias_view(e2c):
            return bob[:, e2c * cfg.WD:(e2c + 1) * cfg.WD]
        fins = []
        ps_free = merge(res["psA_free"], res["psB_free"])
        for H in range(2):
            NTp = cfg.NT // 2

            def midT(ct, tt, H=H, NTp=NTp):
                t0 = (H * NTp + tt) * P
                return oT[:, ct * cfg.S + t0: ct * cfg.S + t0 + P]
            ep = epilogue(c, H, 2, midT, D, wp_blk, bias_view, xin, zscr,
                          merge(res["done"], btok1), "ep", "gpsimd", ps_free, eb)
            ps_free = ep["ev"]
            fins.append(ln_phase(c, H, 2, zscr, xout, ep["st"], lb))
        c.S.op("sync", lambda e: None, waits=merge(*fins))
        c.S.emit(c.stack)
    return c.nc


def build_gather(cfg, shard_specs):
    c = Core(cfg)
    with c.stack:
        lut_in = c.din("lut", (16,), F32)
        sd_in = c.din("sdown", (cfg.D,), F32)
        luts = c.sbuf("luts", (P, 16), F32)
        sdb = c.sbuf("sdb", (P, cfg.D), F32)
        c.S.dma("sync", luts[:], lut_in.unsqueeze(0).broadcast_to((P, 16)), sem="d_b")
        ctok = c.S.dma("sync", sdb[:], sd_in.unsqueeze(0).broadcast_to((P, cfg.D)),
                       sem="d_b")
        ins, outs = {}, {}
        for spec in shard_specs:
            nm = spec["name"]
            ins[nm] = c.din(nm + "_i", (spec["rows"], spec["cols"]), I32)
            outs[nm] = c.dout(nm + "_o", (spec["rows"], spec["cols"]), BF16)
        CMAX = max(s["cols"] for s in shard_specs)
        idxb = c.sbuf("idxb", (P, CMAX), BF16)
        mask = c.sbuf("maskb", (P, CMAX), I8)
        accb = c.sbuf("accb", (P, CMAX), BF16)
        idx_free = None
        acc_free = None
        last_st = None
        for spec in shard_specs:
            nm, R, Cc, mode = spec["name"], spec["rows"], spec["cols"], spec["mode"]
            r0 = 0
            while r0 < R:
                rr = min(P, R - r0)
                ld = c.S.dma("gpsimd", idxb[0:rr, 0:Cc], ins[nm][r0:r0 + rr, :],
                             waits=merge(idx_free), sem="d_i")
                if mode == "lut":
                    t0 = c.S.op("vector", lambda e, rr=rr, Cc=Cc: e.tensor_scalar(
                        accb[0:rr, 0:Cc], idxb[0:rr, 0:Cc], 0.0, luts[0:rr, 0:1],
                        ALU.mult, ALU.add),
                        waits=merge([ld], ctok, acc_free))
                    last = t0
                    for j in range(1, 16):
                        m1 = c.S.op("vector", lambda e, rr=rr, Cc=Cc, j=j:
                                    e.tensor_scalar(
                                        mask[0:rr, 0:Cc], idxb[0:rr, 0:Cc],
                                        -(j - 1.0), 0.0, ALU.add, ALU.max),
                                    waits=merge([ld], [last]))
                        last = c.S.op("vector", lambda e, rr=rr, Cc=Cc, j=j:
                                      e.copy_predicated(
                                          accb[0:rr, 0:Cc], mask[0:rr, 0:Cc],
                                          luts[0:rr, j:j + 1].broadcast_to((rr, Cc))),
                                      waits=[m1])
                elif mode == "deq":
                    last = c.S.op("vector", lambda e, rr=rr, Cc=Cc: e.tensor_scalar(
                        accb[0:rr, 0:Cc], idxb[0:rr, 0:Cc], -128.0, None, ALU.add),
                        waits=merge([ld], acc_free))
                else:
                    t0 = c.S.op("vector", lambda e, rr=rr, Cc=Cc: e.tensor_scalar(
                        accb[0:rr, 0:Cc], idxb[0:rr, 0:Cc], -128.0, None, ALU.add),
                        waits=merge([ld], acc_free))
                    last = c.S.op("vector", lambda e, rr=rr, Cc=Cc: e.tensor_tensor(
                        accb[0:rr, 0:Cc], accb[0:rr, 0:Cc], sdb[0:rr, 0:Cc], ALU.mult),
                        waits=merge([t0], ctok))
                idx_free = last
                st = c.S.dma("sync", outs[nm][r0:r0 + rr, :], accb[0:rr, 0:Cc],
                             waits=[last], sem="d_z")
                acc_free = st
                last_st = st
                r0 += rr
        c.S.op("sync", lambda e: None, waits=merge(last_st))
        c.S.emit(c.stack)
    return c.nc




def queue_gather(c, groups, luts_bf, dls, l0col, sdb, gbuf, deps):
    S = c.S
    NSC = 10
    st = {"par": 0, "st": [None, None], "sc": [None, None], "sg": [None, None]}
    out = {}

    def tile_thunk(in_ap, out_ap, r0, rr, c0, c1, mode, toks, final):
        par = st["par"]
        st["par"] ^= 1
        ib = gbuf["ib"][par]
        acc = gbuf["acc"][par]
        sb = gbuf["s"]
        w = c1 - c0
        ld = S.dma("gpsimd", ib[0:rr, 0:w], in_ap[r0:r0 + rr, c0:c1],
                   waits=merge(st["sc"][par], st["st"][par], deps),
                   sem=f"d_gi{par}")
        if mode == "lut":
            last = S.op("vector", lambda e: e.tensor_scalar(
                acc[0:rr, 0:w], ib[0:rr, 0:w], 0.0, l0col[0:rr, 0:1],
                ALU.mult, ALU.add),
                waits=merge([ld], st["st"][0], st["st"][1], st["sc"][par]))
            for j in range(1, 16):
                sl = j % 2
                m1 = S.op("vector", lambda e, j=j, sl=sl: e.tensor_scalar(
                    sb[sl][0:rr, 0:w], ib[0:rr, 0:w], j - 0.5,
                    dls[0:rr, j:j + 1], ALU.is_ge, ALU.mult),
                    waits=merge([ld], [last], st["sg"][sl]))
                last = S.op("vector", lambda e, sl=sl: e.tensor_tensor(
                    acc[0:rr, 0:w], acc[0:rr, 0:w], sb[sl][0:rr, 0:w], ALU.add),
                    waits=merge([m1], [last]))
                st["sg"][sl] = last
            st["sc"][par] = last
        elif mode == "deq":
            last = S.op("vector", lambda e: e.tensor_scalar(
                acc[0:rr, 0:w], ib[0:rr, 0:w], -128.0, None, ALU.add),
                waits=merge([ld], st["st"][0], st["st"][1], st["sc"][par]))
            st["sc"][par] = last
        else:
            t0 = S.op("vector", lambda e: e.tensor_scalar(
                acc[0:rr, 0:w], ib[0:rr, 0:w], -128.0, None, ALU.add),
                waits=merge([ld], st["st"][0], st["st"][1], st["sc"][par]))
            last = S.op("vector", lambda e, c0=c0, c1=c1: e.tensor_tensor(
                acc[0:rr, 0:w], acc[0:rr, 0:w], sdb[0:rr, c0:c1], ALU.mult),
                waits=merge([t0], deps))
            st["sc"][par] = last
        stk = S.dma("sync", out_ap[r0:r0 + rr, c0:c1], acc[0:rr, 0:w],
                    waits=[last], sem=f"d_gs{par}")
        st["st"][par] = stk
        if final:
            toks.append((c1, merge(stk, S.tokv(f"d_gs{par ^ 1}"))))

    CW = 1024
    for specs in groups:
        for nm, _, _, _, _, _ in specs:
            out[nm] = []
        nch = max((spec[4] + CW - 1) // CW for spec in specs)
        for ch in range(nch):
            for (nm, in_ap, out_ap, R, Cc, mode) in specs:
                c0, c1 = ch * CW, min((ch + 1) * CW, Cc)
                if c0 >= Cc:
                    continue
                toks = out[nm]
                nrt = (R + P - 1) // P
                for rt in range(nrt):
                    r0 = rt * P
                    rr = min(P, R - r0)
                    c.thunks.append(
                        lambda in_ap=in_ap, out_ap=out_ap, r0=r0, rr=rr,
                        c0=c0, c1=c1, mode=mode, toks=toks,
                        final=(rt == nrt - 1):
                        tile_thunk(in_ap, out_ap, r0, rr, c0, c1, mode, toks,
                                   final))
    return out


def ready(c, toks, limit):
    while not toks or toks[-1][0] < limit:
        if not c.thunks:
            raise RuntimeError(f"gather thunks exhausted before limit {limit}")
        c.pump(1)
    got = []
    for (lim, t) in toks:
        got.append(t)
        if lim >= limit:
            break
    return merge(*got)


def build_fused(cfg):
    c = Core(cfg)
    D, ND, NH = cfg.D, cfg.ND, cfg.NH
    I0, I1 = cfg.I0, cfg.I1
    SQ = cfg.S
    xin = c.din("xin", (SQ, D), F32)
    w0 = {k: c.din("w0" + k, (D, D), F32) for k in ("q", "k", "v", "o")}
    b0q = c.din("b0q", (P, ND), F32)
    b0k = c.din("b0k", (P, ND), F32)
    b0v = c.din("b0v", (P, ND), F32)
    b0o = c.din("b0o", (D,), F32)
    lut_in = c.din("lut", (16,), F32)
    g0g_i = c.din("g0g_i", (D, I0), I32)
    g0u_i = c.din("g0u_i", (D, I0), I32)
    g0d_i = c.din("g0d_i", (I0, D), I32)
    w1 = {k: c.din("w1" + k, (D, D), F32) for k in ("q", "k", "v", "o")}
    b1q = c.din("b1q", (P, NH), F32)
    b1k = c.din("b1k", (P, NH), F32)
    b1v = c.din("b1v", (D,), F32)
    b1o = c.din("b1o", (D,), F32)
    g1g_i = c.din("g1g_i", (D, I1), I32)
    g1u_i = c.din("g1u_i", (D, I1), I32)
    g1d_i = c.din("g1d_i", (I1, D), I32)
    sg_in = c.din("sg", (P, cfg.NO1), F32)
    su_in = c.din("su", (P, cfg.NO1), F32)
    sd_in = c.din("sdown", (D,), F32)
    xout = c.dout("xout", (SQ, D), F32)
    x1 = c.dtmp("x1", (SQ, D), F32)
    x2 = c.dtmp("x2", (SQ, D), F32)
    x3 = c.dtmp("x3", (SQ, D), F32)
    zscr = c.dtmp("zscr", (SQ, D), F32)
    wg0 = c.dtmp("wg0", (D, I0), BF16)
    wu0 = c.dtmp("wu0", (D, I0), BF16)
    wd0 = c.dtmp("wd0", (I0, D), BF16)
    wg1 = c.dtmp("wg1", (D, I1), BF16)
    wu1 = c.dtmp("wu1", (D, I1), BF16)
    wd1 = c.dtmp("wd1", (I1, D), BF16)

    with c.stack:
        c.setup_common(need_ones=True, need_tri=True)
        A = c.sbuf("A", (P, ND * SQ), BF16)
        M = c.sbuf("M", (P, max(ND * SQ, 31232)), BF16)
        U = c.sbuf("U", (P, 32192), BF16)
        zr = [carve(U[:, :], i * 2 * D, 2 * D, (P, D), F32) for i in range(2)]
        sq_b = carve(U[:, :], 4 * D, D, (P, D // 2), F32)
        xb = [U[:, 5 * D:6 * D], U[:, 5 * D:6 * D]]
        lb = {"zr": zr, "sq": sq_b, "cols": c.sbuf("lncols", (P, 8), F32)}
        EB0 = 15808
        eb = {"wb": [U[:, EB0 + i * 2048:EB0 + i * 2048 + 4 * cfg.WD]
                     for i in range(2)],
              "rb": [carve(U[:, :], EB0 + 4096 + i * 1024, 1024, (P, cfg.WD), F32)
                     for i in range(2)],
              "zb": [carve(U[:, :], EB0 + 6144 + i * 1024, 1024, (P, cfg.WD), F32)
                     for i in range(2)]}
        BIAS0 = EB0 + 8192
        bias_t = carve(U[:, :], BIAS0, 2 * D, (P, D), F32)
        G0 = BIAS0 + 2 * D
        gbuf = {
            "ib": [U[:, G0 + i * 1024:G0 + (i + 1) * 1024] for i in range(2)],
            "acc": [U[:, G0 + 2048:G0 + 3072], U[:, G0 + 2048:G0 + 3072]],
            "s": [U[:, G0 + 3072:G0 + 4096], U[:, G0 + 3072:G0 + 4096]],
        }
        HPG_ = min(cfg.NH, 2)
        VW_ = HPG_ * cfg.HD
        l1b = {
            "vg": U[:, 0:0 + cfg.NT * VW_],
            "wvb": U[:, 4096:4096 + ND * VW_],
            "qh": U[:, 8192:8192 + SQ],
            "kh": U[:, 10240:10240 + SQ],
            "wqb": [U[:, 12288 + i * 2048:12288 + i * 2048 + ND * P]
                    for i in range(2)],
            "wkb": [U[:, 16384 + i * 2048:16384 + i * 2048 + ND * P]
                    for i in range(2)],
            "pT": [U[:, 20480 + i * 512:20480 + i * 512 + cfg.W] for i in range(2)],
            "rr": carve(U[:, :], 21504, 1024, (P, cfg.W), F32),
        }
        ctx = M[:, 0:ND * cfg.S2]
        SC0 = ND * cfg.S2
        l0bufs = {
            "wA": [M[:, SC0 + i * 2048:SC0 + i * 2048 + ND * P] for i in range(2)],
            "wK": [M[:, SC0 + 4096 + i * 2048:SC0 + 4096 + i * 2048 + ND * P]
                   for i in range(2)],
            "m0": carve(U[:, :], 12288, 2048, (P, cfg.S2), F32),
            "t1b": carve(M[:, :], SC0 + 8192, 2048, (P, cfg.S2), F32),
            "phq": M[:, SC0 + 10240:SC0 + 10240 + cfg.S2],
            "phk": M[:, SC0 + 11264:SC0 + 11264 + cfg.S2],
            "vv": M[:, SC0 + 12288:SC0 + 12288 + cfg.S2],
            "kv": M[:, SC0 + 13312:SC0 + 13312 + cfg.S2],
            "kvc": carve(M[:, :], SC0 + 14336, 2048, (P, cfg.S2), F32),
            "zcol": c.sbuf("zcol", (P, 1), BF16),
            "carr": c.sbuf("carr", (P, ND), F32),
        }
        FW0 = 22528
        ffnb = {
            "gw": [M[:, FW0 + i * 2048:FW0 + i * 2048 + ND * P] for i in range(2)],
            "uw": [M[:, FW0 + 4096 + i * 2048:FW0 + 4096 + i * 2048 + ND * P]
                   for i in range(2)],
            "sgb": M[:, FW0 + 8192:FW0 + 8192 + min(512, SQ // 2)],
        }
        btile = c.sbuf("btile", (P, 3 * ND + 2 * NH), F32)
        sgt = c.sbuf("sgt", (P, cfg.NO1), F32)
        sut = c.sbuf("sut", (P, cfg.NO1), F32)
        sdb = c.sbuf("sdb", (P, D), F32)
        luts = c.sbuf("luts", (P, 16), F32)
        c.S.dma("sync", btile[:, 0:ND], b0q, sem="d_b")
        c.S.dma("sync", btile[:, ND:2 * ND], b0k, sem="d_b")
        c.S.dma("sync", btile[:, 2 * ND:3 * ND], b0v, sem="d_b")
        c.S.dma("sync", btile[:, 3 * ND:3 * ND + NH], b1q, sem="d_b")
        c.S.dma("sync", btile[:, 3 * ND + NH:3 * ND + 2 * NH], b1k, sem="d_b")
        c.S.dma("sync", sgt[:], sg_in, sem="d_b")
        c.S.dma("sync", sut[:], su_in, sem="d_b")
        c.S.dma("sync", sdb[:], sd_in.unsqueeze(0).broadcast_to((P, D)), sem="d_b")
        btok = c.S.dma("sync", luts[:], lut_in.unsqueeze(0).broadcast_to((P, 16)),
                       sem="d_b")
        nb = c.sbuf("nbtile", (P, 2 * ND), F32)
        nbt = c.S.op("vector", lambda e: e.tensor_scalar(
            nb[:, :], btile[:, 0:2 * ND], -1.0, None, ALU.mult),
            waits=merge(btok))
        bt0 = {"bq": btile[:, 0:ND], "bk": btile[:, ND:2 * ND],
               "bv": btile[:, 2 * ND:3 * ND],
               "nbq": nb[:, 0:ND], "nbk": nb[:, ND:2 * ND], "nbt": nbt}
        bt1 = {"bq": btile[:, 3 * ND:3 * ND + NH],
               "bk": btile[:, 3 * ND + NH:3 * ND + 2 * NH]}
        l0bufs["ztok"] = c.S.op("vector", lambda e: e.memset(l0bufs["zcol"][:], 0.0))

        dls = c.sbuf("dls", (P, 16), F32)
        luts_bf = c.sbuf("lutsbf", (P, 16), BF16)
        l0col = c.sbuf("l0col", (P, 1), F32)
        d0 = c.S.op("vector", lambda e: e.tensor_copy(luts_bf[:, :], luts[:, :]),
                    waits=merge(btok))
        d1 = c.S.op("vector", lambda e: e.tensor_tensor(
            dls[:, 1:16], luts_bf[:, 1:16], luts_bf[:, 0:15], ALU.subtract),
            waits=[d0])
        d2 = c.S.op("vector", lambda e: e.tensor_copy(l0col[:, :],
                                                      luts_bf[:, 0:1]),
                    waits=[d0])
        gdep = merge(btok, [d1], [d2])
        groups = [
            [("wg0", g0g_i, wg0, D, I0, "lut"), ("wu0", g0u_i, wu0, D, I0, "lut")],
            [("wd0", g0d_i, wd0, I0, D, "lut")],
            [("wg1", g1g_i, wg1, D, I1, "deq"), ("wu1", g1u_i, wu1, D, I1, "deq")],
            [("wd1", g1d_i, wd1, I1, D, "deqscale")],
        ]
        gtk = queue_gather(c, groups, luts_bf, dls, l0col, sdb, gbuf, gdep)

        def bias_view(e2c):
            return bias_t[:, e2c * cfg.WD:(e2c + 1) * cfg.WD]

        ttok = transpose_in(c, xin, A, [], "t", xb)
        bias_tok = c.S.dma("sync", bias_t, b0o.unsqueeze(0).broadcast_to((P, D)),
                           waits=merge(ttok), sem="d_bias")
        psA_free, psB_free, ctx_free = None, None, None
        fins = []
        for H in range(2):
            res = l0_attn_half(c, H, A, ctx, {"wq": w0["q"], "wk": w0["k"],
                                              "wv": w0["v"]}, bt0, l0bufs,
                               merge(ttok, btok, bt0["nbt"]), psA_free, psB_free,
                               ctx_free)

            def midT(ct, tt):
                return ctx[:, ct * cfg.S2 + tt * P: ct * cfg.S2 + (tt + 1) * P]

            def wp_blk(ct0, ng, e2c, w=w0["o"]):
                return w[ct0 * P:(ct0 + ng) * P,
                         e2c * cfg.WD:(e2c + 1) * cfg.WD].rearrange(
                    "(g p) w -> p g w", p=P)
            ep = epilogue(c, H, 2, midT, D, wp_blk, bias_view, xin, zscr,
                          merge(res["done"], [bias_tok]), "ep", "gpsimd",
                          merge(res["psA_free"], res["psB_free"]), eb)
            psA_free = ep["ev"]
            psB_free = ep["ev"]
            ctx_free = ep["mm"]
            fins.append(ln_phase(c, H, 2, zscr, x1, ep["st"], lb))
        chain = merge(*fins)

        ttok = transpose_in(c, x1, A, dict(lb["rowtoks"]), "t", xb,
                            ps_gate=merge(psA_free, psB_free))
        ftok = ffn_core_f(c, A, M, wg0, wu0, wd0, None, None, cfg.NO0,
                          x1, zscr, x2, merge(ttok),
                          eb, lb, ffnb, psA_free, psB_free,
                          gtk={"g": gtk["wg0"], "u": gtk["wu0"], "d": gtk["wd0"]})
        chain = ftok["done"]
        psA_free, psB_free = ftok["psA_free"], ftok["psB_free"]

        ttok = transpose_in(c, x2, A, dict(lb["rowtoks"]), "t", xb,
                            ps_gate=merge(ftok["psA_free"], ftok["psB_free"]))
        bvb_tok = c.S.dma("sync", bias_t, b1v.unsqueeze(0).broadcast_to((P, D)),
                          waits=merge(chain, [bias_tok]), sem="d_bias")
        res = l1_attn_f(c, A, M, {"wq": w1["q"], "wk": w1["k"], "wv": w1["v"]},
                        bt1, bias_t, merge(ttok, [bvb_tok]), l1b,
                        psA_free, psB_free)
        bias_tok = c.S.dma("sync", bias_t, b1o.unsqueeze(0).broadcast_to((P, D)),
                           waits=res["done"], sem="d_bias")
        fins = []
        ps_free = merge(res["psA_free"], res["psB_free"])
        for H in range(2):
            NTp = cfg.NT // 2

            def midT(ct, tt, H=H, NTp=NTp):
                t0 = (H * NTp + tt) * P
                return M[:, ct * SQ + t0: ct * SQ + t0 + P]

            def wp_blk(ct0, ng, e2c, w=w1["o"]):
                return w[ct0 * P:(ct0 + ng) * P,
                         e2c * cfg.WD:(e2c + 1) * cfg.WD].rearrange(
                    "(g p) w -> p g w", p=P)
            ep = epilogue(c, H, 2, midT, D, wp_blk, bias_view, x2, zscr,
                          merge(res["done"], [bias_tok]), "ep", "gpsimd",
                          ps_free, eb)
            ps_free = ep["ev"]
            fins.append(ln_phase(c, H, 2, zscr, x3, ep["st"], lb))
        chain = merge(*fins)
        psA_free = ps_free
        psB_free = ps_free

        ttok = transpose_in(c, x3, A, dict(lb["rowtoks"]), "t", xb,
                            ps_gate=ps_free)
        ftok = ffn_core_f(c, A, M, wg1, wu1, wd1, sgt, sut, cfg.NO1,
                          x3, zscr, xout, merge(ttok),
                          eb, lb, ffnb, psA_free, psB_free,
                          gtk={"g": gtk["wg1"], "u": gtk["wu1"], "d": gtk["wd1"]})
        c.S.op("sync", lambda e: None, waits=ftok["done"])
        c.S.emit(c.stack)
    return c.nc


def ffn_core_f(c, arena, M, wg_hbm, wu_hbm, wd_hbm, sg_t, su_t, NO,
               resid_hbm, z_hbm, out_hbm, deps, eb, lb, ffnb,
               psA_free0, psB_free0, gtk=None):
    cfg, S = c.cfg, c.S
    D, ND = cfg.D, cfg.ND
    SQ = cfg.S
    NP = 4 if cfg.NT >= 8 else 2
    SP_ = SQ // NP
    WH = min(cfg.W, SP_)
    nW2 = SP_ // WH
    hT = M[:, 0:NO * SP_]
    gw, uw, sgb = ffnb["gw"], ffnb["uw"], ffnb["sgb"]
    gtok = ffnb.setdefault("gtok", [None, None])
    utok = ffnb.setdefault("utok", [None, None])
    out_toks = []
    ep_prev = None
    hT_readers = ffnb.get("hT_readers")
    psg_free0 = psA_free0
    psu_free0 = psB_free0
    for PI in range(NP):
        psg_free = ep_prev["ev"] if ep_prev else psg_free0
        psu_free = ep_prev["ev"] if ep_prev else psu_free0
        sg_free = ffnb.get("sg_free")
        h_toks = []
        for ot in range(NO):
            b = ot % 2
            srcg = wg_hbm.rearrange("(nj p) o -> p nj o", p=P)[
                :, :, ot * P:(ot + 1) * P]
            srcu = wu_hbm.rearrange("(nj p) o -> p nj o", p=P)[
                :, :, ot * P:(ot + 1) * P]
            gw_rdy = ready(c, gtk["g"], (ot + 1) * P) if gtk else []
            gu_rdy = ready(c, gtk["u"], (ot + 1) * P) if gtk else []
            lg = S.dma("gpsimd", gw[b].rearrange("p (nj o) -> p nj o", nj=ND),
                       srcg, waits=merge(deps, gtok[b], gw_rdy), sem=f"d_gg{b}")
            lu = S.dma("sync", uw[b].rearrange("p (nj o) -> p nj o", nj=ND),
                       srcu, waits=merge(deps, utok[b], gu_rdy), sem=f"d_su{b}")
            c.pump(3)

            def mk(wl, ps, b, PI=PI):
                def em(e):
                    inst = None
                    for j in range(ND):
                        for cc in range(nW2):
                            c0 = PI * SP_ + cc * WH
                            inst = e.matmul(ps[:, cc * WH:(cc + 1) * WH],
                                            wl[b][:, j * P:(j + 1) * P],
                                            arena[:, j * SQ + c0: j * SQ + c0 + WH],
                                            start=(j == 0), stop=(j == ND - 1))
                    return inst
                return em
            mmg = S.op("tensor", mk(gw, c.psA, b), waits=merge([lg], deps, psg_free))
            gtok[b] = mmg
            mmu = S.op("tensor", mk(uw, c.psB, b), waits=merge([lu], deps, psu_free))
            utok[b] = mmu
            sg_sc = sg_t[:, ot:ot + 1] if sg_t is not None else 1.0
            s1 = S.op("scalar", lambda e, sg_sc=sg_sc: e.activation(
                sgb[:, 0:SP_], c.psA[:, 0:SP_], AF.Silu, bias=0.0, scale=sg_sc),
                waits=merge([mmg], sg_free))
            psg_free = s1
            dsth = hT[:, ot * SP_:(ot + 1) * SP_]
            hw = merge([mmu], [s1], hT_readers)
            if su_t is None:
                v1 = S.op("vector", lambda e, dsth=dsth: e.tensor_tensor(
                    dsth, c.psB[:, 0:SP_], sgb[:, 0:SP_], ALU.mult), waits=hw)
            else:
                su_sc = su_t[:, ot:ot + 1]
                v1 = S.op("vector", lambda e, dsth=dsth, su_sc=su_sc:
                          e.scalar_tensor_tensor(dsth, c.psB[:, 0:SP_], su_sc,
                                                 sgb[:, 0:SP_], ALU.mult, ALU.mult),
                          waits=hw)
            psu_free = v1
            sg_free = v1
            h_toks.append(v1)

        def midT(ct, tt):
            return hT[:, ct * SP_ + tt * P: ct * SP_ + (tt + 1) * P]

        def wp_blk(ct0, ng, e2c):
            return wd_hbm[ct0 * P:(ct0 + ng) * P,
                          e2c * cfg.WD:(e2c + 1) * cfg.WD].rearrange(
                "(g p) w -> p g w", p=P)
        ep = epilogue(c, PI, NP, midT, NO * P, wp_blk, None, resid_hbm, z_hbm,
                      merge(deps, *h_toks), "fe", "sync",
                      ps_free=merge(*h_toks, psg_free, psu_free), eb=eb,
                      wready=(lambda e2c: ready(c, gtk["d"], (e2c + 1) * cfg.WD))
                      if gtk else None)
        hT_readers = ep["mm"]
        ffnb["hT_readers"] = hT_readers
        ffnb["sg_free"] = sg_free
        ep_prev = ep
        out_toks.append(ln_phase(c, PI, NP, z_hbm, out_hbm, ep["st"], lb))
    return dict(done=merge(*out_toks), psA_free=ep_prev["ev"],
                psB_free=ep_prev["ev"])


def l1_attn_f(c, arena, oT, w_aps, bt, bv_bcast, deps, l1b,
              psA_free0, psB_free0):
    cfg, S = c.cfg, c.S
    D, ND, NT, NH = cfg.D, cfg.ND, cfg.NT, cfg.NH
    SQ = cfg.S
    HPG = min(cfg.NH, 2)
    VW = HPG * cfg.HD
    nW = SQ // cfg.W
    NQ = SQ // cfg.W
    perA = c.psA.shape[1] // cfg.W
    vg, wvb, qh, kh = l1b["vg"], l1b["wvb"], l1b["qh"], l1b["kh"]
    wqb, wkb, pT, rr = l1b["wqb"], l1b["wkb"], l1b["pT"], l1b["rr"]
    wq_tok = [None, None]
    wk_tok = [None, None]
    wv_tok = None
    psA_free = psA_free0
    psB_free = psB_free0
    sslot_free = [psB_free0, psB_free0]
    rslot_free = [psB_free0, psB_free0]
    pT_free = [None, None]
    rr_free = None
    qh_free = None
    kh_free = None
    o_toks = []
    for g in range(NH // HPG):
        srcv = w_aps["wv"].rearrange("(nj p) d -> p nj d", p=P)[
            :, :, g * VW:(g + 1) * VW]
        lv = S.dma("gpsimd", wvb.rearrange("p (nj d) -> p nj d", nj=ND), srcv,
                   waits=merge(deps, wv_tok), sem="d_wv0")
        vg_evs = []
        for tt in range(NT):
            ps = c.ps512(tt % perA)

            def em(e, tt=tt, ps=ps):
                inst = None
                for j in range(ND):
                    inst = e.matmul(ps[:, 0:VW],
                                    arena[:, j * SQ + tt * P: j * SQ + (tt + 1) * P],
                                    wvb[:, j * VW:(j + 1) * VW],
                                    start=(j == 0), stop=(j == ND - 1))
                return inst
            mmv = S.op("tensor", em, waits=merge([lv], deps,
                                                 psA_free if tt < perA else None,
                                                 vg_evs[tt - perA] if tt >= perA else None))
            ev = S.op("vector", lambda e, tt=tt, ps=ps, g=g: e.tensor_tensor(
                vg[:, tt * VW:(tt + 1) * VW], ps[:, 0:VW],
                bv_bcast[:, g * VW:(g + 1) * VW], ALU.add),
                waits=merge([mmv], wv_tok, deps))
            vg_evs.append(ev)
        wv_tok = merge(*vg_evs)
        psA_free = wv_tok
        head_evs = []
        for hh in range(HPG):
            h = g * HPG + hh
            b = h % 2
            srcq = w_aps["wq"].rearrange("(nj p) d -> p nj d", p=P)[
                :, :, h * P:(h + 1) * P]
            srck = w_aps["wk"].rearrange("(nj p) d -> p nj d", p=P)[
                :, :, h * P:(h + 1) * P]
            lq = S.dma("gpsimd", wqb[b].rearrange("p (nj d) -> p nj d", nj=ND),
                       srcq, waits=merge(deps, wq_tok[b]), sem=f"d_wq{b}")
            lk = S.dma("gpsimd", wkb[b].rearrange("p (nj d) -> p nj d", nj=ND),
                       srck, waits=merge(deps, wk_tok[b]), sem=f"d_wk{b}")

            def mkqk(wl, ps, b):
                def em(e):
                    inst = None
                    for j in range(ND):
                        for cc in range(nW):
                            inst = e.matmul(ps[:, cc * cfg.W:(cc + 1) * cfg.W],
                                            wl[b][:, j * P:(j + 1) * P],
                                            arena[:, j * SQ + cc * cfg.W:
                                                  j * SQ + (cc + 1) * cfg.W],
                                            start=(j == 0), stop=(j == ND - 1))
                    return inst
                return em
            mmq = S.op("tensor", mkqk(wqb, c.psA, b),
                       waits=merge([lq], deps, psA_free, qh_free))
            wq_tok[b] = mmq
            sq = S.op("scalar", lambda e, h=h: e.activation(
                qh[:, 0:SQ], c.psA[:, 0:SQ], AF.Identity, bias=bt["bq"][:, h:h + 1],
                scale=1.0),
                waits=merge([mmq], qh_free))
            mmk = S.op("tensor", mkqk(wkb, c.psB, b),
                       waits=merge([lk], deps, psB_free, kh_free,
                                   sslot_free[0], sslot_free[1],
                                   rslot_free[0], rslot_free[1]))
            wk_tok[b] = mmk
            sk = S.op("scalar", lambda e, h=h: e.activation(
                kh[:, 0:SQ], c.psB[:, 0:SQ], AF.Identity, bias=bt["bk"][:, h:h + 1],
                scale=1.0),
                waits=merge([mmk], kh_free))
            psB_free = sk
            evs = []
            for qg in range(NQ):
                c.pump(2)
                q0 = qg * cfg.W
                jmax = min(NT - 1, (q0 + cfg.W - 1) // P)
                rs_i = 2 + (qg % 2)
                rs = c.psB[:, rs_i * cfg.W:(rs_i + 1) * cfg.W]
                last_p = None
                for j in range(jmax + 1):
                    sl = j % 2
                    sT = c.psB[:, sl * cfg.W:(sl + 1) * cfg.W]
                    mm1 = S.op("tensor", lambda e, j=j, q0=q0, sT=sT: e.matmul(
                        sT[:, 0:cfg.W], kh[:, j * P:(j + 1) * P], qh[:, q0:q0 + cfg.W],
                        start=True, stop=True),
                        waits=merge([sq], [sk], sslot_free[sl]))
                    pb = pT[sl]
                    se = S.op("scalar", lambda e, pb=pb, sT=sT: e.activation(
                        pb[:, 0:cfg.W], sT[:, 0:cfg.W], AF.Exp, bias=0.0,
                        scale=cfg.SCALE),
                        waits=merge([mm1], pT_free[sl]))
                    sslot_free[sl] = se
                    off = j * P - q0
                    mtoks = [se]
                    if off > 0:
                        z1 = S.op("vector", lambda e, pb=pb, off=off:
                                  e.memset(pb[:, 0:off], 0.0), waits=[se])
                        mtoks.append(z1)
                    if 0 <= off <= cfg.W - P:
                        z2 = S.op("vector", lambda e, pb=pb, off=off: e.tensor_tensor(
                            pb[:, off:off + P], pb[:, off:off + P], c.tri[:], ALU.mult),
                            waits=merge([se], c.const_tok))
                        mtoks.append(z2)
                    ptok = merge(*mtoks)
                    mm2 = S.op("tensor", lambda e, pb=pb, rs=rs, j=j, jmax=jmax:
                               e.matmul(rs[:, 0:cfg.W], c.ones[:], pb[:, 0:cfg.W],
                                        start=(j == 0), stop=(j == jmax)),
                               waits=merge(ptok, c.const_tok,
                                           rslot_free[rs_i - 2] if j == 0 else None))
                    vsl = vg[:, j * VW + hh * cfg.HD: j * VW + hh * cfg.HD + P]
                    mm3 = S.op("tensor", lambda e, vsl=vsl, pb=pb, q0=q0, j=j, jmax=jmax:
                               e.matmul(c.psA[:, q0:q0 + cfg.W], vsl, pb[:, 0:cfg.W],
                                        start=(j == 0), stop=(j == jmax)),
                               waits=merge(ptok, wv_tok, [sq] if j == 0 else None))
                    last_p = merge([mm2], [mm3])
                    pT_free[sl] = last_p
                v1 = S.op("vector", lambda e, rs=rs: e.reciprocal(rr, rs[:, 0:cfg.W]),
                          waits=merge(last_p, rr_free))
                rslot_free[rs_i - 2] = v1
                dst = oT[:, h * SQ + q0: h * SQ + q0 + cfg.W]
                ev = S.op("vector", lambda e, dst=dst, q0=q0:
                          e.scalar_tensor_tensor(dst, c.psA[:, q0:q0 + cfg.W], 1.0,
                                                 rr, ALU.mult, ALU.mult),
                          waits=merge([v1], last_p))
                rr_free = ev
                evs.append(ev)
            psA_free = merge(*evs)
            qh_free = psA_free
            kh_free = psA_free
            head_evs.extend(evs)
            o_toks.extend(evs)
        wv_tok = merge(wv_tok, *head_evs)
    return dict(done=merge(*o_toks), psA_free=psA_free, psB_free=merge(
        sslot_free[0], sslot_free[1], rslot_free[0], rslot_free[1], psB_free))



def _colmajor(v, nd):
    return np.ascontiguousarray(np.asarray(v, np.float32).reshape(nd, P).T)


def kernel(L0_attn_x, L0_attn_Wq, L0_attn_bq, L0_attn_Wk, L0_attn_bk,
           L0_attn_Wv, L0_attn_bv, L0_attn_Wo, L0_attn_bo,
           L0_ffn_lut, L0_ffn_gate_idx, L0_ffn_up_idx, L0_ffn_down_idx,
           L1_attn_wq, L1_attn_bq, L1_attn_wk, L1_attn_bk, L1_attn_wv,
           L1_attn_bv, L1_attn_wo, L1_attn_bo,
           L1_ffn_wq_gate, L1_ffn_s_gate, L1_ffn_wq_up, L1_ffn_s_up,
           L1_ffn_wq_down, L1_ffn_s_down):
    cfg = FULL
    B = NCORES
    ident = np.eye(P, dtype=np.float32)
    ones = np.ones((P, P), dtype=np.float32)
    tri = np.triu(np.ones((P, P), dtype=np.float32))
    cv = [1e-5, 1.0 / cfg.D, -0.5, cfg.SCALE, 128.0]
    for j in range(1, 11):
        cv.append(-128.0 * (j - 0.5))
    cvals = np.array(cv, dtype=np.float32)

    def T(a):
        return np.ascontiguousarray(np.asarray(a).T)

    x = np.ascontiguousarray(np.asarray(L0_attn_x, np.float32))
    nc = build_fused(cfg)
    common = {
        "w0q": T(L0_attn_Wq), "w0k": T(L0_attn_Wk), "w0v": T(L0_attn_Wv),
        "w0o": T(L0_attn_Wo),
        "b0q": _colmajor(L0_attn_bq, cfg.ND), "b0k": _colmajor(L0_attn_bk, cfg.ND),
        "b0v": _colmajor(L0_attn_bv, cfg.ND),
        "b0o": np.asarray(L0_attn_bo, np.float32),
        "lut": np.asarray(L0_ffn_lut, np.float32),
        "g0g_i": np.ascontiguousarray(T(L0_ffn_gate_idx), dtype=np.int32),
        "g0u_i": np.ascontiguousarray(T(L0_ffn_up_idx), dtype=np.int32),
        "g0d_i": np.ascontiguousarray(T(L0_ffn_down_idx), dtype=np.int32),
        "w1q": T(L1_attn_wq), "w1k": T(L1_attn_wk), "w1v": T(L1_attn_wv),
        "w1o": T(L1_attn_wo),
        "b1q": _colmajor(L1_attn_bq, cfg.NH), "b1k": _colmajor(L1_attn_bk, cfg.NH),
        "b1v": np.asarray(L1_attn_bv, np.float32),
        "b1o": np.asarray(L1_attn_bo, np.float32),
        "g1g_i": np.ascontiguousarray(T(L1_ffn_wq_gate), dtype=np.int32),
        "g1u_i": np.ascontiguousarray(T(L1_ffn_wq_up), dtype=np.int32),
        "g1d_i": np.ascontiguousarray(T(L1_ffn_wq_down), dtype=np.int32),
        "sg": _colmajor(L1_ffn_s_gate, cfg.NO1),
        "su": _colmajor(L1_ffn_s_up, cfg.NO1),
        "sdown": np.asarray(L1_ffn_s_down, np.float32),
        "cident": ident, "cones": ones, "ctri": tri, "cvals": cvals,
    }
    in_maps = [dict(common, xin=np.ascontiguousarray(x[core]))
               for core in range(B)]
    res = run_bass_kernel_spmd(nc, in_maps, core_ids=list(range(B)))
    out = np.stack([res.results[i]["xout"] for i in range(B)], axis=0)
    return np.ascontiguousarray(out.astype(np.float32))


def kernel_unfused(L0_attn_x, L0_attn_Wq, L0_attn_bq, L0_attn_Wk, L0_attn_bk,
           L0_attn_Wv, L0_attn_bv, L0_attn_Wo, L0_attn_bo,
           L0_ffn_lut, L0_ffn_gate_idx, L0_ffn_up_idx, L0_ffn_down_idx,
           L1_attn_wq, L1_attn_bq, L1_attn_wk, L1_attn_bk, L1_attn_wv,
           L1_attn_bv, L1_attn_wo, L1_attn_bo,
           L1_ffn_wq_gate, L1_ffn_s_gate, L1_ffn_wq_up, L1_ffn_s_up,
           L1_ffn_wq_down, L1_ffn_s_down):
    cfg = FULL
    B = NCORES
    ident = np.eye(P, dtype=np.float32)
    ones = np.ones((P, P), dtype=np.float32)
    tri = np.triu(np.ones((P, P), dtype=np.float32))
    cv = [1e-5, 1.0 / cfg.D, -0.5, cfg.SCALE, 128.0]
    for j in range(1, 11):
        cv.append(-128.0 * (j - 0.5))
    cvals = np.array(cv, dtype=np.float32)

    def T(a):
        return np.ascontiguousarray(np.asarray(a).T)

    x = np.ascontiguousarray(np.asarray(L0_attn_x, np.float32))

    g_specs = [
        dict(name="g0g", rows=cfg.D // B, cols=cfg.I0, mode="lut"),
        dict(name="g0u", rows=cfg.D // B, cols=cfg.I0, mode="lut"),
        dict(name="g0d", rows=cfg.I0 // B, cols=cfg.D, mode="lut"),
        dict(name="g1g", rows=cfg.D // B, cols=cfg.I1, mode="deq"),
        dict(name="g1u", rows=cfg.D // B, cols=cfg.I1, mode="deq"),
        dict(name="g1d", rows=cfg.I1 // B, cols=cfg.D, mode="deqscale"),
    ]
    gT = {"g0g": T(L0_ffn_gate_idx), "g0u": T(L0_ffn_up_idx),
          "g0d": T(L0_ffn_down_idx),
          "g1g": T(L1_ffn_wq_gate), "g1u": T(L1_ffn_wq_up),
          "g1d": T(L1_ffn_wq_down)}
    nc_g = build_gather(cfg, g_specs)
    in_maps = []
    for core in range(B):
        m = {"lut": np.asarray(L0_ffn_lut, np.float32),
             "sdown": np.asarray(L1_ffn_s_down, np.float32)}
        for spec in g_specs:
            nm, R = spec["name"], spec["rows"]
            m[nm + "_i"] = np.ascontiguousarray(gT[nm][core * R:(core + 1) * R, :],
                                                dtype=np.int32)
        in_maps.append(m)
    res = run_bass_kernel_spmd(nc_g, in_maps, core_ids=list(range(B)))
    gath = {s["name"]: np.concatenate([res.results[i][s["name"] + "_o"]
                                       for i in range(B)], axis=0)
            for s in g_specs}

    nc_a = build_l0a(cfg)
    wqT, wkT, wvT, woT = T(L0_attn_Wq), T(L0_attn_Wk), T(L0_attn_Wv), T(L0_attn_Wo)
    in_maps = [{
        "xin": np.ascontiguousarray(x[core]),
        "wq": wqT, "wk": wkT, "wv": wvT, "wo": woT,
        "bq": _colmajor(L0_attn_bq, cfg.ND),
        "bk": _colmajor(L0_attn_bk, cfg.ND),
        "bv": _colmajor(L0_attn_bv, cfg.ND),
        "bo": np.asarray(L0_attn_bo, np.float32),
        "cident": ident, "cvals": cvals,
    } for core in range(B)]
    res = run_bass_kernel_spmd(nc_a, in_maps, core_ids=list(range(B)))
    x1 = [res.results[i]["xout"] for i in range(B)]

    nc_b = build_ffn(cfg, cfg.NO0, with_scales=False)
    in_maps = [{
        "xin": x1[core], "wg": gath["g0g"], "wu": gath["g0u"], "wd": gath["g0d"],
        "cident": ident, "cvals": cvals,
    } for core in range(B)]
    res = run_bass_kernel_spmd(nc_b, in_maps, core_ids=list(range(B)))
    x2 = [res.results[i]["xout"] for i in range(B)]

    nc_c = build_l1a(cfg)
    wqT, wkT, wvT, woT = T(L1_attn_wq), T(L1_attn_wk), T(L1_attn_wv), T(L1_attn_wo)
    in_maps = [{
        "xin": x2[core],
        "wq": wqT, "wk": wkT, "wv": wvT, "wo": woT,
        "bq": _colmajor(L1_attn_bq, cfg.NH),
        "bk": _colmajor(L1_attn_bk, cfg.NH),
        "bv": np.asarray(L1_attn_bv, np.float32),
        "bo": np.asarray(L1_attn_bo, np.float32),
        "cident": ident, "cones": ones, "ctri": tri, "cvals": cvals,
    } for core in range(B)]
    res = run_bass_kernel_spmd(nc_c, in_maps, core_ids=list(range(B)))
    x3 = [res.results[i]["xout"] for i in range(B)]

    nc_d = build_ffn(cfg, cfg.NO1, with_scales=True)
    in_maps = [{
        "xin": x3[core], "wg": gath["g1g"], "wu": gath["g1u"], "wd": gath["g1d"],
        "sg": _colmajor(L1_ffn_s_gate, cfg.NO1),
        "su": _colmajor(L1_ffn_s_up, cfg.NO1),
        "cident": ident, "cvals": cvals,
    } for core in range(B)]
    res = run_bass_kernel_spmd(nc_d, in_maps, core_ids=list(range(B)))
    out = np.stack([res.results[i]["xout"] for i in range(B)], axis=0)
    return np.ascontiguousarray(out.astype(np.float32))


# version 17
# speedup vs baseline: 1.0132x; 1.0132x over previous
import numpy as np
from contextlib import ExitStack

import concourse.bass as bass
import concourse.mybir as mybir
from concourse.bass_utils import run_bass_kernel_spmd

F32 = mybir.dt.float32
BF16 = mybir.dt.bfloat16
I32 = mybir.dt.int32
I8 = mybir.dt.uint8
AF = mybir.ActivationFunctionType
ALU = mybir.AluOpType
AX = mybir.AxisListType

NCORES = 8
P = 128


class CFG:
    def __init__(self, S=2048, D=2048, NH=16, I0=5504, I1=5632):
        self.S, self.D, self.NH, self.I0, self.I1 = S, D, NH, I0, I1
        self.NT = S // P
        self.ND = D // P
        self.HD = D // NH
        assert self.HD == 128, "builders assume head_dim == 128"
        self.NO0 = I0 // P
        self.NO1 = I1 // P
        self.S2 = S // 2
        self.NT2 = self.NT // 2
        self.W = min(512, S)
        self.WD = min(512, D)
        self.SCALE = float(self.HD) ** -0.5


FULL = CFG()



class Sched:
    ENGINES = ("tensor", "vector", "scalar", "sync", "gpsimd")

    def __init__(self, nc):
        self.nc = nc
        self.items = {e: [] for e in self.ENGINES}
        self.cnt = {}
        self.default_waits = []

    def tokv(self, sem):
        return (sem, self.cnt.get(sem, 0))

    def op(self, eng, emit, waits=(), sem=None, inc=1):
        sem = sem or ("c_" + eng)
        val = self.cnt.get(sem, 0) + inc
        self.cnt[sem] = val
        ws = tuple((s, v) for (s, v) in merge(list(waits), self.default_waits) if v > 0)
        self.items[eng].append((ws, emit, sem, inc))
        return (sem, val)

    def dma(self, eng, out, in_, waits=(), sem=None):
        assert sem is not None
        return self.op(eng, lambda e, o=out, i=in_: e.dma_start(o, i), waits, sem, 16)

    def emit(self, stack):
        names = {}
        for items in self.items.values():
            for ws, _, sem, _ in items:
                for s, _ in ws:
                    names.setdefault(s, None)
                names.setdefault(sem, None)
        for n in list(names):
            names[n] = stack.enter_context(self.nc.semaphore(n))
        blk = stack.enter_context(self.nc.Block())
        for eng in self.ENGINES:
            if not self.items[eng]:
                continue

            def make(eng):
                def body(e):
                    maxw = {}
                    for ws, emit, sem, inc in self.items[eng]:
                        for (wsem, wval) in ws:
                            if wval > maxw.get(wsem, 0):
                                e.wait_ge(names[wsem], wval)
                                maxw[wsem] = wval
                        inst = emit(e)
                        if inst is not None:
                            inst.then_inc(names[sem], inc)
                return body
            getattr(blk, eng)(make(eng))


def merge(*toks):
    best = {}
    for t in toks:
        if not t:
            continue
        if isinstance(t, tuple) and len(t) == 2 and isinstance(t[0], str):
            t = [t]
        for (s, v) in t:
            if v > best.get(s, 0):
                best[s] = v
    return list(best.items())


class Core:
    def __init__(self, cfg):
        self.cfg = cfg
        self.nc = bass.Bass("TRN2", target_bir_lowering=False)
        self.S = Sched(self.nc)
        self.stack = ExitStack()
        self.thunks = []

    def pump(self, n=1):
        for _ in range(min(n, len(self.thunks))):
            self.thunks.pop(0)()

    def drain(self):
        self.pump(len(self.thunks))

    def sbuf(self, name, shape, dtype):
        return self.stack.enter_context(self.nc.sbuf_tensor(name, list(shape), dtype))

    def psum(self, name, shape, dtype):
        return self.stack.enter_context(self.nc.psum_tensor(name, list(shape), dtype))

    def din(self, name, shape, dtype):
        return self.nc.dram_tensor(name, list(shape), dtype, kind="ExternalInput").ap()

    def dout(self, name, shape, dtype):
        return self.nc.dram_tensor(name, list(shape), dtype, kind="ExternalOutput").ap()

    def dtmp(self, name, shape, dtype):
        return self.nc.dram_tensor(name, list(shape), dtype, kind="Internal").ap()

    CONSTS = ()

    def setup_common(self, need_ones=False, need_tri=False):
        cfg = self.cfg
        base = [1e-5, 1.0 / cfg.D, -0.5, cfg.SCALE, 128.0]
        for j in range(1, 11):
            base.append(-128.0 * (j - 0.5))
        self.CONSTS = tuple(base)
        toks = []
        n = len(self.CONSTS)
        cvals = self.din("cvals", (n,), F32)
        self.cpool = self.sbuf("cpool", (P, n), F32)
        toks.append(self.S.dma("sync", self.cpool[:],
                               cvals.unsqueeze(0).broadcast_to((P, n)), sem="d_c2"))
        for i, v in enumerate(self.CONSTS):
            self.nc.const_aps.register(F32, float(v), self.cpool[:, i:i + 1])
        self.ident_f = self.din("cident", (P, P), F32)
        self.ident = self.sbuf("identb", (P, P), BF16)
        toks.append(self.S.dma("gpsimd", self.ident[:], self.ident_f, sem="d_c"))
        if need_ones:
            self.ones_f = self.din("cones", (P, P), F32)
            self.ones = self.sbuf("onesb", (P, P), BF16)
            toks.append(self.S.dma("gpsimd", self.ones[:], self.ones_f, sem="d_c"))
        if need_tri:
            self.tri_f = self.din("ctri", (P, P), F32)
            self.tri = self.sbuf("trib", (P, P), BF16)
            toks.append(self.S.dma("gpsimd", self.tri[:], self.tri_f, sem="d_c"))
        self.const_tok = merge(*toks)
        self.S.default_waits = self.const_tok
        self.psA = self.psum("psA", (P, 2048), F32)
        self.psB = self.psum("psB", (P, 2048), F32)

    def ps512(self, i):
        W = self.cfg.W
        per = self.psA.shape[1] // W
        t = self.psA if i < per else self.psB
        return t[:, (i % per) * W:((i % per) + 1) * W]



def carve(t, col0, ncol_b, shape, dtype):
    v = t[:, col0:col0 + ncol_b].bitcast(dtype)
    assert v.shape[1] >= shape[1], (v.shape, shape)
    return v[:, 0:shape[1]]


def carve_or_alloc(c, t, col0, shape, dtype, name):
    need = shape[1] * (2 if dtype == F32 else 1)
    if t.shape[1] >= col0 + need:
        return carve(t[:, :] if hasattr(t, "ap") else t, col0, need, shape, dtype)
    return c.sbuf(name, shape, dtype)[:, :]


def transpose_in(c, src_hbm, arena, deps, tag, xb, ps_gate=None):
    cfg, S = c.cfg, c.S
    psT = c.psB[:, 0:512].bitcast(BF16)
    buf_tok = [None, None]
    last_ev = None
    all_ev = []
    rowtoks = deps if isinstance(deps, dict) else None
    for ti in range(cfg.NT):
        b = ti % 2
        d = merge(rowtoks[ti]) if rowtoks is not None else merge(deps)
        ld = S.dma("gpsimd", xb[b][:, 0:cfg.D], src_hbm[ti * P:(ti + 1) * P, :],
                   waits=merge(d, buf_tok[0], buf_tok[1]), sem=f"d_{tag}x{b}")
        nb = (cfg.ND + 7) // 8
        for g in range(nb):
            js = list(range(g * 8, min((g + 1) * 8, cfg.ND)))

            def em(e, js=js, b=b):
                inst = None
                for k, j in enumerate(js):
                    inst = e.transpose(psT[:, k * P:(k + 1) * P],
                                       xb[b][:, j * P:(j + 1) * P], c.ident[:])
                return inst
            mm = S.op("tensor", em, waits=merge([ld], c.const_tok, last_ev,
                                                ps_gate))
            buf_tok[b] = mm
            j0, n = js[0], len(js)
            dst = arena[:, :].rearrange("p (nd s) -> p nd s", nd=cfg.ND)[
                :, j0:j0 + n, ti * P:(ti + 1) * P]
            src = psT[:, 0:n * P].rearrange("p (n q) -> p n q", n=n)
            last_ev = S.op("vector", lambda e, d=dst, s=src: e.tensor_copy(d, s),
                           waits=[mm])
            all_ev.append(last_ev)
    return merge(last_ev, *all_ev)


def epilogue(c, part, nparts, midT, Cdim, wp_blk, bias_view, resid_hbm, z_hbm,
             deps, tag, w_eng, ps_free, eb, wready=None):
    cfg, S = c.cfg, c.S
    NTp = cfg.NT // nparts
    assert NTp <= 8
    NC = Cdim // P
    ne2 = cfg.D // cfg.WD
    wb, rb, zb = eb["wb"], eb["rb"], eb["zb"]
    wtok = eb.setdefault("wtok", [None, None])
    rtok = eb.setdefault("rtok", [None, None])
    zslot = eb.setdefault("zslot", [None] * 2)
    prev_ev = ps_free
    st_all, ev_all, mm_all = [], [], []
    EBG = 4
    nblk = (NC + EBG - 1) // EBG
    for e2c in range(ne2):
        mm = None
        for blk in range(nblk):
            b = blk % 2
            ct0 = blk * EBG
            ng = min(EBG, NC - ct0)
            wrd = wready(e2c) if wready else []
            lt = S.dma(w_eng,
                       wb[b][:, 0:ng * cfg.WD].rearrange("p (g w) -> p g w", g=ng),
                       wp_blk(ct0, ng, e2c),
                       waits=merge(deps, wtok[b], wrd), sem=f"d_epw_{w_eng}{b}")

            def em(e, ct0=ct0, ng=ng, b=b):
                inst = None
                for g in range(ng):
                    for tt in range(NTp):
                        inst = e.matmul(c.ps512(tt)[:, 0:cfg.WD],
                                        midT(ct0 + g, tt),
                                        wb[b][:, g * cfg.WD:(g + 1) * cfg.WD],
                                        start=(ct0 + g == 0),
                                        stop=(ct0 + g == NC - 1))
                return inst
            mm = S.op("tensor", em, waits=merge([lt], deps,
                                                prev_ev if blk == 0 else None))
            wtok[b] = mm
        mm_all.append(mm)
        c.pump(2)
        evs = []
        for tt in range(NTp):
            trow = part * NTp + tt
            b = tt % 2
            zi = tt % 2
            lr = S.dma("sync", rb[b][:, 0:cfg.WD],
                       resid_hbm[trow * P:(trow + 1) * P,
                                 e2c * cfg.WD:(e2c + 1) * cfg.WD],
                       waits=merge(deps, rtok[b]), sem=f"d_epr{b}")
            ps = c.ps512(tt)
            bv = bias_view(e2c) if bias_view is not None else None
            if bv is not None:
                t1 = S.op("vector", lambda e, zi=zi, ps=ps, bv=bv:
                          e.tensor_tensor(zb[zi][:, 0:cfg.WD], ps[:, 0:cfg.WD],
                                          bv, ALU.add),
                          waits=merge([mm], zslot[zi], deps))
                t2 = S.op("vector", lambda e, zi=zi, b=b:
                          e.tensor_tensor(zb[zi][:, 0:cfg.WD], zb[zi][:, 0:cfg.WD],
                                          rb[b][:, 0:cfg.WD], ALU.add),
                          waits=merge([t1], [lr]))
            else:
                t2 = S.op("vector", lambda e, zi=zi, b=b, ps=ps:
                          e.tensor_tensor(zb[zi][:, 0:cfg.WD], ps[:, 0:cfg.WD],
                                          rb[b][:, 0:cfg.WD], ALU.add),
                          waits=merge([mm], [lr], zslot[zi]))
            rtok[b] = t2
            ev_all.append(t2)
            evs.append(t2)
            st = S.dma("scalar", z_hbm[trow * P:(trow + 1) * P,
                                       e2c * cfg.WD:(e2c + 1) * cfg.WD],
                       zb[zi][:, 0:cfg.WD], waits=[t2], sem=f"d_z{zi}")
            zslot[zi] = st
            st_all.append(st)
        prev_ev = merge(*evs)
    return dict(st=merge(*st_all), ev=merge(*ev_all), mm=merge(*mm_all))


def ln_phase(c, part, nparts, z_hbm, out_hbm, deps, lb, eps=1e-5):
    cfg, S = c.cfg, c.S
    NTp = cfg.NT // nparts
    zr, sq = lb["zr"], lb["sq"]
    cols = lb["cols"]
    rs, mneg, ssum, lnv, rstd, bb = (cols[:, i:i + 1] for i in range(6))
    zfree = lb.setdefault("zfree", [None, None])
    xfree = lb.setdefault("xfree", [None, None])
    state = lb.setdefault("state", {})
    sts = []
    for tt in range(NTp):
        trow = part * NTp + tt
        b = tt % 2
        ld = S.dma("sync", zr[b][:, 0:cfg.D], z_hbm[trow * P:(trow + 1) * P, :],
                   waits=merge(deps, zfree[b]), sem=f"d_lnl{b}")
        t1 = S.op("vector", lambda e, b=b: e.reduce_sum(rs, zr[b][:, 0:cfg.D], AX.X),
                  waits=merge([ld], state.get("chain")))
        t2 = S.op("vector", lambda e: e.tensor_scalar(mneg, rs, -1.0 / cfg.D,
                                                      None, ALU.mult),
                  waits=[t1])
        D2 = cfg.D // 2
        ss2 = cols[:, 6:7]
        t3a = S.op("scalar", lambda e, b=b: e.activation(sq[:, 0:D2],
                                                         zr[b][:, 0:D2], AF.Square,
                                                         bias=mneg, scale=1.0,
                                                         accum_out=ssum),
                   waits=merge([t2], state.get("sq")))
        t3b = S.op("scalar", lambda e, b=b: e.activation(sq[:, 0:D2],
                                                         zr[b][:, D2:cfg.D],
                                                         AF.Square,
                                                         bias=mneg, scale=1.0,
                                                         accum_out=ss2),
                   waits=[t3a])
        t3 = S.op("vector", lambda e: e.tensor_tensor(ssum, ssum, ss2, ALU.add),
                  waits=[t3b])
        t4 = S.op("scalar", lambda e: e.activation(lnv, ssum, AF.Ln,
                                                   bias=float(eps), scale=1.0 / cfg.D),
                  waits=[t3])
        t5 = S.op("scalar", lambda e: e.activation(rstd, lnv, AF.Exp,
                                                   bias=0.0, scale=-0.5),
                  waits=[t4])
        t6 = S.op("vector", lambda e: e.tensor_tensor(bb, mneg, rstd, ALU.mult),
                  waits=[t5])
        t7 = S.op("scalar", lambda e, b=b: e.activation(zr[b][:, 0:cfg.D],
                                                        zr[b][:, 0:cfg.D], AF.Identity,
                                                        bias=bb, scale=rstd),
                  waits=merge([t6], xfree[b]))
        st = S.dma("scalar", out_hbm[trow * P:(trow + 1) * P, :], zr[b][:, 0:cfg.D],
                   waits=[t7], sem=f"d_o{b}")
        zfree[b] = st
        xfree[b] = st
        state["chain"] = t6
        state["sq"] = t7
        sts.append(st)
        rowtoks = lb.setdefault("rowtoks", {})
        rowtoks[trow] = st
    return merge(*sts)



def l0_attn_half(c, H, arena, ctx, w_aps, bt, bufs, deps, psA_free, psB_free,
                 ctx_free):
    cfg, S = c.cfg, c.S
    ND, S2, SQ = cfg.ND, cfg.S2, cfg.S
    nW = S2 // min(cfg.W, S2)
    WH = min(cfg.W, S2)
    wA, wK = bufs["wA"], bufs["wK"]
    m0, t1b = bufs["m0"], bufs["t1b"]
    phq, phk, vv, kv = bufs["phq"], bufs["phk"], bufs["vv"], bufs["kv"]
    kvc = bufs["kvc"]
    zcol = bufs["zcol"]
    carr = bufs["carr"]
    ztok = bufs["ztok"]
    wa_tok = bufs.setdefault("wa_tok", [None, None])
    wk_tok = bufs.setdefault("wk_tok", [None, None])
    scr = bufs.setdefault("scr", None)
    kscr = bufs.setdefault("kscr", None)
    vv_free = bufs.setdefault("vv_free", None)
    kv_free = bufs.setdefault("kv_free", None)
    kvc_free = bufs.setdefault("kvc_free", None)
    ctx_tok = []

    def w_src(k, et):
        return w_aps[k].rearrange("(nj p) d -> p nj d", p=P)[:, :, et * P:(et + 1) * P]

    def mk_mm(wl, ps):
        def em(e):
            inst = None
            for j in range(ND):
                for cc in range(nW):
                    c0 = H * S2 + cc * WH
                    inst = e.matmul(ps[:, cc * WH:(cc + 1) * WH],
                                    wl[:, j * P:(j + 1) * P],
                                    arena[:, j * SQ + c0: j * SQ + c0 + WH],
                                    start=(j == 0), stop=(j == ND - 1))
            return inst
        return em

    for et in range(ND):
        c.pump(4)
        b = et % 2
        wAb, wKb = wA[b], wK[b]
        bq = bt["bq"][:, et:et + 1]
        bk = bt["bk"][:, et:et + 1]
        bvc = bt["bv"][:, et:et + 1]
        lq = c.S.dma("gpsimd", wAb.rearrange("p (nj d) -> p nj d", nj=ND),
                     w_src("wq", et), waits=merge(deps, bufs["wa_tok"][b]),
                     sem=f"d_wa{b}")
        mmq = c.S.op("tensor", mk_mm(wAb, c.psA), waits=merge([lq], deps, psA_free))
        nbq = bt["nbq"][:, et:et + 1]
        vq1 = c.S.op("scalar", lambda e, nbq=nbq: e.activation(
            m0[:, 0:S2], c.psA[:, 0:S2], AF.Relu, bias=nbq, scale=-1.0),
            waits=merge([mmq], scr))
        sq1 = c.S.op("scalar", lambda e: e.activation(m0[:, 0:S2], m0[:, 0:S2],
                                                      AF.Exp, bias=0.0, scale=-1.0),
                     waits=[vq1])
        vq2 = c.S.op("scalar", lambda e, bq=bq: e.activation(
            t1b[:, 0:S2], c.psA[:, 0:S2], AF.Relu, bias=bq, scale=1.0),
            waits=merge([mmq], scr, [vq1]))
        vq3 = c.S.op("vector", lambda e: e.tensor_tensor(
            phq[:, 0:S2], t1b[:, 0:S2], m0[:, 0:S2], ALU.add),
            waits=merge([sq1], [vq2], scr))
        lv = c.S.dma("gpsimd", wAb.rearrange("p (nj d) -> p nj d", nj=ND),
                     w_src("wv", et), waits=merge([mmq], deps), sem=f"d_wa{b}")
        mmv = c.S.op("tensor", mk_mm(wAb, c.psA), waits=merge([lv], [vq1], [vq2], deps))
        bufs["wa_tok"][b] = mmv
        sv1 = c.S.op("scalar", lambda e, bvc=bvc: e.activation(
            vv[:, 0:S2], c.psA[:, 0:S2], AF.Identity, bias=bvc, scale=1.0),
            waits=merge([mmv], vv_free))
        psA_free = sv1
        lk = c.S.dma("gpsimd", wKb.rearrange("p (nj d) -> p nj d", nj=ND),
                     w_src("wk", et), waits=merge(deps, bufs["wk_tok"][b]),
                     sem=f"d_wk{b}")
        mmk = c.S.op("tensor", mk_mm(wKb, c.psB), waits=merge([lk], deps, psB_free))
        bufs["wk_tok"][b] = mmk
        nbk = bt["nbk"][:, et:et + 1]
        vk1 = c.S.op("scalar", lambda e, nbk=nbk: e.activation(
            t1b[:, 0:S2], c.psB[:, 0:S2], AF.Relu, bias=nbk, scale=-1.0),
            waits=merge([mmk], [vq3], kscr))
        sk1 = c.S.op("scalar", lambda e: e.activation(t1b[:, 0:S2], t1b[:, 0:S2],
                                                      AF.Exp, bias=0.0, scale=-1.0),
                     waits=[vk1])
        vk2 = c.S.op("scalar", lambda e, bk=bk: e.activation(
            m0[:, 0:S2], c.psB[:, 0:S2], AF.Relu, bias=bk, scale=1.0),
            waits=merge([mmk], [vq3], [sq1], [vk1]))
        vk3 = c.S.op("vector", lambda e: e.tensor_tensor(
            phk[:, 0:S2], m0[:, 0:S2], t1b[:, 0:S2], ALU.add),
            waits=merge([sk1], [vk2], kscr))
        psB_free = vk2
        bufs["kscr"] = None
        v1 = c.S.op("vector", lambda e: e.tensor_tensor(
            kv[:, 0:S2], phk[:, 0:S2], vv[:, 0:S2], ALU.mult),
            waits=merge([vk3], [sv1], kv_free))
        bufs["vv_free"] = v1
        init = 0.0 if H == 0 else carr[:, et:et + 1]
        v2 = c.S.op("vector", lambda e, init=init: e.tensor_tensor_scan(
            kvc[:, 0:S2], kv[:, 0:S2],
            zcol[:, 0:1].broadcast_to((P, S2)), init, ALU.add, ALU.add),
            waits=merge([v1], [ztok], kvc_free,
                        bufs.get("carr_w%d" % et) if H == 1 else None))
        bufs["kv_free"] = v2
        cw = c.S.op("vector", lambda e, et=et: e.tensor_copy(
            carr[:, et:et + 1], kvc[:, S2 - 1:S2]), waits=[v2]) if H == 0 else None
        if H == 0:
            bufs["carr_w%d" % et] = cw
        dst = ctx[:, et * S2:(et + 1) * S2]
        v3 = c.S.op("vector", lambda e, dst=dst: e.scalar_tensor_tensor(
            dst, kvc[:, 0:S2], cfg.SCALE, phq[:, 0:S2], ALU.mult, ALU.mult),
            waits=merge([v2], [vq3], ctx_free, [cw] if cw else None))
        bufs["kvc_free"] = v3
        bufs["scr"] = v3
        bufs["kscr"] = vk3
        scr = v3
        kscr = vk3
        vv_free = v1
        kv_free = v2
        kvc_free = v3
        ctx_tok.append(v3)
    return dict(done=merge(*ctx_tok), psA_free=psA_free, psB_free=psB_free)



def l1_attn(c, arena, oT, w_aps, bt, bv_bcast, deps):
    cfg, S = c.cfg, c.S
    D, ND, NT, NH = cfg.D, cfg.ND, cfg.NT, cfg.NH
    SQ = cfg.S
    HPG = min(cfg.NH, 2)
    VW = HPG * cfg.HD
    nW = SQ // cfg.W
    NQ = SQ // cfg.W
    perA = c.psA.shape[1] // cfg.W
    vg = c.sbuf("l1_vg", (P, NT * VW), BF16)
    qh = c.sbuf("l1_qh", (P, SQ), BF16)
    kh = c.sbuf("l1_kh", (P, SQ), BF16)
    wqb = [c.sbuf(f"l1_wq{i}", (P, ND * P), BF16) for i in range(2)]
    wkb = [c.sbuf(f"l1_wk{i}", (P, ND * P), BF16) for i in range(2)]
    wvb = c.sbuf("l1_wv", (P, ND * VW), BF16)
    pT = [c.sbuf(f"l1_pT{i}", (P, cfg.W), BF16) for i in range(2)]
    rr = c.sbuf("l1_rr", (P, cfg.W), F32)
    wq_tok = [None, None]
    wk_tok = [None, None]
    wv_tok = None
    psA_free = None
    psB_free = None
    sslot_free = [None, None]
    rslot_free = [None, None]
    pT_free = [None, None]
    rr_free = None
    qh_free = None
    kh_free = None
    o_toks = []

    for g in range(NH // HPG):
        srcv = w_aps["wv"].rearrange("(nj p) d -> p nj d", p=P)[
            :, :, g * VW:(g + 1) * VW]
        lv = S.dma("gpsimd", wvb[:].rearrange("p (nj d) -> p nj d", nj=ND), srcv,
                   waits=merge(deps, wv_tok), sem="d_wv0")
        vg_evs = []
        for tt in range(NT):
            ps = c.ps512(tt % perA)

            def em(e, tt=tt, ps=ps):
                inst = None
                for j in range(ND):
                    inst = e.matmul(ps[:, 0:VW],
                                    arena[:, j * SQ + tt * P: j * SQ + (tt + 1) * P],
                                    wvb[:, j * VW:(j + 1) * VW],
                                    start=(j == 0), stop=(j == ND - 1))
                return inst
            mmv = S.op("tensor", em, waits=merge([lv], deps,
                                                 psA_free if tt < perA else None,
                                                 vg_evs[tt - perA] if tt >= perA else None))
            ev = S.op("vector", lambda e, tt=tt, ps=ps, g=g: e.tensor_tensor(
                vg[:, tt * VW:(tt + 1) * VW], ps[:, 0:VW],
                bv_bcast[:, g * VW:(g + 1) * VW], ALU.add),
                waits=merge([mmv], wv_tok))
            vg_evs.append(ev)
        wv_tok = merge(*vg_evs)
        psA_free = wv_tok
        head_evs = []
        for hh in range(HPG):
            h = g * HPG + hh
            b = h % 2
            srcq = w_aps["wq"].rearrange("(nj p) d -> p nj d", p=P)[
                :, :, h * P:(h + 1) * P]
            srck = w_aps["wk"].rearrange("(nj p) d -> p nj d", p=P)[
                :, :, h * P:(h + 1) * P]
            lq = S.dma("gpsimd", wqb[b][:].rearrange("p (nj d) -> p nj d", nj=ND),
                       srcq, waits=merge(deps, wq_tok[b]), sem=f"d_wq{b}")
            lk = S.dma("gpsimd", wkb[b][:].rearrange("p (nj d) -> p nj d", nj=ND),
                       srck, waits=merge(deps, wk_tok[b]), sem=f"d_wk{b}")

            def mkqk(wl, ps, b):
                def em(e):
                    inst = None
                    for j in range(ND):
                        for cc in range(nW):
                            inst = e.matmul(ps[:, cc * cfg.W:(cc + 1) * cfg.W],
                                            wl[b][:, j * P:(j + 1) * P],
                                            arena[:, j * SQ + cc * cfg.W:
                                                  j * SQ + (cc + 1) * cfg.W],
                                            start=(j == 0), stop=(j == ND - 1))
                    return inst
                return em
            mmq = S.op("tensor", mkqk(wqb, c.psA, b),
                       waits=merge([lq], deps, psA_free, qh_free))
            wq_tok[b] = mmq
            sq = S.op("scalar", lambda e, h=h: e.activation(
                qh[:], c.psA[:, 0:SQ], AF.Identity, bias=bt["bq"][:, h:h + 1],
                scale=1.0),
                waits=merge([mmq], qh_free))
            mmk = S.op("tensor", mkqk(wkb, c.psB, b),
                       waits=merge([lk], deps, psB_free, kh_free,
                                   sslot_free[0], sslot_free[1],
                                   rslot_free[0], rslot_free[1]))
            wk_tok[b] = mmk
            sk = S.op("scalar", lambda e, h=h: e.activation(
                kh[:], c.psB[:, 0:SQ], AF.Identity, bias=bt["bk"][:, h:h + 1],
                scale=1.0),
                waits=merge([mmk], kh_free))
            psB_free = sk
            evs = []
            for qg in range(NQ):
                q0 = qg * cfg.W
                jmax = min(NT - 1, (q0 + cfg.W - 1) // P)
                rs_i = 2 + (qg % 2)
                rs = c.psB[:, rs_i * cfg.W:(rs_i + 1) * cfg.W]
                last_p = None
                for j in range(jmax + 1):
                    sl = j % 2
                    sT = c.psB[:, sl * cfg.W:(sl + 1) * cfg.W]
                    mm1 = S.op("tensor", lambda e, j=j, q0=q0, sT=sT: e.matmul(
                        sT[:, 0:cfg.W], kh[:, j * P:(j + 1) * P], qh[:, q0:q0 + cfg.W],
                        start=True, stop=True),
                        waits=merge([sq], [sk], sslot_free[sl]))
                    pb = pT[sl]
                    se = S.op("scalar", lambda e, pb=pb, sT=sT: e.activation(
                        pb[:], sT[:, 0:cfg.W], AF.Exp, bias=0.0, scale=cfg.SCALE),
                        waits=merge([mm1], pT_free[sl]))
                    sslot_free[sl] = se
                    off = j * P - q0
                    mtoks = [se]
                    if off > 0:
                        z1 = S.op("vector", lambda e, pb=pb, off=off:
                                  e.memset(pb[:, 0:off], 0.0), waits=[se])
                        mtoks.append(z1)
                    if 0 <= off <= cfg.W - P:
                        z2 = S.op("vector", lambda e, pb=pb, off=off: e.tensor_tensor(
                            pb[:, off:off + P], pb[:, off:off + P], c.tri[:], ALU.mult),
                            waits=merge([se], c.const_tok))
                        mtoks.append(z2)
                    ptok = merge(*mtoks)
                    mm2 = S.op("tensor", lambda e, pb=pb, rs=rs, j=j, jmax=jmax:
                               e.matmul(rs[:, 0:cfg.W], c.ones[:], pb[:],
                                        start=(j == 0), stop=(j == jmax)),
                               waits=merge(ptok, c.const_tok,
                                           rslot_free[rs_i - 2] if j == 0 else None))
                    vsl = vg[:, j * VW + hh * cfg.HD: j * VW + hh * cfg.HD + P]
                    mm3 = S.op("tensor", lambda e, vsl=vsl, pb=pb, q0=q0, j=j, jmax=jmax:
                               e.matmul(c.psA[:, q0:q0 + cfg.W], vsl, pb[:],
                                        start=(j == 0), stop=(j == jmax)),
                               waits=merge(ptok, wv_tok, [sq] if j == 0 else None))
                    last_p = merge([mm2], [mm3])
                    pT_free[sl] = last_p
                v1 = S.op("vector", lambda e, rs=rs: e.reciprocal(rr[:], rs[:, 0:cfg.W]),
                          waits=merge(last_p, rr_free))
                rslot_free[rs_i - 2] = v1
                dst = oT[:, h * SQ + q0: h * SQ + q0 + cfg.W]
                ev = S.op("vector", lambda e, dst=dst, q0=q0:
                          e.scalar_tensor_tensor(dst, c.psA[:, q0:q0 + cfg.W], 1.0,
                                                 rr[:], ALU.mult, ALU.mult),
                          waits=merge([v1], last_p))
                rr_free = ev
                evs.append(ev)
            psA_free = merge(*evs)
            qh_free = psA_free
            kh_free = psA_free
            head_evs.extend(evs)
            o_toks.extend(evs)
        wv_tok = merge(wv_tok, *head_evs)
    return dict(done=merge(*o_toks), psA_free=psA_free, psB_free=merge(
        sslot_free[0], sslot_free[1], rslot_free[0], rslot_free[1], psB_free))



def ffn_core(c, arena, hT, wg_hbm, wu_hbm, wd_hbm, sg_t, su_t, NO,
             resid_hbm, z_hbm, out_hbm, deps, eb, lb):
    cfg, S = c.cfg, c.S
    D, ND = cfg.D, cfg.ND
    SQ = cfg.S
    NP = 4 if cfg.NT >= 8 else 2
    SP_ = SQ // NP
    WH = min(cfg.W, SP_)
    nW2 = SP_ // WH
    gw = [c.sbuf(f"ff_gw{i}", (P, ND * P), BF16) for i in range(2)]
    uw = [c.sbuf(f"ff_uw{i}", (P, ND * P), BF16) for i in range(2)]
    sgb = c.sbuf("ff_sg", (P, SP_), BF16)
    gtok = [None, None]
    utok = [None, None]
    out_toks = []
    ep_prev = None
    hT_readers = None
    for PI in range(NP):
        psg_free = ep_prev["ev"] if ep_prev else None
        psu_free = ep_prev["ev"] if ep_prev else None
        sg_free = None
        h_toks = []
        for ot in range(NO):
            b = ot % 2
            srcg = wg_hbm.rearrange("(nj p) o -> p nj o", p=P)[
                :, :, ot * P:(ot + 1) * P]
            srcu = wu_hbm.rearrange("(nj p) o -> p nj o", p=P)[
                :, :, ot * P:(ot + 1) * P]
            lg = S.dma("sync", gw[b][:].rearrange("p (nj o) -> p nj o", nj=ND),
                       srcg, waits=merge(deps, gtok[b]), sem=f"d_g{b}")
            lu = S.dma("sync", uw[b][:].rearrange("p (nj o) -> p nj o", nj=ND),
                       srcu, waits=merge(deps, utok[b]), sem=f"d_u{b}")

            def mk(wl, ps, b, PI=PI):
                def em(e):
                    inst = None
                    for j in range(ND):
                        for cc in range(nW2):
                            c0 = PI * SP_ + cc * WH
                            inst = e.matmul(ps[:, cc * WH:(cc + 1) * WH],
                                            wl[b][:, j * P:(j + 1) * P],
                                            arena[:, j * SQ + c0: j * SQ + c0 + WH],
                                            start=(j == 0), stop=(j == ND - 1))
                    return inst
                return em
            mmg = S.op("tensor", mk(gw, c.psA, b), waits=merge([lg], deps, psg_free))
            gtok[b] = mmg
            mmu = S.op("tensor", mk(uw, c.psB, b), waits=merge([lu], deps, psu_free))
            utok[b] = mmu
            sg_sc = sg_t[:, ot:ot + 1] if sg_t is not None else 1.0
            s1 = S.op("scalar", lambda e, sg_sc=sg_sc: e.activation(
                sgb[:, 0:SP_], c.psA[:, 0:SP_], AF.Silu, bias=0.0, scale=sg_sc),
                waits=merge([mmg], sg_free))
            psg_free = s1
            dsth = hT[:, ot * SP_:(ot + 1) * SP_]
            hw = merge([mmu], [s1], hT_readers if PI > 0 else None)
            if su_t is None:
                v1 = S.op("vector", lambda e, dsth=dsth: e.tensor_tensor(
                    dsth, c.psB[:, 0:SP_], sgb[:, 0:SP_], ALU.mult), waits=hw)
            else:
                su_sc = su_t[:, ot:ot + 1]
                v1 = S.op("vector", lambda e, dsth=dsth, su_sc=su_sc:
                          e.scalar_tensor_tensor(dsth, c.psB[:, 0:SP_], su_sc,
                                                 sgb[:, 0:SP_], ALU.mult, ALU.mult),
                          waits=hw)
            psu_free = v1
            sg_free = v1
            h_toks.append(v1)

        def midT(ct, tt):
            return hT[:, ct * SP_ + tt * P: ct * SP_ + (tt + 1) * P]

        def wp_blk(ct0, ng, e2c):
            return wd_hbm[ct0 * P:(ct0 + ng) * P,
                          e2c * cfg.WD:(e2c + 1) * cfg.WD].rearrange(
                "(g p) w -> p g w", p=P)
        ep = epilogue(c, PI, NP, midT, NO * P, wp_blk, None, resid_hbm, z_hbm,
                      merge(deps, *h_toks), "fe", "sync",
                      ps_free=merge(*h_toks, psg_free, psu_free), eb=eb)
        hT_readers = ep["mm"]
        ep_prev = ep
        out_toks.append(ln_phase(c, PI, NP, z_hbm, out_hbm, ep["st"], lb))
    return merge(*out_toks)



def _mk_eb(c, tag, wb=None):
    return {
        "wb": wb or [c.sbuf(f"{tag}_wb{i}", (P, 4 * c.cfg.WD), BF16)[:, :]
                     for i in range(2)],
        "rb": [c.sbuf(f"{tag}_rb{i}", (P, c.cfg.WD), F32) for i in range(2)],
        "zb": [c.sbuf(f"{tag}_zb{i}", (P, c.cfg.WD), F32) for i in range(2)],
    }


def _mk_lb_carve(c, t, col0):
    D = c.cfg.D
    f32w = 2 * D
    return {
        "zr": [carve(t, col0 + i * f32w, f32w, (P, D), F32) for i in range(2)],
        "xo": [carve(t, col0 + (2 + i) * f32w, f32w, (P, D), F32) for i in range(2)],
        "sq": carve(t, col0 + 4 * f32w, f32w, (P, D), F32),
        "cols": carve(t, col0 + 5 * f32w, 16, (P, 8), F32),
    }


def _mk_lb_alloc(c, tag):
    D = c.cfg.D
    return {
        "zr": [c.sbuf(f"{tag}_zr{i}", (P, D), F32) for i in range(2)],
        "xo": [c.sbuf(f"{tag}_xo{i}", (P, D), F32) for i in range(2)],
        "sq": c.sbuf(f"{tag}_sq", (P, D), F32),
        "cols": c.sbuf(f"{tag}_cols", (P, 8), F32),
    }


def build_l0a(cfg):
    c = Core(cfg)
    D, ND = cfg.D, cfg.ND
    xin = c.din("xin", (cfg.S, D), F32)
    wq = c.din("wq", (D, D), F32)
    wk = c.din("wk", (D, D), F32)
    wv = c.din("wv", (D, D), F32)
    wo = c.din("wo", (D, D), F32)
    bq = c.din("bq", (P, ND), F32)
    bk = c.din("bk", (P, ND), F32)
    bv = c.din("bv", (P, ND), F32)
    bo = c.din("bo", (D,), F32)
    xout = c.dout("xout", (cfg.S, D), F32)
    zscr = c.dtmp("zscr", (cfg.S, D), F32)
    with c.stack:
        c.setup_common()
        arena = c.sbuf("arena", (P, ND * cfg.S), BF16)
        ctx = c.sbuf("ctx", (P, ND * cfg.S2), BF16)
        pool = [c.sbuf(f"pool{i}", (P, D), F32) for i in range(2)]
        xbT = c.sbuf("xbT", (P, 2 * D), BF16)
        xb = [xbT[:, 0:D], xbT[:, D:2 * D]]
        lb = {"zr": [pool[0], pool[1]],
              "sq": c.sbuf("sqb", (P, D // 2), F32),
              "cols": c.sbuf("lncols", (P, 8), F32)}
        eb = _mk_eb(c, "ep")
        bufs = {
            "wA": [c.sbuf(f"wA{i}", (P, ND * P), BF16)[:, :] for i in range(2)],
            "wK": [c.sbuf(f"wK{i}", (P, ND * P), BF16)[:, :] for i in range(2)],
            "m0": c.sbuf("m0", (P, cfg.S2), F32)[:, :],
            "t1b": c.sbuf("t1b", (P, cfg.S2), F32)[:, :],
            "phq": c.sbuf("phq", (P, cfg.S2), BF16)[:, :],
            "phk": c.sbuf("phk", (P, cfg.S2), BF16)[:, :],
            "vv": c.sbuf("vvb", (P, cfg.S2), BF16)[:, :],
            "kv": c.sbuf("kvb", (P, cfg.S2), BF16)[:, :],
            "kvc": c.sbuf("kvcb", (P, cfg.S2), F32)[:, :],
            "zcol": c.sbuf("zcol", (P, 1), BF16),
            "carr": c.sbuf("carr", (P, ND), F32),
        }
        btile = c.sbuf("btile", (P, 3 * ND), F32)
        bo_b = c.sbuf("bo_b", (P, D), F32)
        c.S.dma("sync", btile[:, 0:ND], bq, sem="d_b")
        c.S.dma("sync", btile[:, ND:2 * ND], bk, sem="d_b")
        c.S.dma("sync", btile[:, 2 * ND:3 * ND], bv, sem="d_b")
        t4 = c.S.dma("sync", bo_b[:], bo.unsqueeze(0).broadcast_to((P, D)), sem="d_b")
        btok = merge(t4)
        nb = c.sbuf("nbtile", (P, 2 * ND), F32)
        nbt = c.S.op("vector", lambda e: e.tensor_scalar(
            nb[:, :], btile[:, 0:2 * ND], -1.0, None, ALU.mult),
            waits=merge(btok))
        bt = {"bq": btile[:, 0:ND], "bk": btile[:, ND:2 * ND],
              "bv": btile[:, 2 * ND:3 * ND],
              "nbq": nb[:, 0:ND], "nbk": nb[:, ND:2 * ND], "nbt": nbt}
        bufs["ztok"] = c.S.op("vector", lambda e: e.memset(bufs["zcol"][:], 0.0))
        ttok = transpose_in(c, xin, arena, btok, "t", xb)
        w_aps = {"wq": wq, "wk": wk, "wv": wv}

        def wp_blk(ct0, ng, e2c):
            return wo[ct0 * P:(ct0 + ng) * P,
                      e2c * cfg.WD:(e2c + 1) * cfg.WD].rearrange(
                "(g p) w -> p g w", p=P)

        def bias_view(e2c):
            return bo_b[:, e2c * cfg.WD:(e2c + 1) * cfg.WD]
        fins = []
        psA_free, psB_free = None, None
        ctx_free = None
        ln_dep0 = ttok
        for H in range(2):
            res = l0_attn_half(c, H, arena, ctx, w_aps, bt, bufs,
                               merge(ttok), psA_free, psB_free, ctx_free)

            def midT(ct, tt):
                return ctx[:, ct * cfg.S2 + tt * P: ct * cfg.S2 + (tt + 1) * P]
            ep = epilogue(c, H, 2, midT, D, wp_blk, bias_view, xin, zscr,
                          res["done"], "ep", "gpsimd",
                          merge(res["psA_free"], res["psB_free"]), eb)
            psA_free = ep["ev"]
            psB_free = ep["ev"]
            ctx_free = ep["mm"]
            fins.append(ln_phase(c, H, 2, zscr, xout, merge(ep["st"], ln_dep0), lb))
        c.S.op("sync", lambda e: None, waits=merge(*fins))
        c.S.emit(c.stack)
    return c.nc


def build_ffn(cfg, NO, with_scales):
    c = Core(cfg)
    D, ND = cfg.D, cfg.ND
    I = NO * P
    xin = c.din("xin", (cfg.S, D), F32)
    wg = c.din("wg", (D, I), BF16)
    wu = c.din("wu", (D, I), BF16)
    wd = c.din("wd", (I, D), BF16)
    if with_scales:
        sg = c.din("sg", (P, NO), F32)
        su = c.din("su", (P, NO), F32)
    xout = c.dout("xout", (cfg.S, D), F32)
    zscr = c.dtmp("zscr", (cfg.S, D), F32)
    with c.stack:
        c.setup_common()
        NP = 4 if cfg.NT >= 8 else 2
        arena = c.sbuf("arena", (P, ND * cfg.S), BF16)
        hT = c.sbuf("hT", (P, NO * (cfg.S // NP)), BF16)
        pool = [c.sbuf(f"pool{i}", (P, D), F32) for i in range(2)]
        xbT = c.sbuf("xbT", (P, 2 * D), BF16)
        xb = [xbT[:, 0:D], xbT[:, D:2 * D]]
        lb = {"zr": [pool[0], pool[1]],
              "sq": c.sbuf("sqb", (P, D // 2), F32),
              "cols": c.sbuf("lncols", (P, 8), F32)}
        eb = _mk_eb(c, "ep")
        if with_scales:
            sgt = c.sbuf("sgt", (P, NO), F32)
            sut = c.sbuf("sut", (P, NO), F32)
            c.S.dma("sync", sgt[:], sg, sem="d_b")
            s2 = c.S.dma("sync", sut[:], su, sem="d_b")
            stok = merge(s2)
            sg_t, su_t = sgt, sut
        else:
            stok = []
            sg_t, su_t = None, None
        ttok = transpose_in(c, xin, arena, stok, "t", xb)
        ftok = ffn_core(c, arena, hT, wg, wu, wd, sg_t, su_t, NO, xin, zscr, xout,
                        merge(ttok, stok), eb, lb)
        c.S.op("sync", lambda e: None, waits=ftok)
        c.S.emit(c.stack)
    return c.nc


def build_l1a(cfg):
    c = Core(cfg)
    D, ND = cfg.D, cfg.ND
    xin = c.din("xin", (cfg.S, D), F32)
    wq = c.din("wq", (D, D), F32)
    wk = c.din("wk", (D, D), F32)
    wv = c.din("wv", (D, D), F32)
    wo = c.din("wo", (D, D), F32)
    bq = c.din("bq", (P, cfg.NH), F32)
    bk = c.din("bk", (P, cfg.NH), F32)
    bv = c.din("bv", (D,), F32)
    bo = c.din("bo", (D,), F32)
    xout = c.dout("xout", (cfg.S, D), F32)
    zscr = c.dtmp("zscr", (cfg.S, D), F32)
    with c.stack:
        c.setup_common(need_ones=True, need_tri=True)
        arena = c.sbuf("arena", (P, ND * cfg.S), BF16)
        oT = c.sbuf("oT", (P, ND * cfg.S), BF16)
        xf = [carve_or_alloc(c, oT, i * 2 * D, (P, D), F32, f"xf{i}")
              for i in range(2)]
        xb = [carve_or_alloc(c, oT, 4 * D + i * D, (P, D), BF16, f"xb{i}")
              for i in range(2)]
        lb = {"zr": [carve_or_alloc(c, arena, i * 2 * D, (P, D), F32, f"zr{i}")
                     for i in range(2)],
              "xo": [carve_or_alloc(c, arena, (4 + 2 * i) * D, (P, D), F32, f"xo{i}")
                     for i in range(2)],
              "sq": carve_or_alloc(c, arena, 8 * D, (P, D), F32, "sqb"),
              "cols": c.sbuf("lncols", (P, 8), F32)}
        eb = {"wb": [carve_or_alloc(c, arena, 10 * D + i * cfg.WD, (P, cfg.WD),
                                    BF16, f"ewb{i}") for i in range(2)],
              "rb": [carve_or_alloc(c, arena, 11 * D + i * 2 * cfg.WD, (P, cfg.WD),
                                    F32, f"erb{i}") for i in range(2)],
              "zb": [carve_or_alloc(c, arena, 13 * D + i * 2 * cfg.WD, (P, cfg.WD),
                                    F32, f"ezb{i}") for i in range(4)]}
        bob = c.sbuf("bob", (P, D), F32)[:, :]
        btile = c.sbuf("btile", (P, 2 * cfg.NH), F32)
        bvb = c.sbuf("bvb", (P, D), F32)
        c.S.dma("sync", btile[:, 0:cfg.NH], bq, sem="d_b")
        c.S.dma("sync", btile[:, cfg.NH:2 * cfg.NH], bk, sem="d_b")
        btok0 = c.S.dma("sync", bvb[:], bv.unsqueeze(0).broadcast_to((P, D)),
                        sem="d_b")
        bt = {"bq": btile[:, 0:cfg.NH], "bk": btile[:, cfg.NH:2 * cfg.NH]}
        ttok = transpose_in(c, xin, arena, merge(btok0), "t", xb)
        res = l1_attn(c, arena, oT, {"wq": wq, "wk": wk, "wv": wv}, bt, bvb, ttok)
        btok1 = c.S.dma("sync", bob, bo.unsqueeze(0).broadcast_to((P, D)),
                        waits=res["done"], sem="d_b")

        def midT(ct, tt_local):
            pass
        def wp_blk(ct0, ng, e2c):
            return wo[ct0 * P:(ct0 + ng) * P,
                      e2c * cfg.WD:(e2c + 1) * cfg.WD].rearrange(
                "(g p) w -> p g w", p=P)

        def bias_view(e2c):
            return bob[:, e2c * cfg.WD:(e2c + 1) * cfg.WD]
        fins = []
        ps_free = merge(res["psA_free"], res["psB_free"])
        for H in range(2):
            NTp = cfg.NT // 2

            def midT(ct, tt, H=H, NTp=NTp):
                t0 = (H * NTp + tt) * P
                return oT[:, ct * cfg.S + t0: ct * cfg.S + t0 + P]
            ep = epilogue(c, H, 2, midT, D, wp_blk, bias_view, xin, zscr,
                          merge(res["done"], btok1), "ep", "gpsimd", ps_free, eb)
            ps_free = ep["ev"]
            fins.append(ln_phase(c, H, 2, zscr, xout, ep["st"], lb))
        c.S.op("sync", lambda e: None, waits=merge(*fins))
        c.S.emit(c.stack)
    return c.nc


def build_gather(cfg, shard_specs):
    c = Core(cfg)
    with c.stack:
        lut_in = c.din("lut", (16,), F32)
        sd_in = c.din("sdown", (cfg.D,), F32)
        luts = c.sbuf("luts", (P, 16), F32)
        sdb = c.sbuf("sdb", (P, cfg.D), F32)
        c.S.dma("sync", luts[:], lut_in.unsqueeze(0).broadcast_to((P, 16)), sem="d_b")
        ctok = c.S.dma("sync", sdb[:], sd_in.unsqueeze(0).broadcast_to((P, cfg.D)),
                       sem="d_b")
        ins, outs = {}, {}
        for spec in shard_specs:
            nm = spec["name"]
            ins[nm] = c.din(nm + "_i", (spec["rows"], spec["cols"]), I32)
            outs[nm] = c.dout(nm + "_o", (spec["rows"], spec["cols"]), BF16)
        CMAX = max(s["cols"] for s in shard_specs)
        idxb = c.sbuf("idxb", (P, CMAX), BF16)
        mask = c.sbuf("maskb", (P, CMAX), I8)
        accb = c.sbuf("accb", (P, CMAX), BF16)
        idx_free = None
        acc_free = None
        last_st = None
        for spec in shard_specs:
            nm, R, Cc, mode = spec["name"], spec["rows"], spec["cols"], spec["mode"]
            r0 = 0
            while r0 < R:
                rr = min(P, R - r0)
                ld = c.S.dma("gpsimd", idxb[0:rr, 0:Cc], ins[nm][r0:r0 + rr, :],
                             waits=merge(idx_free), sem="d_i")
                if mode == "lut":
                    t0 = c.S.op("vector", lambda e, rr=rr, Cc=Cc: e.tensor_scalar(
                        accb[0:rr, 0:Cc], idxb[0:rr, 0:Cc], 0.0, luts[0:rr, 0:1],
                        ALU.mult, ALU.add),
                        waits=merge([ld], ctok, acc_free))
                    last = t0
                    for j in range(1, 16):
                        m1 = c.S.op("vector", lambda e, rr=rr, Cc=Cc, j=j:
                                    e.tensor_scalar(
                                        mask[0:rr, 0:Cc], idxb[0:rr, 0:Cc],
                                        -(j - 1.0), 0.0, ALU.add, ALU.max),
                                    waits=merge([ld], [last]))
                        last = c.S.op("vector", lambda e, rr=rr, Cc=Cc, j=j:
                                      e.copy_predicated(
                                          accb[0:rr, 0:Cc], mask[0:rr, 0:Cc],
                                          luts[0:rr, j:j + 1].broadcast_to((rr, Cc))),
                                      waits=[m1])
                elif mode == "deq":
                    last = c.S.op("vector", lambda e, rr=rr, Cc=Cc: e.tensor_scalar(
                        accb[0:rr, 0:Cc], idxb[0:rr, 0:Cc], -128.0, None, ALU.add),
                        waits=merge([ld], acc_free))
                else:
                    t0 = c.S.op("vector", lambda e, rr=rr, Cc=Cc: e.tensor_scalar(
                        accb[0:rr, 0:Cc], idxb[0:rr, 0:Cc], -128.0, None, ALU.add),
                        waits=merge([ld], acc_free))
                    last = c.S.op("vector", lambda e, rr=rr, Cc=Cc: e.tensor_tensor(
                        accb[0:rr, 0:Cc], accb[0:rr, 0:Cc], sdb[0:rr, 0:Cc], ALU.mult),
                        waits=merge([t0], ctok))
                idx_free = last
                st = c.S.dma("sync", outs[nm][r0:r0 + rr, :], accb[0:rr, 0:Cc],
                             waits=[last], sem="d_z")
                acc_free = st
                last_st = st
                r0 += rr
        c.S.op("sync", lambda e: None, waits=merge(last_st))
        c.S.emit(c.stack)
    return c.nc




def queue_gather(c, groups, luts_bf, dls, l0col, sdb, gbuf, deps):
    S = c.S
    NSC = 10
    st = {"par": 0, "st": [None, None], "sc": [None, None], "sg": [None, None]}
    out = {}

    def tile_thunk(in_ap, out_ap, r0, rr, c0, c1, mode, toks, final):
        par = st["par"]
        st["par"] ^= 1
        ib = gbuf["ib"][par]
        acc = gbuf["acc"][par]
        sb = gbuf["s"]
        w = c1 - c0
        ld = S.dma("gpsimd", ib[0:rr, 0:w], in_ap[r0:r0 + rr, c0:c1],
                   waits=merge(st["sc"][par], st["st"][par], deps),
                   sem=f"d_gi{par}")
        if mode == "lut":
            last = S.op("vector", lambda e: e.tensor_scalar(
                acc[0:rr, 0:w], ib[0:rr, 0:w], 0.0, l0col[0:rr, 0:1],
                ALU.mult, ALU.add),
                waits=merge([ld], st["st"][0], st["st"][1], st["sc"][par]))
            for j in range(1, 16):
                sl = j % 2
                m1 = S.op("vector", lambda e, j=j, sl=sl: e.tensor_scalar(
                    sb[sl][0:rr, 0:w], ib[0:rr, 0:w], j - 0.5,
                    dls[0:rr, j:j + 1], ALU.is_ge, ALU.mult),
                    waits=merge([ld], [last], st["sg"][sl]))
                last = S.op("vector", lambda e, sl=sl: e.tensor_tensor(
                    acc[0:rr, 0:w], acc[0:rr, 0:w], sb[sl][0:rr, 0:w], ALU.add),
                    waits=merge([m1], [last]))
                st["sg"][sl] = last
            st["sc"][par] = last
        elif mode == "deq":
            last = S.op("vector", lambda e: e.tensor_scalar(
                acc[0:rr, 0:w], ib[0:rr, 0:w], -128.0, None, ALU.add),
                waits=merge([ld], st["st"][0], st["st"][1], st["sc"][par]))
            st["sc"][par] = last
        else:
            t0 = S.op("vector", lambda e: e.tensor_scalar(
                acc[0:rr, 0:w], ib[0:rr, 0:w], -128.0, None, ALU.add),
                waits=merge([ld], st["st"][0], st["st"][1], st["sc"][par]))
            last = S.op("vector", lambda e, c0=c0, c1=c1: e.tensor_tensor(
                acc[0:rr, 0:w], acc[0:rr, 0:w], sdb[0:rr, c0:c1], ALU.mult),
                waits=merge([t0], deps))
            st["sc"][par] = last
        stk = S.dma("sync", out_ap[r0:r0 + rr, c0:c1], acc[0:rr, 0:w],
                    waits=[last], sem=f"d_gs{par}")
        st["st"][par] = stk
        if final:
            toks.append((c1, merge(stk, S.tokv(f"d_gs{par ^ 1}"))))

    CW = 1024
    for specs in groups:
        for nm, _, _, _, _, _ in specs:
            out[nm] = []
        nch = max((spec[4] + CW - 1) // CW for spec in specs)
        for ch in range(nch):
            for (nm, in_ap, out_ap, R, Cc, mode) in specs:
                c0, c1 = ch * CW, min((ch + 1) * CW, Cc)
                if c0 >= Cc:
                    continue
                toks = out[nm]
                nrt = (R + P - 1) // P
                for rt in range(nrt):
                    r0 = rt * P
                    rr = min(P, R - r0)
                    c.thunks.append(
                        lambda in_ap=in_ap, out_ap=out_ap, r0=r0, rr=rr,
                        c0=c0, c1=c1, mode=mode, toks=toks,
                        final=(rt == nrt - 1):
                        tile_thunk(in_ap, out_ap, r0, rr, c0, c1, mode, toks,
                                   final))
    return out


def ready(c, toks, limit):
    while not toks or toks[-1][0] < limit:
        if not c.thunks:
            raise RuntimeError(f"gather thunks exhausted before limit {limit}")
        c.pump(1)
    got = []
    for (lim, t) in toks:
        got.append(t)
        if lim >= limit:
            break
    return merge(*got)


def build_fused(cfg):
    c = Core(cfg)
    D, ND, NH = cfg.D, cfg.ND, cfg.NH
    I0, I1 = cfg.I0, cfg.I1
    SQ = cfg.S
    xin = c.din("xin", (SQ, D), F32)
    w0 = {k: c.din("w0" + k, (D, D), F32) for k in ("q", "k", "v", "o")}
    b0q = c.din("b0q", (P, ND), F32)
    b0k = c.din("b0k", (P, ND), F32)
    b0v = c.din("b0v", (P, ND), F32)
    b0o = c.din("b0o", (D,), F32)
    lut_in = c.din("lut", (16,), F32)
    g0g_i = c.din("g0g_i", (D, I0), I32)
    g0u_i = c.din("g0u_i", (D, I0), I32)
    g0d_i = c.din("g0d_i", (I0, D), I32)
    w1 = {k: c.din("w1" + k, (D, D), F32) for k in ("q", "k", "v", "o")}
    b1q = c.din("b1q", (P, NH), F32)
    b1k = c.din("b1k", (P, NH), F32)
    b1v = c.din("b1v", (D,), F32)
    b1o = c.din("b1o", (D,), F32)
    g1g_i = c.din("g1g_i", (D, I1), I32)
    g1u_i = c.din("g1u_i", (D, I1), I32)
    g1d_i = c.din("g1d_i", (I1, D), I32)
    sg_in = c.din("sg", (P, cfg.NO1), F32)
    su_in = c.din("su", (P, cfg.NO1), F32)
    sd_in = c.din("sdown", (D,), F32)
    xout = c.dout("xout", (SQ, D), F32)
    x1 = c.dtmp("x1", (SQ, D), F32)
    x2 = c.dtmp("x2", (SQ, D), F32)
    x3 = c.dtmp("x3", (SQ, D), F32)
    zscr = c.dtmp("zscr", (SQ, D), F32)
    wg0 = c.dtmp("wg0", (D, I0), BF16)
    wu0 = c.dtmp("wu0", (D, I0), BF16)
    wd0 = c.dtmp("wd0", (I0, D), BF16)
    wg1 = c.dtmp("wg1", (D, I1), BF16)
    wu1 = c.dtmp("wu1", (D, I1), BF16)
    wd1 = c.dtmp("wd1", (I1, D), BF16)

    with c.stack:
        c.setup_common(need_ones=True, need_tri=True)
        A = c.sbuf("A", (P, ND * SQ), BF16)
        M = c.sbuf("M", (P, max(ND * SQ, 31232)), BF16)
        U = c.sbuf("U", (P, 32192), BF16)
        zr = [carve(U[:, :], i * 2 * D, 2 * D, (P, D), F32) for i in range(2)]
        sq_b = carve(U[:, :], 4 * D, D, (P, D // 2), F32)
        xb = [U[:, 5 * D:6 * D], U[:, 5 * D:6 * D]]
        lb = {"zr": zr, "sq": sq_b, "cols": c.sbuf("lncols", (P, 8), F32)}
        EB0 = 15808
        eb = {"wb": [U[:, EB0 + i * 2048:EB0 + i * 2048 + 4 * cfg.WD]
                     for i in range(2)],
              "rb": [carve(U[:, :], EB0 + 4096 + i * 1024, 1024, (P, cfg.WD), F32)
                     for i in range(2)],
              "zb": [carve(U[:, :], EB0 + 6144 + i * 1024, 1024, (P, cfg.WD), F32)
                     for i in range(2)]}
        BIAS0 = EB0 + 8192
        bias_t = carve(U[:, :], BIAS0, 2 * D, (P, D), F32)
        G0 = BIAS0 + 2 * D
        gbuf = {
            "ib": [U[:, G0 + i * 1024:G0 + (i + 1) * 1024] for i in range(2)],
            "acc": [U[:, G0 + 2048:G0 + 3072], U[:, G0 + 2048:G0 + 3072]],
            "s": [U[:, G0 + 3072:G0 + 4096], U[:, G0 + 3072:G0 + 4096]],
        }
        HPG_ = min(cfg.NH, 2)
        VW_ = HPG_ * cfg.HD
        l1b = {
            "vg": U[:, 0:0 + cfg.NT * VW_],
            "wvb": U[:, 4096:4096 + ND * VW_],
            "qh": U[:, 8192:8192 + SQ],
            "kh": U[:, 10240:10240 + SQ],
            "wqb": [U[:, 12288 + i * 2048:12288 + i * 2048 + ND * P]
                    for i in range(2)],
            "wkb": [U[:, 16384 + i * 2048:16384 + i * 2048 + ND * P]
                    for i in range(2)],
            "pT": [U[:, 20480 + i * 512:20480 + i * 512 + cfg.W] for i in range(2)],
            "rr": carve(U[:, :], 21504, 1024, (P, cfg.W), F32),
        }
        ctx = M[:, 0:ND * cfg.S2]
        SC0 = ND * cfg.S2
        l0bufs = {
            "wA": [M[:, SC0 + i * 2048:SC0 + i * 2048 + ND * P] for i in range(2)],
            "wK": [M[:, SC0 + 4096 + i * 2048:SC0 + 4096 + i * 2048 + ND * P]
                   for i in range(2)],
            "m0": carve(U[:, :], 12288, 2048, (P, cfg.S2), F32),
            "t1b": carve(M[:, :], SC0 + 8192, 2048, (P, cfg.S2), F32),
            "phq": M[:, SC0 + 10240:SC0 + 10240 + cfg.S2],
            "phk": M[:, SC0 + 11264:SC0 + 11264 + cfg.S2],
            "vv": M[:, SC0 + 12288:SC0 + 12288 + cfg.S2],
            "kv": M[:, SC0 + 13312:SC0 + 13312 + cfg.S2],
            "kvc": carve(M[:, :], SC0 + 14336, 2048, (P, cfg.S2), F32),
            "zcol": c.sbuf("zcol", (P, 1), BF16),
            "carr": c.sbuf("carr", (P, ND), F32),
        }
        FW0 = 22528
        ffnb = {
            "gw": [M[:, FW0 + i * 2048:FW0 + i * 2048 + ND * P] for i in range(2)],
            "uw": [M[:, FW0 + 4096 + i * 2048:FW0 + 4096 + i * 2048 + ND * P]
                   for i in range(2)],
            "sgb": M[:, FW0 + 8192:FW0 + 8192 + min(512, SQ // 2)],
        }
        btile = c.sbuf("btile", (P, 3 * ND + 2 * NH), F32)
        sgt = c.sbuf("sgt", (P, cfg.NO1), F32)
        sut = c.sbuf("sut", (P, cfg.NO1), F32)
        sdb = c.sbuf("sdb", (P, D), F32)
        luts = c.sbuf("luts", (P, 16), F32)
        c.S.dma("sync", btile[:, 0:ND], b0q, sem="d_b")
        c.S.dma("sync", btile[:, ND:2 * ND], b0k, sem="d_b")
        c.S.dma("sync", btile[:, 2 * ND:3 * ND], b0v, sem="d_b")
        c.S.dma("sync", btile[:, 3 * ND:3 * ND + NH], b1q, sem="d_b")
        c.S.dma("sync", btile[:, 3 * ND + NH:3 * ND + 2 * NH], b1k, sem="d_b")
        c.S.dma("sync", sgt[:], sg_in, sem="d_b")
        c.S.dma("sync", sut[:], su_in, sem="d_b")
        c.S.dma("sync", sdb[:], sd_in.unsqueeze(0).broadcast_to((P, D)), sem="d_b")
        btok = c.S.dma("sync", luts[:], lut_in.unsqueeze(0).broadcast_to((P, 16)),
                       sem="d_b")
        nb = c.sbuf("nbtile", (P, 2 * ND), F32)
        nbt = c.S.op("vector", lambda e: e.tensor_scalar(
            nb[:, :], btile[:, 0:2 * ND], -1.0, None, ALU.mult),
            waits=merge(btok))
        bt0 = {"bq": btile[:, 0:ND], "bk": btile[:, ND:2 * ND],
               "bv": btile[:, 2 * ND:3 * ND],
               "nbq": nb[:, 0:ND], "nbk": nb[:, ND:2 * ND], "nbt": nbt}
        bt1 = {"bq": btile[:, 3 * ND:3 * ND + NH],
               "bk": btile[:, 3 * ND + NH:3 * ND + 2 * NH]}
        l0bufs["ztok"] = c.S.op("vector", lambda e: e.memset(l0bufs["zcol"][:], 0.0))

        dls = c.sbuf("dls", (P, 16), F32)
        luts_bf = c.sbuf("lutsbf", (P, 16), BF16)
        l0col = c.sbuf("l0col", (P, 1), F32)
        d0 = c.S.op("vector", lambda e: e.tensor_copy(luts_bf[:, :], luts[:, :]),
                    waits=merge(btok))
        d1 = c.S.op("vector", lambda e: e.tensor_tensor(
            dls[:, 1:16], luts_bf[:, 1:16], luts_bf[:, 0:15], ALU.subtract),
            waits=[d0])
        d2 = c.S.op("vector", lambda e: e.tensor_copy(l0col[:, :],
                                                      luts_bf[:, 0:1]),
                    waits=[d0])
        gdep = merge(btok, [d1], [d2])
        groups = [
            [("wg0", g0g_i, wg0, D, I0, "lut"), ("wu0", g0u_i, wu0, D, I0, "lut")],
            [("wd0", g0d_i, wd0, I0, D, "lut")],
            [("wg1", g1g_i, wg1, D, I1, "deq"), ("wu1", g1u_i, wu1, D, I1, "deq")],
            [("wd1", g1d_i, wd1, I1, D, "deqscale")],
        ]
        gtk = queue_gather(c, groups, luts_bf, dls, l0col, sdb, gbuf, gdep)

        def bias_view(e2c):
            return bias_t[:, e2c * cfg.WD:(e2c + 1) * cfg.WD]

        ttok = transpose_in(c, xin, A, [], "t", xb)
        bias_tok = c.S.dma("sync", bias_t, b0o.unsqueeze(0).broadcast_to((P, D)),
                           waits=merge(ttok), sem="d_bias")
        psA_free, psB_free, ctx_free = None, None, None
        fins = []
        for H in range(2):
            res = l0_attn_half(c, H, A, ctx, {"wq": w0["q"], "wk": w0["k"],
                                              "wv": w0["v"]}, bt0, l0bufs,
                               merge(ttok, btok, bt0["nbt"]), psA_free, psB_free,
                               ctx_free)

            def midT(ct, tt):
                return ctx[:, ct * cfg.S2 + tt * P: ct * cfg.S2 + (tt + 1) * P]

            def wp_blk(ct0, ng, e2c, w=w0["o"]):
                return w[ct0 * P:(ct0 + ng) * P,
                         e2c * cfg.WD:(e2c + 1) * cfg.WD].rearrange(
                    "(g p) w -> p g w", p=P)
            ep = epilogue(c, H, 2, midT, D, wp_blk, bias_view, xin, zscr,
                          merge(res["done"], [bias_tok]), "ep", "gpsimd",
                          merge(res["psA_free"], res["psB_free"]), eb)
            psA_free = ep["ev"]
            psB_free = ep["ev"]
            ctx_free = ep["mm"]
            fins.append(ln_phase(c, H, 2, zscr, x1, ep["st"], lb))
        chain = merge(*fins)

        ttok = transpose_in(c, x1, A, dict(lb["rowtoks"]), "t", xb,
                            ps_gate=merge(psA_free, psB_free))
        ftok = ffn_core_f(c, A, M, wg0, wu0, wd0, None, None, cfg.NO0,
                          x1, zscr, x2, merge(ttok),
                          eb, lb, ffnb, psA_free, psB_free,
                          gtk={"g": gtk["wg0"], "u": gtk["wu0"], "d": gtk["wd0"]})
        chain = ftok["done"]
        psA_free, psB_free = ftok["psA_free"], ftok["psB_free"]

        ttok = transpose_in(c, x2, A, dict(lb["rowtoks"]), "t", xb,
                            ps_gate=merge(ftok["psA_free"], ftok["psB_free"]))
        bvb_tok = c.S.dma("sync", bias_t, b1v.unsqueeze(0).broadcast_to((P, D)),
                          waits=merge(chain, [bias_tok]), sem="d_bias")
        res = l1_attn_f(c, A, M, {"wq": w1["q"], "wk": w1["k"], "wv": w1["v"]},
                        bt1, bias_t, merge(ttok, [bvb_tok]), l1b,
                        psA_free, psB_free)
        bias_tok = c.S.dma("sync", bias_t, b1o.unsqueeze(0).broadcast_to((P, D)),
                           waits=res["done"], sem="d_bias")
        fins = []
        ps_free = merge(res["psA_free"], res["psB_free"])
        for H in range(2):
            NTp = cfg.NT // 2

            def midT(ct, tt, H=H, NTp=NTp):
                t0 = (H * NTp + tt) * P
                return M[:, ct * SQ + t0: ct * SQ + t0 + P]

            def wp_blk(ct0, ng, e2c, w=w1["o"]):
                return w[ct0 * P:(ct0 + ng) * P,
                         e2c * cfg.WD:(e2c + 1) * cfg.WD].rearrange(
                    "(g p) w -> p g w", p=P)
            ep = epilogue(c, H, 2, midT, D, wp_blk, bias_view, x2, zscr,
                          merge(res["done"], [bias_tok]), "ep", "gpsimd",
                          ps_free, eb)
            ps_free = ep["ev"]
            fins.append(ln_phase(c, H, 2, zscr, x3, ep["st"], lb))
        chain = merge(*fins)
        psA_free = ps_free
        psB_free = ps_free

        ttok = transpose_in(c, x3, A, dict(lb["rowtoks"]), "t", xb,
                            ps_gate=ps_free)
        ftok = ffn_core_f(c, A, M, wg1, wu1, wd1, sgt, sut, cfg.NO1,
                          x3, zscr, xout, merge(ttok),
                          eb, lb, ffnb, psA_free, psB_free,
                          gtk={"g": gtk["wg1"], "u": gtk["wu1"], "d": gtk["wd1"]})
        c.S.op("sync", lambda e: None, waits=ftok["done"])
        c.S.emit(c.stack)
    return c.nc


def ffn_core_f(c, arena, M, wg_hbm, wu_hbm, wd_hbm, sg_t, su_t, NO,
               resid_hbm, z_hbm, out_hbm, deps, eb, lb, ffnb,
               psA_free0, psB_free0, gtk=None):
    cfg, S = c.cfg, c.S
    D, ND = cfg.D, cfg.ND
    SQ = cfg.S
    NP = 4 if cfg.NT >= 8 else 2
    SP_ = SQ // NP
    WH = min(cfg.W, SP_)
    nW2 = SP_ // WH
    hT = M[:, 0:NO * SP_]
    gw, uw, sgb = ffnb["gw"], ffnb["uw"], ffnb["sgb"]
    gtok = ffnb.setdefault("gtok", [None, None])
    utok = ffnb.setdefault("utok", [None, None])
    out_toks = []
    ep_prev = None
    hT_readers = ffnb.get("hT_readers")
    psg_free0 = psA_free0
    psu_free0 = psB_free0
    for PI in range(NP):
        psg_free = ep_prev["ev"] if ep_prev else psg_free0
        psu_free = ep_prev["ev"] if ep_prev else psu_free0
        sg_free = ffnb.get("sg_free")
        h_toks = []
        for ot in range(NO):
            b = ot % 2
            srcg = wg_hbm.rearrange("(nj p) o -> p nj o", p=P)[
                :, :, ot * P:(ot + 1) * P]
            srcu = wu_hbm.rearrange("(nj p) o -> p nj o", p=P)[
                :, :, ot * P:(ot + 1) * P]
            gw_rdy = ready(c, gtk["g"], (ot + 1) * P) if gtk else []
            gu_rdy = ready(c, gtk["u"], (ot + 1) * P) if gtk else []
            lg = S.dma("gpsimd", gw[b].rearrange("p (nj o) -> p nj o", nj=ND),
                       srcg, waits=merge(deps, gtok[b], gw_rdy), sem=f"d_gg{b}")
            lu = S.dma("sync", uw[b].rearrange("p (nj o) -> p nj o", nj=ND),
                       srcu, waits=merge(deps, utok[b], gu_rdy), sem=f"d_su{b}")
            c.pump(3)

            def mk(wl, ps, b, PI=PI):
                def em(e):
                    inst = None
                    for j in range(ND):
                        for cc in range(nW2):
                            c0 = PI * SP_ + cc * WH
                            inst = e.matmul(ps[:, cc * WH:(cc + 1) * WH],
                                            wl[b][:, j * P:(j + 1) * P],
                                            arena[:, j * SQ + c0: j * SQ + c0 + WH],
                                            start=(j == 0), stop=(j == ND - 1))
                    return inst
                return em
            mmg = S.op("tensor", mk(gw, c.psA, b), waits=merge([lg], deps, psg_free))
            gtok[b] = mmg
            mmu = S.op("tensor", mk(uw, c.psB, b), waits=merge([lu], deps, psu_free))
            utok[b] = mmu
            sg_sc = sg_t[:, ot:ot + 1] if sg_t is not None else 1.0
            s1 = S.op("scalar", lambda e, sg_sc=sg_sc: e.activation(
                sgb[:, 0:SP_], c.psA[:, 0:SP_], AF.Silu, bias=0.0, scale=sg_sc),
                waits=merge([mmg], sg_free))
            psg_free = s1
            dsth = hT[:, ot * SP_:(ot + 1) * SP_]
            hw = merge([mmu], [s1], hT_readers)
            if su_t is None:
                v1 = S.op("vector", lambda e, dsth=dsth: e.tensor_tensor(
                    dsth, c.psB[:, 0:SP_], sgb[:, 0:SP_], ALU.mult), waits=hw)
            else:
                su_sc = su_t[:, ot:ot + 1]
                v1 = S.op("vector", lambda e, dsth=dsth, su_sc=su_sc:
                          e.scalar_tensor_tensor(dsth, c.psB[:, 0:SP_], su_sc,
                                                 sgb[:, 0:SP_], ALU.mult, ALU.mult),
                          waits=hw)
            psu_free = v1
            sg_free = v1
            h_toks.append(v1)

        def midT(ct, tt):
            return hT[:, ct * SP_ + tt * P: ct * SP_ + (tt + 1) * P]

        def wp_blk(ct0, ng, e2c):
            return wd_hbm[ct0 * P:(ct0 + ng) * P,
                          e2c * cfg.WD:(e2c + 1) * cfg.WD].rearrange(
                "(g p) w -> p g w", p=P)
        ep = epilogue(c, PI, NP, midT, NO * P, wp_blk, None, resid_hbm, z_hbm,
                      merge(deps, *h_toks), "fe", "sync",
                      ps_free=merge(*h_toks, psg_free, psu_free), eb=eb,
                      wready=(lambda e2c: ready(c, gtk["d"], (e2c + 1) * cfg.WD))
                      if gtk else None)
        hT_readers = ep["mm"]
        ffnb["hT_readers"] = hT_readers
        ffnb["sg_free"] = sg_free
        ep_prev = ep
        out_toks.append(ln_phase(c, PI, NP, z_hbm, out_hbm, ep["st"], lb))
    return dict(done=merge(*out_toks), psA_free=ep_prev["ev"],
                psB_free=ep_prev["ev"])


def l1_attn_f(c, arena, oT, w_aps, bt, bv_bcast, deps, l1b,
              psA_free0, psB_free0):
    cfg, S = c.cfg, c.S
    D, ND, NT, NH = cfg.D, cfg.ND, cfg.NT, cfg.NH
    SQ = cfg.S
    HPG = min(cfg.NH, 2)
    VW = HPG * cfg.HD
    nW = SQ // cfg.W
    NQ = SQ // cfg.W
    perA = c.psA.shape[1] // cfg.W
    vg, wvb, qh, kh = l1b["vg"], l1b["wvb"], l1b["qh"], l1b["kh"]
    wqb, wkb, pT, rr = l1b["wqb"], l1b["wkb"], l1b["pT"], l1b["rr"]
    wq_tok = [None, None]
    wk_tok = [None, None]
    wv_tok = None
    psA_free = psA_free0
    psB_free = psB_free0
    sslot_free = [psB_free0, psB_free0]
    rslot_free = [psB_free0, psB_free0]
    pT_free = [None, None]
    rr_free = None
    qh_free = None
    kh_free = None
    o_toks = []
    for g in range(NH // HPG):
        srcv = w_aps["wv"].rearrange("(nj p) d -> p nj d", p=P)[
            :, :, g * VW:(g + 1) * VW]
        lv = S.dma("gpsimd", wvb.rearrange("p (nj d) -> p nj d", nj=ND), srcv,
                   waits=merge(deps, wv_tok), sem="d_wv0")
        vg_evs = []
        for tt in range(NT):
            ps = c.ps512(tt % perA)

            def em(e, tt=tt, ps=ps):
                inst = None
                for j in range(ND):
                    inst = e.matmul(ps[:, 0:VW],
                                    arena[:, j * SQ + tt * P: j * SQ + (tt + 1) * P],
                                    wvb[:, j * VW:(j + 1) * VW],
                                    start=(j == 0), stop=(j == ND - 1))
                return inst
            mmv = S.op("tensor", em, waits=merge([lv], deps,
                                                 psA_free if tt < perA else None,
                                                 vg_evs[tt - perA] if tt >= perA else None))
            ev = S.op("vector", lambda e, tt=tt, ps=ps, g=g: e.tensor_tensor(
                vg[:, tt * VW:(tt + 1) * VW], ps[:, 0:VW],
                bv_bcast[:, g * VW:(g + 1) * VW], ALU.add),
                waits=merge([mmv], wv_tok, deps))
            vg_evs.append(ev)
        wv_tok = merge(*vg_evs)
        psA_free = wv_tok
        head_evs = []
        for hh in range(HPG):
            h = g * HPG + hh
            b = h % 2
            srcq = w_aps["wq"].rearrange("(nj p) d -> p nj d", p=P)[
                :, :, h * P:(h + 1) * P]
            srck = w_aps["wk"].rearrange("(nj p) d -> p nj d", p=P)[
                :, :, h * P:(h + 1) * P]
            lq = S.dma("gpsimd", wqb[b].rearrange("p (nj d) -> p nj d", nj=ND),
                       srcq, waits=merge(deps, wq_tok[b]), sem=f"d_wq{b}")
            lk = S.dma("gpsimd", wkb[b].rearrange("p (nj d) -> p nj d", nj=ND),
                       srck, waits=merge(deps, wk_tok[b]), sem=f"d_wk{b}")

            def mkqk(wl, ps, b):
                def em(e):
                    inst = None
                    for j in range(ND):
                        for cc in range(nW):
                            inst = e.matmul(ps[:, cc * cfg.W:(cc + 1) * cfg.W],
                                            wl[b][:, j * P:(j + 1) * P],
                                            arena[:, j * SQ + cc * cfg.W:
                                                  j * SQ + (cc + 1) * cfg.W],
                                            start=(j == 0), stop=(j == ND - 1))
                    return inst
                return em
            mmq = S.op("tensor", mkqk(wqb, c.psA, b),
                       waits=merge([lq], deps, psA_free, qh_free))
            wq_tok[b] = mmq
            sq = S.op("scalar", lambda e, h=h: e.activation(
                qh[:, 0:SQ], c.psA[:, 0:SQ], AF.Identity, bias=bt["bq"][:, h:h + 1],
                scale=1.0),
                waits=merge([mmq], qh_free))
            mmk = S.op("tensor", mkqk(wkb, c.psB, b),
                       waits=merge([lk], deps, psB_free, kh_free,
                                   sslot_free[0], sslot_free[1],
                                   rslot_free[0], rslot_free[1]))
            wk_tok[b] = mmk
            sk = S.op("scalar", lambda e, h=h: e.activation(
                kh[:, 0:SQ], c.psB[:, 0:SQ], AF.Identity, bias=bt["bk"][:, h:h + 1],
                scale=1.0),
                waits=merge([mmk], kh_free))
            psB_free = sk
            evs = []
            for qg in range(NQ):
                c.pump(2)
                q0 = qg * cfg.W
                jmax = min(NT - 1, (q0 + cfg.W - 1) // P)
                rs_i = 2 + (qg % 2)
                rs = c.psB[:, rs_i * cfg.W:(rs_i + 1) * cfg.W]
                last_p = None
                for j in range(jmax + 1):
                    sl = j % 2
                    sT = c.psB[:, sl * cfg.W:(sl + 1) * cfg.W]
                    mm1 = S.op("tensor", lambda e, j=j, q0=q0, sT=sT: e.matmul(
                        sT[:, 0:cfg.W], kh[:, j * P:(j + 1) * P], qh[:, q0:q0 + cfg.W],
                        start=True, stop=True),
                        waits=merge([sq], [sk], sslot_free[sl]))
                    pb = pT[sl]
                    se = S.op("scalar", lambda e, pb=pb, sT=sT: e.activation(
                        pb[:, 0:cfg.W], sT[:, 0:cfg.W], AF.Exp, bias=0.0,
                        scale=cfg.SCALE),
                        waits=merge([mm1], pT_free[sl]))
                    sslot_free[sl] = se
                    off = j * P - q0
                    mtoks = [se]
                    if off > 0:
                        z1 = S.op("vector", lambda e, pb=pb, off=off:
                                  e.memset(pb[:, 0:off], 0.0), waits=[se])
                        mtoks.append(z1)
                    if 0 <= off <= cfg.W - P:
                        z2 = S.op("vector", lambda e, pb=pb, off=off: e.tensor_tensor(
                            pb[:, off:off + P], pb[:, off:off + P], c.tri[:], ALU.mult),
                            waits=merge([se], c.const_tok))
                        mtoks.append(z2)
                    ptok = merge(*mtoks)
                    mm2 = S.op("tensor", lambda e, pb=pb, rs=rs, j=j, jmax=jmax:
                               e.matmul(rs[:, 0:cfg.W], c.ones[:], pb[:, 0:cfg.W],
                                        start=(j == 0), stop=(j == jmax)),
                               waits=merge(ptok, c.const_tok,
                                           rslot_free[rs_i - 2] if j == 0 else None))
                    vsl = vg[:, j * VW + hh * cfg.HD: j * VW + hh * cfg.HD + P]
                    mm3 = S.op("tensor", lambda e, vsl=vsl, pb=pb, q0=q0, j=j, jmax=jmax:
                               e.matmul(c.psA[:, q0:q0 + cfg.W], vsl, pb[:, 0:cfg.W],
                                        start=(j == 0), stop=(j == jmax)),
                               waits=merge(ptok, wv_tok, [sq] if j == 0 else None))
                    last_p = merge([mm2], [mm3])
                    pT_free[sl] = last_p
                v1 = S.op("vector", lambda e, rs=rs: e.reciprocal(rr, rs[:, 0:cfg.W]),
                          waits=merge(last_p, rr_free))
                rslot_free[rs_i - 2] = v1
                dst = oT[:, h * SQ + q0: h * SQ + q0 + cfg.W]
                ev = S.op("vector", lambda e, dst=dst, q0=q0:
                          e.scalar_tensor_tensor(dst, c.psA[:, q0:q0 + cfg.W], 1.0,
                                                 rr, ALU.mult, ALU.mult),
                          waits=merge([v1], last_p))
                rr_free = ev
                evs.append(ev)
            psA_free = merge(*evs)
            qh_free = psA_free
            kh_free = psA_free
            head_evs.extend(evs)
            o_toks.extend(evs)
        wv_tok = merge(wv_tok, *head_evs)
    return dict(done=merge(*o_toks), psA_free=psA_free, psB_free=merge(
        sslot_free[0], sslot_free[1], rslot_free[0], rslot_free[1], psB_free))



def _colmajor(v, nd):
    return np.ascontiguousarray(np.asarray(v, np.float32).reshape(nd, P).T)


def kernel(L0_attn_x, L0_attn_Wq, L0_attn_bq, L0_attn_Wk, L0_attn_bk,
           L0_attn_Wv, L0_attn_bv, L0_attn_Wo, L0_attn_bo,
           L0_ffn_lut, L0_ffn_gate_idx, L0_ffn_up_idx, L0_ffn_down_idx,
           L1_attn_wq, L1_attn_bq, L1_attn_wk, L1_attn_bk, L1_attn_wv,
           L1_attn_bv, L1_attn_wo, L1_attn_bo,
           L1_ffn_wq_gate, L1_ffn_s_gate, L1_ffn_wq_up, L1_ffn_s_up,
           L1_ffn_wq_down, L1_ffn_s_down):
    cfg = FULL
    B = NCORES
    ident = np.eye(P, dtype=np.float32)
    ones = np.ones((P, P), dtype=np.float32)
    tri = np.triu(np.ones((P, P), dtype=np.float32))
    cv = [1e-5, 1.0 / cfg.D, -0.5, cfg.SCALE, 128.0]
    for j in range(1, 11):
        cv.append(-128.0 * (j - 0.5))
    cvals = np.array(cv, dtype=np.float32)

    def T(a):
        return np.ascontiguousarray(np.asarray(a).T)

    x = np.ascontiguousarray(np.asarray(L0_attn_x, np.float32))
    nc = build_fused(cfg)
    common = {
        "w0q": T(L0_attn_Wq), "w0k": T(L0_attn_Wk), "w0v": T(L0_attn_Wv),
        "w0o": T(L0_attn_Wo),
        "b0q": _colmajor(L0_attn_bq, cfg.ND), "b0k": _colmajor(L0_attn_bk, cfg.ND),
        "b0v": _colmajor(L0_attn_bv, cfg.ND),
        "b0o": np.asarray(L0_attn_bo, np.float32),
        "lut": np.asarray(L0_ffn_lut, np.float32),
        "g0g_i": np.ascontiguousarray(T(L0_ffn_gate_idx), dtype=np.int32),
        "g0u_i": np.ascontiguousarray(T(L0_ffn_up_idx), dtype=np.int32),
        "g0d_i": np.ascontiguousarray(T(L0_ffn_down_idx), dtype=np.int32),
        "w1q": T(L1_attn_wq), "w1k": T(L1_attn_wk), "w1v": T(L1_attn_wv),
        "w1o": T(L1_attn_wo),
        "b1q": _colmajor(L1_attn_bq, cfg.NH), "b1k": _colmajor(L1_attn_bk, cfg.NH),
        "b1v": np.asarray(L1_attn_bv, np.float32),
        "b1o": np.asarray(L1_attn_bo, np.float32),
        "g1g_i": np.ascontiguousarray(T(L1_ffn_wq_gate), dtype=np.int32),
        "g1u_i": np.ascontiguousarray(T(L1_ffn_wq_up), dtype=np.int32),
        "g1d_i": np.ascontiguousarray(T(L1_ffn_wq_down), dtype=np.int32),
        "sg": _colmajor(L1_ffn_s_gate, cfg.NO1),
        "su": _colmajor(L1_ffn_s_up, cfg.NO1),
        "sdown": np.asarray(L1_ffn_s_down, np.float32),
        "cident": ident, "cones": ones, "ctri": tri, "cvals": cvals,
    }
    in_maps = [dict(common, xin=np.ascontiguousarray(x[core]))
               for core in range(B)]
    res = run_bass_kernel_spmd(nc, in_maps, core_ids=list(range(B)))
    out = np.stack([res.results[i]["xout"] for i in range(B)], axis=0)
    return np.ascontiguousarray(out.astype(np.float32))


def kernel_unfused(L0_attn_x, L0_attn_Wq, L0_attn_bq, L0_attn_Wk, L0_attn_bk,
           L0_attn_Wv, L0_attn_bv, L0_attn_Wo, L0_attn_bo,
           L0_ffn_lut, L0_ffn_gate_idx, L0_ffn_up_idx, L0_ffn_down_idx,
           L1_attn_wq, L1_attn_bq, L1_attn_wk, L1_attn_bk, L1_attn_wv,
           L1_attn_bv, L1_attn_wo, L1_attn_bo,
           L1_ffn_wq_gate, L1_ffn_s_gate, L1_ffn_wq_up, L1_ffn_s_up,
           L1_ffn_wq_down, L1_ffn_s_down):
    cfg = FULL
    B = NCORES
    ident = np.eye(P, dtype=np.float32)
    ones = np.ones((P, P), dtype=np.float32)
    tri = np.triu(np.ones((P, P), dtype=np.float32))
    cv = [1e-5, 1.0 / cfg.D, -0.5, cfg.SCALE, 128.0]
    for j in range(1, 11):
        cv.append(-128.0 * (j - 0.5))
    cvals = np.array(cv, dtype=np.float32)

    def T(a):
        return np.ascontiguousarray(np.asarray(a).T)

    x = np.ascontiguousarray(np.asarray(L0_attn_x, np.float32))

    g_specs = [
        dict(name="g0g", rows=cfg.D // B, cols=cfg.I0, mode="lut"),
        dict(name="g0u", rows=cfg.D // B, cols=cfg.I0, mode="lut"),
        dict(name="g0d", rows=cfg.I0 // B, cols=cfg.D, mode="lut"),
        dict(name="g1g", rows=cfg.D // B, cols=cfg.I1, mode="deq"),
        dict(name="g1u", rows=cfg.D // B, cols=cfg.I1, mode="deq"),
        dict(name="g1d", rows=cfg.I1 // B, cols=cfg.D, mode="deqscale"),
    ]
    gT = {"g0g": T(L0_ffn_gate_idx), "g0u": T(L0_ffn_up_idx),
          "g0d": T(L0_ffn_down_idx),
          "g1g": T(L1_ffn_wq_gate), "g1u": T(L1_ffn_wq_up),
          "g1d": T(L1_ffn_wq_down)}
    nc_g = build_gather(cfg, g_specs)
    in_maps = []
    for core in range(B):
        m = {"lut": np.asarray(L0_ffn_lut, np.float32),
             "sdown": np.asarray(L1_ffn_s_down, np.float32)}
        for spec in g_specs:
            nm, R = spec["name"], spec["rows"]
            m[nm + "_i"] = np.ascontiguousarray(gT[nm][core * R:(core + 1) * R, :],
                                                dtype=np.int32)
        in_maps.append(m)
    res = run_bass_kernel_spmd(nc_g, in_maps, core_ids=list(range(B)))
    gath = {s["name"]: np.concatenate([res.results[i][s["name"] + "_o"]
                                       for i in range(B)], axis=0)
            for s in g_specs}

    nc_a = build_l0a(cfg)
    wqT, wkT, wvT, woT = T(L0_attn_Wq), T(L0_attn_Wk), T(L0_attn_Wv), T(L0_attn_Wo)
    in_maps = [{
        "xin": np.ascontiguousarray(x[core]),
        "wq": wqT, "wk": wkT, "wv": wvT, "wo": woT,
        "bq": _colmajor(L0_attn_bq, cfg.ND),
        "bk": _colmajor(L0_attn_bk, cfg.ND),
        "bv": _colmajor(L0_attn_bv, cfg.ND),
        "bo": np.asarray(L0_attn_bo, np.float32),
        "cident": ident, "cvals": cvals,
    } for core in range(B)]
    res = run_bass_kernel_spmd(nc_a, in_maps, core_ids=list(range(B)))
    x1 = [res.results[i]["xout"] for i in range(B)]

    nc_b = build_ffn(cfg, cfg.NO0, with_scales=False)
    in_maps = [{
        "xin": x1[core], "wg": gath["g0g"], "wu": gath["g0u"], "wd": gath["g0d"],
        "cident": ident, "cvals": cvals,
    } for core in range(B)]
    res = run_bass_kernel_spmd(nc_b, in_maps, core_ids=list(range(B)))
    x2 = [res.results[i]["xout"] for i in range(B)]

    nc_c = build_l1a(cfg)
    wqT, wkT, wvT, woT = T(L1_attn_wq), T(L1_attn_wk), T(L1_attn_wv), T(L1_attn_wo)
    in_maps = [{
        "xin": x2[core],
        "wq": wqT, "wk": wkT, "wv": wvT, "wo": woT,
        "bq": _colmajor(L1_attn_bq, cfg.NH),
        "bk": _colmajor(L1_attn_bk, cfg.NH),
        "bv": np.asarray(L1_attn_bv, np.float32),
        "bo": np.asarray(L1_attn_bo, np.float32),
        "cident": ident, "cones": ones, "ctri": tri, "cvals": cvals,
    } for core in range(B)]
    res = run_bass_kernel_spmd(nc_c, in_maps, core_ids=list(range(B)))
    x3 = [res.results[i]["xout"] for i in range(B)]

    nc_d = build_ffn(cfg, cfg.NO1, with_scales=True)
    in_maps = [{
        "xin": x3[core], "wg": gath["g1g"], "wu": gath["g1u"], "wd": gath["g1d"],
        "sg": _colmajor(L1_ffn_s_gate, cfg.NO1),
        "su": _colmajor(L1_ffn_s_up, cfg.NO1),
        "cident": ident, "cvals": cvals,
    } for core in range(B)]
    res = run_bass_kernel_spmd(nc_d, in_maps, core_ids=list(range(B)))
    out = np.stack([res.results[i]["xout"] for i in range(B)], axis=0)
    return np.ascontiguousarray(out.astype(np.float32))


# version 19 (final)
# speedup vs baseline: 1.1360x; 1.1212x over previous
import numpy as np
from contextlib import ExitStack

import concourse.bass as bass
import concourse.mybir as mybir
from concourse.bass_utils import run_bass_kernel_spmd

F32 = mybir.dt.float32
BF16 = mybir.dt.bfloat16
I32 = mybir.dt.int32
I8 = mybir.dt.uint8
AF = mybir.ActivationFunctionType
ALU = mybir.AluOpType
AX = mybir.AxisListType

NCORES = 8
P = 128


class CFG:
    def __init__(self, S=2048, D=2048, NH=16, I0=5504, I1=5632):
        self.S, self.D, self.NH, self.I0, self.I1 = S, D, NH, I0, I1
        self.NT = S // P
        self.ND = D // P
        self.HD = D // NH
        assert self.HD == 128, "builders assume head_dim == 128"
        self.NO0 = I0 // P
        self.NO1 = I1 // P
        self.S2 = S // 2
        self.NT2 = self.NT // 2
        self.W = min(512, S)
        self.WD = min(512, D)
        self.SCALE = float(self.HD) ** -0.5


FULL = CFG()



class Sched:
    ENGINES = ("tensor", "vector", "scalar", "sync", "gpsimd")

    def __init__(self, nc):
        self.nc = nc
        self.items = {e: [] for e in self.ENGINES}
        self.cnt = {}
        self.default_waits = []

    def tokv(self, sem):
        return (sem, self.cnt.get(sem, 0))

    def op(self, eng, emit, waits=(), sem=None, inc=1, quiet=False):
        sem = sem or ("c_" + eng)
        if quiet:
            ws = tuple((s, v) for (s, v) in merge(list(waits), self.default_waits)
                       if v > 0)
            self.items[eng].append((ws, emit, sem, 0))
            return (sem, self.cnt.get(sem, 0))
        val = self.cnt.get(sem, 0) + inc
        self.cnt[sem] = val
        ws = tuple((s, v) for (s, v) in merge(list(waits), self.default_waits) if v > 0)
        self.items[eng].append((ws, emit, sem, inc))
        return (sem, val)

    def dma(self, eng, out, in_, waits=(), sem=None):
        assert sem is not None
        return self.op(eng, lambda e, o=out, i=in_: e.dma_start(o, i), waits, sem, 16)

    def emit(self, stack):
        names = {}
        for items in self.items.values():
            for ws, _, sem, _ in items:
                for s, _ in ws:
                    names.setdefault(s, None)
                names.setdefault(sem, None)
        for n in list(names):
            names[n] = stack.enter_context(self.nc.semaphore(n))
        blk = stack.enter_context(self.nc.Block())
        for eng in self.ENGINES:
            if not self.items[eng]:
                continue

            def make(eng):
                def body(e):
                    maxw = {}
                    for ws, emit, sem, inc in self.items[eng]:
                        for (wsem, wval) in ws:
                            if wval > maxw.get(wsem, 0):
                                e.wait_ge(names[wsem], wval)
                                maxw[wsem] = wval
                        inst = emit(e)
                        if inst is not None and inc > 0:
                            inst.then_inc(names[sem], inc)
                return body
            getattr(blk, eng)(make(eng))


def merge(*toks):
    best = {}
    for t in toks:
        if not t:
            continue
        if isinstance(t, tuple) and len(t) == 2 and isinstance(t[0], str):
            t = [t]
        for (s, v) in t:
            if v > best.get(s, 0):
                best[s] = v
    return list(best.items())


class Core:
    def __init__(self, cfg):
        self.cfg = cfg
        self.nc = bass.Bass("TRN2", target_bir_lowering=False)
        self.S = Sched(self.nc)
        self.stack = ExitStack()
        self.thunks = []

    def pump(self, n=1):
        for _ in range(min(n, len(self.thunks))):
            self.thunks.pop(0)()

    def drain(self):
        self.pump(len(self.thunks))

    def sbuf(self, name, shape, dtype):
        return self.stack.enter_context(self.nc.sbuf_tensor(name, list(shape), dtype))

    def psum(self, name, shape, dtype):
        return self.stack.enter_context(self.nc.psum_tensor(name, list(shape), dtype))

    def din(self, name, shape, dtype):
        return self.nc.dram_tensor(name, list(shape), dtype, kind="ExternalInput").ap()

    def dout(self, name, shape, dtype):
        return self.nc.dram_tensor(name, list(shape), dtype, kind="ExternalOutput").ap()

    def dtmp(self, name, shape, dtype):
        return self.nc.dram_tensor(name, list(shape), dtype, kind="Internal").ap()

    CONSTS = ()

    def setup_common(self, need_ones=False, need_tri=False):
        cfg = self.cfg
        base = [1e-5, 1.0 / cfg.D, -0.5, cfg.SCALE, 128.0]
        for j in range(1, 11):
            base.append(-128.0 * (j - 0.5))
        self.CONSTS = tuple(base)
        toks = []
        n = len(self.CONSTS)
        cvals = self.din("cvals", (n,), F32)
        self.cpool = self.sbuf("cpool", (P, n), F32)
        toks.append(self.S.dma("sync", self.cpool[:],
                               cvals.unsqueeze(0).broadcast_to((P, n)), sem="d_c2"))
        for i, v in enumerate(self.CONSTS):
            self.nc.const_aps.register(F32, float(v), self.cpool[:, i:i + 1])
        self.ident_f = self.din("cident", (P, P), F32)
        self.ident = self.sbuf("identb", (P, P), BF16)
        toks.append(self.S.dma("gpsimd", self.ident[:], self.ident_f, sem="d_c"))
        if need_ones:
            self.ones_f = self.din("cones", (P, P), F32)
            self.ones = self.sbuf("onesb", (P, P), BF16)
            toks.append(self.S.dma("gpsimd", self.ones[:], self.ones_f, sem="d_c"))
        if need_tri:
            self.tri_f = self.din("ctri", (P, P), F32)
            self.tri = self.sbuf("trib", (P, P), BF16)
            toks.append(self.S.dma("gpsimd", self.tri[:], self.tri_f, sem="d_c"))
        self.const_tok = merge(*toks)
        self.S.default_waits = self.const_tok
        self.psA = self.psum("psA", (P, 2048), F32)
        self.psB = self.psum("psB", (P, 2048), F32)

    def ps512(self, i):
        W = self.cfg.W
        per = self.psA.shape[1] // W
        t = self.psA if i < per else self.psB
        return t[:, (i % per) * W:((i % per) + 1) * W]



def carve(t, col0, ncol_b, shape, dtype):
    v = t[:, col0:col0 + ncol_b].bitcast(dtype)
    assert v.shape[1] >= shape[1], (v.shape, shape)
    return v[:, 0:shape[1]]


def carve_or_alloc(c, t, col0, shape, dtype, name):
    need = shape[1] * (2 if dtype == F32 else 1)
    if t.shape[1] >= col0 + need:
        return carve(t[:, :] if hasattr(t, "ap") else t, col0, need, shape, dtype)
    return c.sbuf(name, shape, dtype)[:, :]


def transpose_in(c, src_hbm, arena, deps, tag, xf, xb):
    cfg, S = c.cfg, c.S
    psT = c.psB[:, 0:512].bitcast(BF16)
    cast_tok = [None, None]
    last_ev = None
    all_ev = []
    for ti in range(cfg.NT):
        b = ti % 2
        ld = S.dma("sync", xf[b][:, 0:cfg.D], src_hbm[ti * P:(ti + 1) * P, :],
                   waits=merge(deps, cast_tok[b]), sem=f"d_{tag}x{b}")
        ct = S.op("scalar", lambda e, b=b:
                  e.activation(xb[b][:, 0:cfg.D], xf[b][:, 0:cfg.D], AF.Identity,
                               bias=0.0, scale=1.0),
                  waits=merge([ld], cast_tok[b]))
        cast_tok[b] = ct
        nb = (cfg.ND + 7) // 8
        for g in range(nb):
            js = list(range(g * 8, min((g + 1) * 8, cfg.ND)))

            def em(e, js=js, b=b):
                inst = None
                for k, j in enumerate(js):
                    inst = e.transpose(psT[:, k * P:(k + 1) * P],
                                       xb[b][:, j * P:(j + 1) * P], c.ident[:])
                return inst
            mm = S.op("tensor", em, waits=merge([ct], c.const_tok, last_ev))
            j0, n = js[0], len(js)
            dst = arena[:, :].rearrange("p (nd s) -> p nd s", nd=cfg.ND)[
                :, j0:j0 + n, ti * P:(ti + 1) * P]
            src = psT[:, 0:n * P].rearrange("p (n q) -> p n q", n=n)
            last_ev = S.op("vector", lambda e, d=dst, s=src: e.tensor_copy(d, s),
                           waits=[mm])
            all_ev.append(last_ev)
    return merge(last_ev, cast_tok[0], cast_tok[1], *all_ev)


def epilogue(c, part, nparts, midT, Cdim, wp_blk, bias_view, resid_hbm, z_hbm,
             deps, tag, w_eng, ps_free, eb, wready=None):
    cfg, S = c.cfg, c.S
    NTp = cfg.NT // nparts
    assert NTp <= 8
    NC = Cdim // P
    ne2 = cfg.D // cfg.WD
    wb, rb, zb = eb["wb"], eb["rb"], eb["zb"]
    wtok = eb.setdefault("wtok", [None, None])
    rtok = eb.setdefault("rtok", [None, None])
    zslot = eb.setdefault("zslot", [None] * 2)
    prev_ev = ps_free
    st_all, ev_all, mm_all = [], [], []
    EBG = 4
    nblk = (NC + EBG - 1) // EBG
    for e2c in range(ne2):
        mm = None
        for blk in range(nblk):
            b = blk % 2
            ct0 = blk * EBG
            ng = min(EBG, NC - ct0)
            wrd = wready(e2c) if wready else []
            c.pump(1)
            lt = S.dma(w_eng,
                       wb[b][:, 0:ng * cfg.WD].rearrange("p (g w) -> p g w", g=ng),
                       wp_blk(ct0, ng, e2c),
                       waits=merge(deps, wtok[b], wrd), sem=f"d_epw_{w_eng}{b}")

            def em(e, ct0=ct0, ng=ng, b=b):
                inst = None
                for g in range(ng):
                    for tt in range(NTp):
                        inst = e.matmul(c.ps512(tt)[:, 0:cfg.WD],
                                        midT(ct0 + g, tt),
                                        wb[b][:, g * cfg.WD:(g + 1) * cfg.WD],
                                        start=(ct0 + g == 0),
                                        stop=(ct0 + g == NC - 1))
                return inst
            mm = S.op("tensor", em, waits=merge([lt], deps,
                                                prev_ev if blk == 0 else None))
            wtok[b] = mm
        mm_all.append(mm)
        evs = []
        for tt in range(NTp):
            trow = part * NTp + tt
            b = tt % 2
            zi = tt % 2
            lr = S.dma("sync", rb[b][:, 0:cfg.WD],
                       resid_hbm[trow * P:(trow + 1) * P,
                                 e2c * cfg.WD:(e2c + 1) * cfg.WD],
                       waits=merge(deps, rtok[b]), sem=f"d_epr{b}")
            ps = c.ps512(tt)
            bv = bias_view(e2c) if bias_view is not None else None
            if bv is not None:
                t1 = S.op("vector", lambda e, zi=zi, ps=ps, bv=bv:
                          e.tensor_tensor(zb[zi][:, 0:cfg.WD], ps[:, 0:cfg.WD],
                                          bv, ALU.add),
                          waits=merge([mm], zslot[zi], deps))
                t2 = S.op("vector", lambda e, zi=zi, b=b:
                          e.tensor_tensor(zb[zi][:, 0:cfg.WD], zb[zi][:, 0:cfg.WD],
                                          rb[b][:, 0:cfg.WD], ALU.add),
                          waits=merge([t1], [lr]))
            else:
                t2 = S.op("vector", lambda e, zi=zi, b=b, ps=ps:
                          e.tensor_tensor(zb[zi][:, 0:cfg.WD], ps[:, 0:cfg.WD],
                                          rb[b][:, 0:cfg.WD], ALU.add),
                          waits=merge([mm], [lr], zslot[zi]))
            rtok[b] = t2
            ev_all.append(t2)
            evs.append(t2)
            st = S.dma("scalar", z_hbm[trow * P:(trow + 1) * P,
                                       e2c * cfg.WD:(e2c + 1) * cfg.WD],
                       zb[zi][:, 0:cfg.WD], waits=[t2], sem=f"d_z{zi}")
            zslot[zi] = st
            st_all.append(st)
        prev_ev = merge(*evs)
    return dict(st=merge(*st_all), ev=merge(*ev_all), mm=merge(*mm_all))


def ln_phase(c, part, nparts, z_hbm, out_hbm, deps, lb, eps=1e-5):
    cfg, S = c.cfg, c.S
    NTp = cfg.NT // nparts
    zr, sq = lb["zr"], lb["sq"]
    cols = lb["cols"]
    rs, mneg, ssum, lnv, rstd, bb = (cols[:, i:i + 1] for i in range(6))
    zfree = lb.setdefault("zfree", [None, None])
    xfree = lb.setdefault("xfree", [None, None])
    state = lb.setdefault("state", {})
    sts = []
    for tt in range(NTp):
        c.pump(1)
        trow = part * NTp + tt
        b = tt % 2
        ld = S.dma("sync", zr[b][:, 0:cfg.D], z_hbm[trow * P:(trow + 1) * P, :],
                   waits=merge(deps, zfree[b]), sem=f"d_lnl{b}")
        t1 = S.op("vector", lambda e, b=b: e.reduce_sum(rs, zr[b][:, 0:cfg.D], AX.X),
                  waits=merge([ld], state.get("chain")))
        t2 = S.op("vector", lambda e: e.tensor_scalar(mneg, rs, -1.0 / cfg.D,
                                                      None, ALU.mult),
                  waits=[t1])
        t3 = S.op("scalar", lambda e, b=b: e.activation(sq[:, 0:cfg.D],
                                                        zr[b][:, 0:cfg.D], AF.Square,
                                                        bias=mneg, scale=1.0,
                                                        accum_out=ssum),
                  waits=merge([t2], state.get("sq")))
        t4 = S.op("scalar", lambda e: e.activation(lnv, ssum, AF.Ln,
                                                   bias=float(eps), scale=1.0 / cfg.D),
                  waits=[t3])
        t5 = S.op("scalar", lambda e: e.activation(rstd, lnv, AF.Exp,
                                                   bias=0.0, scale=-0.5),
                  waits=[t4])
        t6 = S.op("vector", lambda e: e.tensor_tensor(bb, mneg, rstd, ALU.mult),
                  waits=[t5])
        t7 = S.op("scalar", lambda e, b=b: e.activation(zr[b][:, 0:cfg.D],
                                                        zr[b][:, 0:cfg.D], AF.Identity,
                                                        bias=bb, scale=rstd),
                  waits=merge([t6], xfree[b]))
        st = S.dma("scalar", out_hbm[trow * P:(trow + 1) * P, :], zr[b][:, 0:cfg.D],
                   waits=[t7], sem="d_o")
        zfree[b] = st
        xfree[b] = st
        state["chain"] = t6
        state["sq"] = t7
        sts.append(st)
    return merge(*sts)



def l0_attn_half(c, H, arena, ctx, w_aps, bt, bufs, deps, psA_free, psB_free,
                 ctx_free):
    cfg, S = c.cfg, c.S
    ND, S2, SQ = cfg.ND, cfg.S2, cfg.S
    nW = S2 // min(cfg.W, S2)
    WH = min(cfg.W, S2)
    wA, wK = bufs["wA"], bufs["wK"]
    m0, t1b = bufs["m0"], bufs["t1b"]
    phq, phk, vv, kv = bufs["phq"], bufs["phk"], bufs["vv"], bufs["kv"]
    kvc = bufs["kvc"]
    zcol = bufs["zcol"]
    carr = bufs["carr"]
    ztok = bufs["ztok"]
    wa_tok = bufs.setdefault("wa_tok", None)
    wk_tok = bufs.setdefault("wk_tok", None)
    scr = bufs.setdefault("scr", None)
    kscr = bufs.setdefault("kscr", None)
    vv_free = bufs.setdefault("vv_free", None)
    kv_free = bufs.setdefault("kv_free", None)
    kvc_free = bufs.setdefault("kvc_free", None)
    ctx_tok = []

    def w_src(k, et):
        return w_aps[k].rearrange("(nj p) d -> p nj d", p=P)[:, :, et * P:(et + 1) * P]

    def mk_mm(wl, ps):
        def em(e):
            inst = None
            for j in range(ND):
                for cc in range(nW):
                    c0 = H * S2 + cc * WH
                    inst = e.matmul(ps[:, cc * WH:(cc + 1) * WH],
                                    wl[:, j * P:(j + 1) * P],
                                    arena[:, j * SQ + c0: j * SQ + c0 + WH],
                                    start=(j == 0), stop=(j == ND - 1))
            return inst
        return em

    for et in range(ND):
        c.pump(3)
        bq = bt["bq"][:, et:et + 1]
        bk = bt["bk"][:, et:et + 1]
        bvc = bt["bv"][:, et:et + 1]
        lq = c.S.dma("gpsimd", wA.rearrange("p (nj d) -> p nj d", nj=ND),
                     w_src("wq", et), waits=merge(deps, bufs["wa_tok"]), sem="d_wa")
        mmq = c.S.op("tensor", mk_mm(wA, c.psA), waits=merge([lq], deps, psA_free))
        vq1 = c.S.op("vector", lambda e, bq=bq: e.tensor_scalar(
            m0[:, 0:S2], c.psA[:, 0:S2], bq, 0.0, ALU.add, ALU.min),
            waits=merge([mmq], scr))
        sq1 = c.S.op("scalar", lambda e: e.activation(m0[:, 0:S2], m0[:, 0:S2],
                                                      AF.Exp, bias=0.0, scale=1.0),
                     waits=[vq1])
        vq2 = c.S.op("vector", lambda e, bq=bq: e.tensor_scalar(
            t1b[:, 0:S2], c.psA[:, 0:S2], bq, 0.0, ALU.add, ALU.max),
            waits=merge([mmq], scr, [vq1]))
        vq3 = c.S.op("vector", lambda e: e.tensor_tensor(
            phq[:, 0:S2], t1b[:, 0:S2], m0[:, 0:S2], ALU.add),
            waits=merge([sq1], [vq2], scr))
        lv = c.S.dma("gpsimd", wA.rearrange("p (nj d) -> p nj d", nj=ND),
                     w_src("wv", et), waits=merge([mmq], deps), sem="d_wa")
        bufs["wa_tok"] = None
        mmv = c.S.op("tensor", mk_mm(wA, c.psA), waits=merge([lv], [vq1], [vq2], deps))
        bufs["wa_tok"] = mmv
        sv1 = c.S.op("scalar", lambda e, bvc=bvc: e.activation(
            vv[:, 0:S2], c.psA[:, 0:S2], AF.Identity, bias=bvc, scale=1.0),
            waits=merge([mmv], vv_free))
        psA_free = sv1
        lk = c.S.dma("gpsimd", wK.rearrange("p (nj d) -> p nj d", nj=ND),
                     w_src("wk", et), waits=merge(deps, bufs["wk_tok"]), sem="d_wk")
        mmk = c.S.op("tensor", mk_mm(wK, c.psB), waits=merge([lk], deps, psB_free))
        bufs["wk_tok"] = mmk
        vk1 = c.S.op("vector", lambda e, bk=bk: e.tensor_scalar(
            t1b[:, 0:S2], c.psB[:, 0:S2], bk, 0.0, ALU.add, ALU.min),
            waits=merge([mmk], [vq3], kscr))
        sk1 = c.S.op("scalar", lambda e: e.activation(t1b[:, 0:S2], t1b[:, 0:S2],
                                                      AF.Exp, bias=0.0, scale=1.0),
                     waits=[vk1])
        vk2 = c.S.op("vector", lambda e, bk=bk: e.tensor_scalar(
            m0[:, 0:S2], c.psB[:, 0:S2], bk, 0.0, ALU.add, ALU.max),
            waits=merge([mmk], [vq3], [sq1], [vk1]))
        vk3 = c.S.op("vector", lambda e: e.tensor_tensor(
            phk[:, 0:S2], m0[:, 0:S2], t1b[:, 0:S2], ALU.add),
            waits=merge([sk1], [vk2], kscr))
        psB_free = vk2
        bufs["kscr"] = None
        v1 = c.S.op("vector", lambda e: e.tensor_tensor(
            kv[:, 0:S2], phk[:, 0:S2], vv[:, 0:S2], ALU.mult),
            waits=merge([vk3], [sv1], kv_free))
        bufs["vv_free"] = v1
        init = 0.0 if H == 0 else carr[:, et:et + 1]
        v2 = c.S.op("vector", lambda e, init=init: e.tensor_tensor_scan(
            kvc[:, 0:S2], kv[:, 0:S2],
            zcol[:, 0:1].broadcast_to((P, S2)), init, ALU.add, ALU.add),
            waits=merge([v1], [ztok], kvc_free,
                        bufs.get("carr_w%d" % et) if H == 1 else None))
        bufs["kv_free"] = v2
        cw = c.S.op("vector", lambda e, et=et: e.tensor_copy(
            carr[:, et:et + 1], kvc[:, S2 - 1:S2]), waits=[v2]) if H == 0 else None
        if H == 0:
            bufs["carr_w%d" % et] = cw
        dst = ctx[:, et * S2:(et + 1) * S2]
        v3 = c.S.op("vector", lambda e, dst=dst: e.scalar_tensor_tensor(
            dst, kvc[:, 0:S2], cfg.SCALE, phq[:, 0:S2], ALU.mult, ALU.mult),
            waits=merge([v2], [vq3], ctx_free, [cw] if cw else None))
        bufs["kvc_free"] = v3
        bufs["scr"] = v3
        bufs["kscr"] = vk3
        scr = v3
        kscr = vk3
        vv_free = v1
        kv_free = v2
        kvc_free = v3
        ctx_tok.append(v3)
    return dict(done=merge(*ctx_tok), psA_free=psA_free, psB_free=psB_free)



def l1_attn(c, arena, oT, w_aps, bt, bv_bcast, deps):
    cfg, S = c.cfg, c.S
    D, ND, NT, NH = cfg.D, cfg.ND, cfg.NT, cfg.NH
    SQ = cfg.S
    HPG = min(cfg.NH, 2)
    VW = HPG * cfg.HD
    nW = SQ // cfg.W
    NQ = SQ // cfg.W
    perA = c.psA.shape[1] // cfg.W
    vg = c.sbuf("l1_vg", (P, NT * VW), BF16)
    qh = c.sbuf("l1_qh", (P, SQ), BF16)
    kh = c.sbuf("l1_kh", (P, SQ), BF16)
    wqb = [c.sbuf(f"l1_wq{i}", (P, ND * P), BF16) for i in range(2)]
    wkb = [c.sbuf(f"l1_wk{i}", (P, ND * P), BF16) for i in range(2)]
    wvb = c.sbuf("l1_wv", (P, ND * VW), BF16)
    pT = [c.sbuf(f"l1_pT{i}", (P, cfg.W), BF16) for i in range(2)]
    rr = c.sbuf("l1_rr", (P, cfg.W), F32)
    wq_tok = [None, None]
    wk_tok = [None, None]
    wv_tok = None
    psA_free = None
    psB_free = None
    sslot_free = [None, None]
    rslot_free = [None, None]
    pT_free = [None, None]
    rr_free = None
    qh_free = None
    kh_free = None
    o_toks = []

    for g in range(NH // HPG):
        srcv = w_aps["wv"].rearrange("(nj p) d -> p nj d", p=P)[
            :, :, g * VW:(g + 1) * VW]
        lv = S.dma("gpsimd", wvb[:].rearrange("p (nj d) -> p nj d", nj=ND), srcv,
                   waits=merge(deps, wv_tok), sem="d_wv0")
        vg_evs = []
        for tt in range(NT):
            ps = c.ps512(tt % perA)

            def em(e, tt=tt, ps=ps):
                inst = None
                for j in range(ND):
                    inst = e.matmul(ps[:, 0:VW],
                                    arena[:, j * SQ + tt * P: j * SQ + (tt + 1) * P],
                                    wvb[:, j * VW:(j + 1) * VW],
                                    start=(j == 0), stop=(j == ND - 1))
                return inst
            mmv = S.op("tensor", em, waits=merge([lv], deps,
                                                 psA_free if tt < perA else None,
                                                 vg_evs[tt - perA] if tt >= perA else None))
            ev = S.op("vector", lambda e, tt=tt, ps=ps, g=g: e.tensor_tensor(
                vg[:, tt * VW:(tt + 1) * VW], ps[:, 0:VW],
                bv_bcast[:, g * VW:(g + 1) * VW], ALU.add),
                waits=merge([mmv], wv_tok))
            vg_evs.append(ev)
        wv_tok = merge(*vg_evs)
        psA_free = wv_tok
        head_evs = []
        for hh in range(HPG):
            h = g * HPG + hh
            b = h % 2
            srcq = w_aps["wq"].rearrange("(nj p) d -> p nj d", p=P)[
                :, :, h * P:(h + 1) * P]
            srck = w_aps["wk"].rearrange("(nj p) d -> p nj d", p=P)[
                :, :, h * P:(h + 1) * P]
            lq = S.dma("gpsimd", wqb[b][:].rearrange("p (nj d) -> p nj d", nj=ND),
                       srcq, waits=merge(deps, wq_tok[b]), sem=f"d_wq{b}")
            lk = S.dma("gpsimd", wkb[b][:].rearrange("p (nj d) -> p nj d", nj=ND),
                       srck, waits=merge(deps, wk_tok[b]), sem=f"d_wk{b}")

            def mkqk(wl, ps, b):
                def em(e):
                    inst = None
                    for j in range(ND):
                        for cc in range(nW):
                            inst = e.matmul(ps[:, cc * cfg.W:(cc + 1) * cfg.W],
                                            wl[b][:, j * P:(j + 1) * P],
                                            arena[:, j * SQ + cc * cfg.W:
                                                  j * SQ + (cc + 1) * cfg.W],
                                            start=(j == 0), stop=(j == ND - 1))
                    return inst
                return em
            mmq = S.op("tensor", mkqk(wqb, c.psA, b),
                       waits=merge([lq], deps, psA_free, qh_free))
            wq_tok[b] = mmq
            sq = S.op("scalar", lambda e, h=h: e.activation(
                qh[:], c.psA[:, 0:SQ], AF.Identity, bias=bt["bq"][:, h:h + 1],
                scale=1.0),
                waits=merge([mmq], qh_free))
            mmk = S.op("tensor", mkqk(wkb, c.psB, b),
                       waits=merge([lk], deps, psB_free, kh_free,
                                   sslot_free[0], sslot_free[1],
                                   rslot_free[0], rslot_free[1]))
            wk_tok[b] = mmk
            sk = S.op("scalar", lambda e, h=h: e.activation(
                kh[:], c.psB[:, 0:SQ], AF.Identity, bias=bt["bk"][:, h:h + 1],
                scale=1.0),
                waits=merge([mmk], kh_free))
            psB_free = sk
            evs = []
            for qg in range(NQ):
                q0 = qg * cfg.W
                jmax = min(NT - 1, (q0 + cfg.W - 1) // P)
                rs_i = 2 + (qg % 2)
                rs = c.psB[:, rs_i * cfg.W:(rs_i + 1) * cfg.W]
                last_p = None
                for j in range(jmax + 1):
                    sl = j % 2
                    sT = c.psB[:, sl * cfg.W:(sl + 1) * cfg.W]
                    mm1 = S.op("tensor", lambda e, j=j, q0=q0, sT=sT: e.matmul(
                        sT[:, 0:cfg.W], kh[:, j * P:(j + 1) * P], qh[:, q0:q0 + cfg.W],
                        start=True, stop=True),
                        waits=merge([sq], [sk], sslot_free[sl]))
                    pb = pT[sl]
                    se = S.op("scalar", lambda e, pb=pb, sT=sT: e.activation(
                        pb[:], sT[:, 0:cfg.W], AF.Exp, bias=0.0, scale=cfg.SCALE),
                        waits=merge([mm1], pT_free[sl]))
                    sslot_free[sl] = se
                    off = j * P - q0
                    mtoks = [se]
                    if off > 0:
                        z1 = S.op("vector", lambda e, pb=pb, off=off:
                                  e.memset(pb[:, 0:off], 0.0), waits=[se])
                        mtoks.append(z1)
                    if 0 <= off <= cfg.W - P:
                        z2 = S.op("vector", lambda e, pb=pb, off=off: e.tensor_tensor(
                            pb[:, off:off + P], pb[:, off:off + P], c.tri[:], ALU.mult),
                            waits=merge([se], c.const_tok))
                        mtoks.append(z2)
                    ptok = merge(*mtoks)
                    mm2 = S.op("tensor", lambda e, pb=pb, rs=rs, j=j, jmax=jmax:
                               e.matmul(rs[:, 0:cfg.W], c.ones[:], pb[:],
                                        start=(j == 0), stop=(j == jmax)),
                               waits=merge(ptok, c.const_tok,
                                           rslot_free[rs_i - 2] if j == 0 else None))
                    vsl = vg[:, j * VW + hh * cfg.HD: j * VW + hh * cfg.HD + P]
                    mm3 = S.op("tensor", lambda e, vsl=vsl, pb=pb, q0=q0, j=j, jmax=jmax:
                               e.matmul(c.psA[:, q0:q0 + cfg.W], vsl, pb[:],
                                        start=(j == 0), stop=(j == jmax)),
                               waits=merge(ptok, wv_tok, [sq] if j == 0 else None))
                    last_p = merge([mm2], [mm3])
                    pT_free[sl] = last_p
                v1 = S.op("vector", lambda e, rs=rs: e.reciprocal(rr[:], rs[:, 0:cfg.W]),
                          waits=merge(last_p, rr_free))
                rslot_free[rs_i - 2] = v1
                dst = oT[:, h * SQ + q0: h * SQ + q0 + cfg.W]
                ev = S.op("vector", lambda e, dst=dst, q0=q0:
                          e.scalar_tensor_tensor(dst, c.psA[:, q0:q0 + cfg.W], 1.0,
                                                 rr[:], ALU.mult, ALU.mult),
                          waits=merge([v1], last_p))
                rr_free = ev
                evs.append(ev)
            psA_free = merge(*evs)
            qh_free = psA_free
            kh_free = psA_free
            head_evs.extend(evs)
            o_toks.extend(evs)
        wv_tok = merge(wv_tok, *head_evs)
    return dict(done=merge(*o_toks), psA_free=psA_free, psB_free=merge(
        sslot_free[0], sslot_free[1], rslot_free[0], rslot_free[1], psB_free))



def ffn_core(c, arena, hT, wg_hbm, wu_hbm, wd_hbm, sg_t, su_t, NO,
             resid_hbm, z_hbm, out_hbm, deps, eb, lb):
    cfg, S = c.cfg, c.S
    D, ND = cfg.D, cfg.ND
    SQ = cfg.S
    NP = 4 if cfg.NT >= 8 else 2
    SP_ = SQ // NP
    WH = min(cfg.W, SP_)
    nW2 = SP_ // WH
    gw = [c.sbuf(f"ff_gw{i}", (P, ND * P), BF16) for i in range(2)]
    uw = [c.sbuf(f"ff_uw{i}", (P, ND * P), BF16) for i in range(2)]
    sgb = c.sbuf("ff_sg", (P, SP_), BF16)
    gtok = [None, None]
    utok = [None, None]
    out_toks = []
    ep_prev = None
    hT_readers = None
    for PI in range(NP):
        psg_free = ep_prev["ev"] if ep_prev else None
        psu_free = ep_prev["ev"] if ep_prev else None
        sg_free = None
        h_toks = []
        for ot in range(NO):
            b = ot % 2
            srcg = wg_hbm.rearrange("(nj p) o -> p nj o", p=P)[
                :, :, ot * P:(ot + 1) * P]
            srcu = wu_hbm.rearrange("(nj p) o -> p nj o", p=P)[
                :, :, ot * P:(ot + 1) * P]
            lg = S.dma("sync", gw[b][:].rearrange("p (nj o) -> p nj o", nj=ND),
                       srcg, waits=merge(deps, gtok[b]), sem=f"d_g{b}")
            lu = S.dma("sync", uw[b][:].rearrange("p (nj o) -> p nj o", nj=ND),
                       srcu, waits=merge(deps, utok[b]), sem=f"d_u{b}")

            def mk(wl, ps, b, PI=PI):
                def em(e):
                    inst = None
                    for j in range(ND):
                        for cc in range(nW2):
                            c0 = PI * SP_ + cc * WH
                            inst = e.matmul(ps[:, cc * WH:(cc + 1) * WH],
                                            wl[b][:, j * P:(j + 1) * P],
                                            arena[:, j * SQ + c0: j * SQ + c0 + WH],
                                            start=(j == 0), stop=(j == ND - 1))
                    return inst
                return em
            mmg = S.op("tensor", mk(gw, c.psA, b), waits=merge([lg], deps, psg_free))
            gtok[b] = mmg
            mmu = S.op("tensor", mk(uw, c.psB, b), waits=merge([lu], deps, psu_free))
            utok[b] = mmu
            sg_sc = sg_t[:, ot:ot + 1] if sg_t is not None else 1.0
            s1 = S.op("scalar", lambda e, sg_sc=sg_sc: e.activation(
                sgb[:, 0:SP_], c.psA[:, 0:SP_], AF.Silu, bias=0.0, scale=sg_sc),
                waits=merge([mmg], sg_free))
            psg_free = s1
            dsth = hT[:, ot * SP_:(ot + 1) * SP_]
            hw = merge([mmu], [s1], hT_readers if PI > 0 else None)
            if su_t is None:
                v1 = S.op("vector", lambda e, dsth=dsth: e.tensor_tensor(
                    dsth, c.psB[:, 0:SP_], sgb[:, 0:SP_], ALU.mult), waits=hw)
            else:
                su_sc = su_t[:, ot:ot + 1]
                v1 = S.op("vector", lambda e, dsth=dsth, su_sc=su_sc:
                          e.scalar_tensor_tensor(dsth, c.psB[:, 0:SP_], su_sc,
                                                 sgb[:, 0:SP_], ALU.mult, ALU.mult),
                          waits=hw)
            psu_free = v1
            sg_free = v1
            h_toks.append(v1)

        def midT(ct, tt):
            return hT[:, ct * SP_ + tt * P: ct * SP_ + (tt + 1) * P]

        def wp_blk(ct0, ng, e2c):
            return wd_hbm[ct0 * P:(ct0 + ng) * P,
                          e2c * cfg.WD:(e2c + 1) * cfg.WD].rearrange(
                "(g p) w -> p g w", p=P)
        ep = epilogue(c, PI, NP, midT, NO * P, wp_blk, None, resid_hbm, z_hbm,
                      merge(deps, *h_toks), "fe", "sync",
                      ps_free=merge(*h_toks, psg_free, psu_free), eb=eb)
        hT_readers = ep["mm"]
        ep_prev = ep
        out_toks.append(ln_phase(c, PI, NP, z_hbm, out_hbm, ep["st"], lb))
    return merge(*out_toks)



def _mk_eb(c, tag, wb=None):
    return {
        "wb": wb or [c.sbuf(f"{tag}_wb{i}", (P, 4 * c.cfg.WD), BF16)[:, :]
                     for i in range(2)],
        "rb": [c.sbuf(f"{tag}_rb{i}", (P, c.cfg.WD), F32) for i in range(2)],
        "zb": [c.sbuf(f"{tag}_zb{i}", (P, c.cfg.WD), F32) for i in range(2)],
    }


def _mk_lb_carve(c, t, col0):
    D = c.cfg.D
    f32w = 2 * D
    return {
        "zr": [carve(t, col0 + i * f32w, f32w, (P, D), F32) for i in range(2)],
        "xo": [carve(t, col0 + (2 + i) * f32w, f32w, (P, D), F32) for i in range(2)],
        "sq": carve(t, col0 + 4 * f32w, f32w, (P, D), F32),
        "cols": carve(t, col0 + 5 * f32w, 16, (P, 8), F32),
    }


def _mk_lb_alloc(c, tag):
    D = c.cfg.D
    return {
        "zr": [c.sbuf(f"{tag}_zr{i}", (P, D), F32) for i in range(2)],
        "xo": [c.sbuf(f"{tag}_xo{i}", (P, D), F32) for i in range(2)],
        "sq": c.sbuf(f"{tag}_sq", (P, D), F32),
        "cols": c.sbuf(f"{tag}_cols", (P, 8), F32),
    }


def build_l0a(cfg):
    c = Core(cfg)
    D, ND = cfg.D, cfg.ND
    xin = c.din("xin", (cfg.S, D), F32)
    wq = c.din("wq", (D, D), F32)
    wk = c.din("wk", (D, D), F32)
    wv = c.din("wv", (D, D), F32)
    wo = c.din("wo", (D, D), F32)
    bq = c.din("bq", (P, ND), F32)
    bk = c.din("bk", (P, ND), F32)
    bv = c.din("bv", (P, ND), F32)
    bo = c.din("bo", (D,), F32)
    xout = c.dout("xout", (cfg.S, D), F32)
    zscr = c.dtmp("zscr", (cfg.S, D), F32)
    with c.stack:
        c.setup_common()
        arena = c.sbuf("arena", (P, ND * cfg.S), BF16)
        ctx = c.sbuf("ctx", (P, ND * cfg.S2), BF16)
        pool = [c.sbuf(f"pool{i}", (P, D), F32) for i in range(4)]
        xbT = c.sbuf("xbT", (P, 2 * D), BF16)
        xb = [xbT[:, 0:D], xbT[:, D:2 * D]]
        lb = {"zr": [pool[0], pool[1]], "xo": [pool[2], pool[3]],
              "sq": carve(xbT[:, :], 0, 2 * D, (P, D), F32),
              "cols": c.sbuf("lncols", (P, 8), F32)}
        eb = _mk_eb(c, "ep")
        bufs = {
            "wA": c.sbuf("wA", (P, ND * P), BF16)[:, :],
            "wK": c.sbuf("wK", (P, ND * P), BF16)[:, :],
            "m0": c.sbuf("m0", (P, cfg.S2), F32)[:, :],
            "t1b": c.sbuf("t1b", (P, cfg.S2), F32)[:, :],
            "phq": c.sbuf("phq", (P, cfg.S2), BF16)[:, :],
            "phk": c.sbuf("phk", (P, cfg.S2), BF16)[:, :],
            "vv": c.sbuf("vvb", (P, cfg.S2), BF16)[:, :],
            "kv": c.sbuf("kvb", (P, cfg.S2), BF16)[:, :],
            "kvc": c.sbuf("kvcb", (P, cfg.S2), F32)[:, :],
            "zcol": c.sbuf("zcol", (P, 1), BF16),
            "carr": c.sbuf("carr", (P, ND), F32),
        }
        btile = c.sbuf("btile", (P, 3 * ND), F32)
        bo_b = c.sbuf("bo_b", (P, D), F32)
        c.S.dma("sync", btile[:, 0:ND], bq, sem="d_b")
        c.S.dma("sync", btile[:, ND:2 * ND], bk, sem="d_b")
        c.S.dma("sync", btile[:, 2 * ND:3 * ND], bv, sem="d_b")
        t4 = c.S.dma("sync", bo_b[:], bo.unsqueeze(0).broadcast_to((P, D)), sem="d_b")
        btok = merge(t4)
        bt = {"bq": btile[:, 0:ND], "bk": btile[:, ND:2 * ND],
              "bv": btile[:, 2 * ND:3 * ND]}
        bufs["ztok"] = c.S.op("vector", lambda e: e.memset(bufs["zcol"][:], 0.0))
        ttok = transpose_in(c, xin, arena, btok, "t", pool[0:2], xb)
        w_aps = {"wq": wq, "wk": wk, "wv": wv}

        def wp_blk(ct0, ng, e2c):
            return wo[ct0 * P:(ct0 + ng) * P,
                      e2c * cfg.WD:(e2c + 1) * cfg.WD].rearrange(
                "(g p) w -> p g w", p=P)

        def bias_view(e2c):
            return bo_b[:, e2c * cfg.WD:(e2c + 1) * cfg.WD]
        fins = []
        psA_free, psB_free = None, None
        ctx_free = None
        ln_dep0 = ttok
        for H in range(2):
            res = l0_attn_half(c, H, arena, ctx, w_aps, bt, bufs,
                               merge(ttok), psA_free, psB_free, ctx_free)

            def midT(ct, tt):
                return ctx[:, ct * cfg.S2 + tt * P: ct * cfg.S2 + (tt + 1) * P]
            ep = epilogue(c, H, 2, midT, D, wp_blk, bias_view, xin, zscr,
                          res["done"], "ep", "gpsimd",
                          merge(res["psA_free"], res["psB_free"]), eb)
            psA_free = ep["ev"]
            psB_free = ep["ev"]
            ctx_free = ep["mm"]
            fins.append(ln_phase(c, H, 2, zscr, xout, merge(ep["st"], ln_dep0), lb))
        c.S.op("sync", lambda e: None, waits=merge(*fins))
        c.S.emit(c.stack)
    return c.nc


def build_ffn(cfg, NO, with_scales):
    c = Core(cfg)
    D, ND = cfg.D, cfg.ND
    I = NO * P
    xin = c.din("xin", (cfg.S, D), F32)
    wg = c.din("wg", (D, I), BF16)
    wu = c.din("wu", (D, I), BF16)
    wd = c.din("wd", (I, D), BF16)
    if with_scales:
        sg = c.din("sg", (P, NO), F32)
        su = c.din("su", (P, NO), F32)
    xout = c.dout("xout", (cfg.S, D), F32)
    zscr = c.dtmp("zscr", (cfg.S, D), F32)
    with c.stack:
        c.setup_common()
        NP = 4 if cfg.NT >= 8 else 2
        arena = c.sbuf("arena", (P, ND * cfg.S), BF16)
        hT = c.sbuf("hT", (P, NO * (cfg.S // NP)), BF16)
        pool = [c.sbuf(f"pool{i}", (P, D), F32) for i in range(4)]
        xbT = c.sbuf("xbT", (P, 2 * D), BF16)
        xb = [xbT[:, 0:D], xbT[:, D:2 * D]]
        lb = {"zr": [pool[0], pool[1]], "xo": [pool[2], pool[3]],
              "sq": carve(xbT[:, :], 0, 2 * D, (P, D), F32),
              "cols": c.sbuf("lncols", (P, 8), F32)}
        eb = _mk_eb(c, "ep")
        if with_scales:
            sgt = c.sbuf("sgt", (P, NO), F32)
            sut = c.sbuf("sut", (P, NO), F32)
            c.S.dma("sync", sgt[:], sg, sem="d_b")
            s2 = c.S.dma("sync", sut[:], su, sem="d_b")
            stok = merge(s2)
            sg_t, su_t = sgt, sut
        else:
            stok = []
            sg_t, su_t = None, None
        ttok = transpose_in(c, xin, arena, stok, "t", pool[0:2], xb)
        ftok = ffn_core(c, arena, hT, wg, wu, wd, sg_t, su_t, NO, xin, zscr, xout,
                        merge(ttok, stok), eb, lb)
        c.S.op("sync", lambda e: None, waits=ftok)
        c.S.emit(c.stack)
    return c.nc


def build_l1a(cfg):
    c = Core(cfg)
    D, ND = cfg.D, cfg.ND
    xin = c.din("xin", (cfg.S, D), F32)
    wq = c.din("wq", (D, D), F32)
    wk = c.din("wk", (D, D), F32)
    wv = c.din("wv", (D, D), F32)
    wo = c.din("wo", (D, D), F32)
    bq = c.din("bq", (P, cfg.NH), F32)
    bk = c.din("bk", (P, cfg.NH), F32)
    bv = c.din("bv", (D,), F32)
    bo = c.din("bo", (D,), F32)
    xout = c.dout("xout", (cfg.S, D), F32)
    zscr = c.dtmp("zscr", (cfg.S, D), F32)
    with c.stack:
        c.setup_common(need_ones=True, need_tri=True)
        arena = c.sbuf("arena", (P, ND * cfg.S), BF16)
        oT = c.sbuf("oT", (P, ND * cfg.S), BF16)
        xf = [carve_or_alloc(c, oT, i * 2 * D, (P, D), F32, f"xf{i}")
              for i in range(2)]
        xb = [carve_or_alloc(c, oT, 4 * D + i * D, (P, D), BF16, f"xb{i}")
              for i in range(2)]
        lb = {"zr": [carve_or_alloc(c, arena, i * 2 * D, (P, D), F32, f"zr{i}")
                     for i in range(2)],
              "xo": [carve_or_alloc(c, arena, (4 + 2 * i) * D, (P, D), F32, f"xo{i}")
                     for i in range(2)],
              "sq": carve_or_alloc(c, arena, 8 * D, (P, D), F32, "sqb"),
              "cols": c.sbuf("lncols", (P, 8), F32)}
        eb = {"wb": [carve_or_alloc(c, arena, 10 * D + i * cfg.WD, (P, cfg.WD),
                                    BF16, f"ewb{i}") for i in range(2)],
              "rb": [carve_or_alloc(c, arena, 11 * D + i * 2 * cfg.WD, (P, cfg.WD),
                                    F32, f"erb{i}") for i in range(2)],
              "zb": [carve_or_alloc(c, arena, 13 * D + i * 2 * cfg.WD, (P, cfg.WD),
                                    F32, f"ezb{i}") for i in range(4)]}
        bob = c.sbuf("bob", (P, D), F32)[:, :]
        btile = c.sbuf("btile", (P, 2 * cfg.NH), F32)
        bvb = c.sbuf("bvb", (P, D), F32)
        c.S.dma("sync", btile[:, 0:cfg.NH], bq, sem="d_b")
        c.S.dma("sync", btile[:, cfg.NH:2 * cfg.NH], bk, sem="d_b")
        btok0 = c.S.dma("sync", bvb[:], bv.unsqueeze(0).broadcast_to((P, D)),
                        sem="d_b")
        bt = {"bq": btile[:, 0:cfg.NH], "bk": btile[:, cfg.NH:2 * cfg.NH]}
        ttok = transpose_in(c, xin, arena, merge(btok0), "t", xf, xb)
        res = l1_attn(c, arena, oT, {"wq": wq, "wk": wk, "wv": wv}, bt, bvb, ttok)
        btok1 = c.S.dma("sync", bob, bo.unsqueeze(0).broadcast_to((P, D)),
                        waits=res["done"], sem="d_b")

        def midT(ct, tt_local):
            pass
        def wp_blk(ct0, ng, e2c):
            return wo[ct0 * P:(ct0 + ng) * P,
                      e2c * cfg.WD:(e2c + 1) * cfg.WD].rearrange(
                "(g p) w -> p g w", p=P)

        def bias_view(e2c):
            return bob[:, e2c * cfg.WD:(e2c + 1) * cfg.WD]
        fins = []
        ps_free = merge(res["psA_free"], res["psB_free"])
        for H in range(2):
            NTp = cfg.NT // 2

            def midT(ct, tt, H=H, NTp=NTp):
                t0 = (H * NTp + tt) * P
                return oT[:, ct * cfg.S + t0: ct * cfg.S + t0 + P]
            ep = epilogue(c, H, 2, midT, D, wp_blk, bias_view, xin, zscr,
                          merge(res["done"], btok1), "ep", "gpsimd", ps_free, eb)
            ps_free = ep["ev"]
            fins.append(ln_phase(c, H, 2, zscr, xout, ep["st"], lb))
        c.S.op("sync", lambda e: None, waits=merge(*fins))
        c.S.emit(c.stack)
    return c.nc


def build_gather(cfg, shard_specs):
    c = Core(cfg)
    with c.stack:
        lut_in = c.din("lut", (16,), F32)
        sd_in = c.din("sdown", (cfg.D,), F32)
        luts = c.sbuf("luts", (P, 16), F32)
        sdb = c.sbuf("sdb", (P, cfg.D), F32)
        c.S.dma("sync", luts[:], lut_in.unsqueeze(0).broadcast_to((P, 16)), sem="d_b")
        ctok = c.S.dma("sync", sdb[:], sd_in.unsqueeze(0).broadcast_to((P, cfg.D)),
                       sem="d_b")
        ins, outs = {}, {}
        for spec in shard_specs:
            nm = spec["name"]
            ins[nm] = c.din(nm + "_i", (spec["rows"], spec["cols"]), I32)
            outs[nm] = c.dout(nm + "_o", (spec["rows"], spec["cols"]), BF16)
        CMAX = max(s["cols"] for s in shard_specs)
        idxb = c.sbuf("idxb", (P, CMAX), BF16)
        mask = c.sbuf("maskb", (P, CMAX), I8)
        accb = c.sbuf("accb", (P, CMAX), BF16)
        idx_free = None
        acc_free = None
        last_st = None
        for spec in shard_specs:
            nm, R, Cc, mode = spec["name"], spec["rows"], spec["cols"], spec["mode"]
            r0 = 0
            while r0 < R:
                rr = min(P, R - r0)
                ld = c.S.dma("gpsimd", idxb[0:rr, 0:Cc], ins[nm][r0:r0 + rr, :],
                             waits=merge(idx_free), sem="d_i")
                if mode == "lut":
                    t0 = c.S.op("vector", lambda e, rr=rr, Cc=Cc: e.tensor_scalar(
                        accb[0:rr, 0:Cc], idxb[0:rr, 0:Cc], 0.0, luts[0:rr, 0:1],
                        ALU.mult, ALU.add),
                        waits=merge([ld], ctok, acc_free))
                    last = t0
                    for j in range(1, 16):
                        m1 = c.S.op("vector", lambda e, rr=rr, Cc=Cc, j=j:
                                    e.tensor_scalar(
                                        mask[0:rr, 0:Cc], idxb[0:rr, 0:Cc],
                                        -(j - 1.0), 0.0, ALU.add, ALU.max),
                                    waits=merge([ld], [last]))
                        last = c.S.op("vector", lambda e, rr=rr, Cc=Cc, j=j:
                                      e.copy_predicated(
                                          accb[0:rr, 0:Cc], mask[0:rr, 0:Cc],
                                          luts[0:rr, j:j + 1].broadcast_to((rr, Cc))),
                                      waits=[m1])
                elif mode == "deq":
                    last = c.S.op("vector", lambda e, rr=rr, Cc=Cc: e.tensor_scalar(
                        accb[0:rr, 0:Cc], idxb[0:rr, 0:Cc], -128.0, None, ALU.add),
                        waits=merge([ld], acc_free))
                else:
                    t0 = c.S.op("vector", lambda e, rr=rr, Cc=Cc: e.tensor_scalar(
                        accb[0:rr, 0:Cc], idxb[0:rr, 0:Cc], -128.0, None, ALU.add),
                        waits=merge([ld], acc_free))
                    last = c.S.op("vector", lambda e, rr=rr, Cc=Cc: e.tensor_tensor(
                        accb[0:rr, 0:Cc], accb[0:rr, 0:Cc], sdb[0:rr, 0:Cc], ALU.mult),
                        waits=merge([t0], ctok))
                idx_free = last
                st = c.S.dma("sync", outs[nm][r0:r0 + rr, :], accb[0:rr, 0:Cc],
                             waits=[last], sem="d_z")
                acc_free = st
                last_st = st
                r0 += rr
        c.S.op("sync", lambda e: None, waits=merge(last_st))
        c.S.emit(c.stack)
    return c.nc




def queue_gather(c, groups, luts_bf, dls, l0col, sdb, gbuf, deps):
    S = c.S
    NSC = 10
    st = {"par": 0, "st": [None, None], "sc": [None, None], "sg": [None, None]}
    out = {}

    def tile_thunk(in_ap, out_ap, r0, rr, c0, c1, mode, toks, final):
        par = st["par"]
        st["par"] ^= 1
        ib = gbuf["ib"][par]
        acc = gbuf["acc"][par]
        sb = gbuf["s"]
        w = c1 - c0
        ld = S.dma("gpsimd", ib[0:rr, 0:w], in_ap[r0:r0 + rr, c0:c1],
                   waits=merge(st["sc"][par], st["st"][par], deps),
                   sem=f"d_gi{par}")
        if mode == "lut":
            S.op("vector", lambda e: e.tensor_scalar(
                acc[0:rr, 0:w], ib[0:rr, 0:w], 0.0, l0col[0:rr, 0:1],
                ALU.mult, ALU.add),
                waits=merge([ld], st["st"][0], st["st"][1], st["sc"][par]),
                quiet=True)
            for j in range(1, 16):
                sl = j % 2
                S.op("vector", lambda e, j=j, sl=sl: e.tensor_scalar(
                    sb[sl][0:rr, 0:w], ib[0:rr, 0:w], j - 0.5,
                    dls[0:rr, j:j + 1], ALU.is_ge, ALU.mult),
                    waits=[], quiet=True)
                qt = (j == 15)
                last = S.op("vector", lambda e, sl=sl: e.tensor_tensor(
                    acc[0:rr, 0:w], acc[0:rr, 0:w], sb[sl][0:rr, 0:w], ALU.add),
                    waits=[], quiet=not qt)
            st["sg"][0] = last
            st["sg"][1] = last
            st["sc"][par] = last
        elif mode == "deq":
            last = S.op("vector", lambda e: e.tensor_scalar(
                acc[0:rr, 0:w], ib[0:rr, 0:w], -128.0, None, ALU.add),
                waits=merge([ld], st["st"][0], st["st"][1], st["sc"][par]))
            st["sc"][par] = last
        else:
            t0 = S.op("vector", lambda e: e.tensor_scalar(
                acc[0:rr, 0:w], ib[0:rr, 0:w], -128.0, None, ALU.add),
                waits=merge([ld], st["st"][0], st["st"][1], st["sc"][par]))
            last = S.op("vector", lambda e, c0=c0, c1=c1: e.tensor_tensor(
                acc[0:rr, 0:w], acc[0:rr, 0:w], sdb[0:rr, c0:c1], ALU.mult),
                waits=merge([t0], deps))
            st["sc"][par] = last
        stk = S.dma("sync", out_ap[r0:r0 + rr, c0:c1], acc[0:rr, 0:w],
                    waits=[last], sem=f"d_gs{par}")
        st["st"][par] = stk
        if final:
            toks.append((c1, merge(stk, S.tokv(f"d_gs{par ^ 1}"))))

    CW = 1024
    for specs in groups:
        for nm, _, _, _, _, _ in specs:
            out[nm] = []
        nch = max((spec[4] + CW - 1) // CW for spec in specs)
        for ch in range(nch):
            for (nm, in_ap, out_ap, R, Cc, mode) in specs:
                c0, c1 = ch * CW, min((ch + 1) * CW, Cc)
                if c0 >= Cc:
                    continue
                toks = out[nm]
                nrt = (R + P - 1) // P
                for rt in range(nrt):
                    r0 = rt * P
                    rr = min(P, R - r0)
                    c.thunks.append(
                        lambda in_ap=in_ap, out_ap=out_ap, r0=r0, rr=rr,
                        c0=c0, c1=c1, mode=mode, toks=toks,
                        final=(rt == nrt - 1):
                        tile_thunk(in_ap, out_ap, r0, rr, c0, c1, mode, toks,
                                   final))
    return out


def ready(c, toks, limit):
    while not toks or toks[-1][0] < limit:
        if not c.thunks:
            raise RuntimeError(f"gather thunks exhausted before limit {limit}")
        c.pump(1)
    got = []
    for (lim, t) in toks:
        got.append(t)
        if lim >= limit:
            break
    return merge(*got)


def build_fused(cfg):
    c = Core(cfg)
    D, ND, NH = cfg.D, cfg.ND, cfg.NH
    I0, I1 = cfg.I0, cfg.I1
    SQ = cfg.S
    xin = c.din("xin", (SQ, D), F32)
    w0 = {k: c.din("w0" + k, (D, D), F32) for k in ("q", "k", "v", "o")}
    b0q = c.din("b0q", (P, ND), F32)
    b0k = c.din("b0k", (P, ND), F32)
    b0v = c.din("b0v", (P, ND), F32)
    b0o = c.din("b0o", (D,), F32)
    lut_in = c.din("lut", (16,), F32)
    g0g_i = c.din("g0g_i", (D, I0), I32)
    g0u_i = c.din("g0u_i", (D, I0), I32)
    g0d_i = c.din("g0d_i", (I0, D), I32)
    w1 = {k: c.din("w1" + k, (D, D), F32) for k in ("q", "k", "v", "o")}
    b1q = c.din("b1q", (P, NH), F32)
    b1k = c.din("b1k", (P, NH), F32)
    b1v = c.din("b1v", (D,), F32)
    b1o = c.din("b1o", (D,), F32)
    g1g_i = c.din("g1g_i", (D, I1), I32)
    g1u_i = c.din("g1u_i", (D, I1), I32)
    g1d_i = c.din("g1d_i", (I1, D), I32)
    sg_in = c.din("sg", (P, cfg.NO1), F32)
    su_in = c.din("su", (P, cfg.NO1), F32)
    sd_in = c.din("sdown", (D,), F32)
    xout = c.dout("xout", (SQ, D), F32)
    x1 = c.dtmp("x1", (SQ, D), F32)
    x2 = c.dtmp("x2", (SQ, D), F32)
    x3 = c.dtmp("x3", (SQ, D), F32)
    zscr = c.dtmp("zscr", (SQ, D), F32)
    wg0 = c.dtmp("wg0", (D, I0), BF16)
    wu0 = c.dtmp("wu0", (D, I0), BF16)
    wd0 = c.dtmp("wd0", (I0, D), BF16)
    wg1 = c.dtmp("wg1", (D, I1), BF16)
    wu1 = c.dtmp("wu1", (D, I1), BF16)
    wd1 = c.dtmp("wd1", (I1, D), BF16)

    with c.stack:
        c.setup_common(need_ones=True, need_tri=True)
        A = c.sbuf("A", (P, ND * SQ), BF16)
        M = c.sbuf("M", (P, max(ND * SQ, 31232)), BF16)
        U = c.sbuf("U", (P, 32192), BF16)
        xf = [carve(U[:, :], i * 2 * D, 2 * D, (P, D), F32) for i in range(2)]
        xb = [U[:, 4 * D + i * D: 4 * D + (i + 1) * D] for i in range(2)]
        sq_b = carve(U[:, :], 4 * D, 4 * D, (P, D), F32)
        lb = {"zr": xf, "sq": sq_b, "cols": c.sbuf("lncols", (P, 8), F32)}
        EB0 = 15808
        eb = {"wb": [U[:, EB0 + i * 2048:EB0 + i * 2048 + 4 * cfg.WD]
                     for i in range(2)],
              "rb": [carve(U[:, :], EB0 + 4096 + i * 1024, 1024, (P, cfg.WD), F32)
                     for i in range(2)],
              "zb": [carve(U[:, :], EB0 + 6144 + i * 1024, 1024, (P, cfg.WD), F32)
                     for i in range(2)]}
        BIAS0 = EB0 + 8192
        bias_t = carve(U[:, :], BIAS0, 2 * D, (P, D), F32)
        G0 = BIAS0 + 2 * D
        gbuf = {
            "ib": [U[:, G0 + i * 1024:G0 + (i + 1) * 1024] for i in range(2)],
            "acc": [U[:, G0 + 2048:G0 + 3072], U[:, G0 + 2048:G0 + 3072]],
            "s": [U[:, G0 + 3072:G0 + 4096], U[:, G0 + 3072:G0 + 4096]],
        }
        HPG_ = min(cfg.NH, 2)
        VW_ = HPG_ * cfg.HD
        l1b = {
            "vg": U[:, 0:0 + cfg.NT * VW_],
            "wvb": U[:, 4096:4096 + ND * VW_],
            "qh": U[:, 8192:8192 + SQ],
            "kh": U[:, 10240:10240 + SQ],
            "wqb": [U[:, 12288 + i * 2048:12288 + i * 2048 + ND * P]
                    for i in range(2)],
            "wkb": [U[:, 16384 + i * 2048:16384 + i * 2048 + ND * P]
                    for i in range(2)],
            "pT": [U[:, 20480 + i * 512:20480 + i * 512 + cfg.W] for i in range(2)],
            "rr": carve(U[:, :], 21504, 1024, (P, cfg.W), F32),
        }
        ctx = M[:, 0:ND * cfg.S2]
        SC0 = ND * cfg.S2
        l0bufs = {
            "wA": M[:, SC0:SC0 + ND * P],
            "wK": M[:, SC0 + 2048:SC0 + 2048 + ND * P],
            "m0": carve(M[:, :], SC0 + 4096, 2048, (P, cfg.S2), F32),
            "t1b": carve(M[:, :], SC0 + 6144, 2048, (P, cfg.S2), F32),
            "phq": M[:, SC0 + 8192:SC0 + 8192 + cfg.S2],
            "phk": M[:, SC0 + 9216:SC0 + 9216 + cfg.S2],
            "vv": M[:, SC0 + 10240:SC0 + 10240 + cfg.S2],
            "kv": M[:, SC0 + 11264:SC0 + 11264 + cfg.S2],
            "kvc": carve(M[:, :], SC0 + 12288, 2048, (P, cfg.S2), F32),
            "zcol": c.sbuf("zcol", (P, 1), BF16),
            "carr": c.sbuf("carr", (P, ND), F32),
        }
        FW0 = 22528
        ffnb = {
            "gw": [M[:, FW0 + i * 2048:FW0 + i * 2048 + ND * P] for i in range(2)],
            "uw": [M[:, FW0 + 4096 + i * 2048:FW0 + 4096 + i * 2048 + ND * P]
                   for i in range(2)],
            "sgb": M[:, FW0 + 8192:FW0 + 8192 + min(512, SQ // 2)],
        }
        btile = c.sbuf("btile", (P, 3 * ND + 2 * NH), F32)
        sgt = c.sbuf("sgt", (P, cfg.NO1), F32)
        sut = c.sbuf("sut", (P, cfg.NO1), F32)
        sdb = c.sbuf("sdb", (P, D), F32)
        luts = c.sbuf("luts", (P, 16), F32)
        c.S.dma("sync", btile[:, 0:ND], b0q, sem="d_b")
        c.S.dma("sync", btile[:, ND:2 * ND], b0k, sem="d_b")
        c.S.dma("sync", btile[:, 2 * ND:3 * ND], b0v, sem="d_b")
        c.S.dma("sync", btile[:, 3 * ND:3 * ND + NH], b1q, sem="d_b")
        c.S.dma("sync", btile[:, 3 * ND + NH:3 * ND + 2 * NH], b1k, sem="d_b")
        c.S.dma("sync", sgt[:], sg_in, sem="d_b")
        c.S.dma("sync", sut[:], su_in, sem="d_b")
        c.S.dma("sync", sdb[:], sd_in.unsqueeze(0).broadcast_to((P, D)), sem="d_b")
        btok = c.S.dma("sync", luts[:], lut_in.unsqueeze(0).broadcast_to((P, 16)),
                       sem="d_b")
        bt0 = {"bq": btile[:, 0:ND], "bk": btile[:, ND:2 * ND],
               "bv": btile[:, 2 * ND:3 * ND]}
        bt1 = {"bq": btile[:, 3 * ND:3 * ND + NH],
               "bk": btile[:, 3 * ND + NH:3 * ND + 2 * NH]}
        l0bufs["ztok"] = c.S.op("vector", lambda e: e.memset(l0bufs["zcol"][:], 0.0))

        dls = c.sbuf("dls", (P, 16), F32)
        luts_bf = c.sbuf("lutsbf", (P, 16), BF16)
        l0col = c.sbuf("l0col", (P, 1), F32)
        d0 = c.S.op("vector", lambda e: e.tensor_copy(luts_bf[:, :], luts[:, :]),
                    waits=merge(btok))
        d1 = c.S.op("vector", lambda e: e.tensor_tensor(
            dls[:, 1:16], luts_bf[:, 1:16], luts_bf[:, 0:15], ALU.subtract),
            waits=[d0])
        d2 = c.S.op("vector", lambda e: e.tensor_copy(l0col[:, :],
                                                      luts_bf[:, 0:1]),
                    waits=[d0])
        gdep = merge(btok, [d1], [d2])
        groups = [
            [("wg0", g0g_i, wg0, D, I0, "lut"), ("wu0", g0u_i, wu0, D, I0, "lut")],
            [("wd0", g0d_i, wd0, I0, D, "lut")],
            [("wg1", g1g_i, wg1, D, I1, "deq"), ("wu1", g1u_i, wu1, D, I1, "deq")],
            [("wd1", g1d_i, wd1, I1, D, "deqscale")],
        ]
        gtk = queue_gather(c, groups, luts_bf, dls, l0col, sdb, gbuf, gdep)

        def bias_view(e2c):
            return bias_t[:, e2c * cfg.WD:(e2c + 1) * cfg.WD]

        ttok = transpose_in(c, xin, A, [], "t", xf, xb)
        bias_tok = c.S.dma("sync", bias_t, b0o.unsqueeze(0).broadcast_to((P, D)),
                           waits=merge(ttok), sem="d_bias")
        psA_free, psB_free, ctx_free = None, None, None
        fins = []
        for H in range(2):
            res = l0_attn_half(c, H, A, ctx, {"wq": w0["q"], "wk": w0["k"],
                                              "wv": w0["v"]}, bt0, l0bufs,
                               merge(ttok, btok), psA_free, psB_free, ctx_free)

            def midT(ct, tt):
                return ctx[:, ct * cfg.S2 + tt * P: ct * cfg.S2 + (tt + 1) * P]

            def wp_blk(ct0, ng, e2c, w=w0["o"]):
                return w[ct0 * P:(ct0 + ng) * P,
                         e2c * cfg.WD:(e2c + 1) * cfg.WD].rearrange(
                    "(g p) w -> p g w", p=P)
            ep = epilogue(c, H, 2, midT, D, wp_blk, bias_view, xin, zscr,
                          merge(res["done"], [bias_tok]), "ep", "gpsimd",
                          merge(res["psA_free"], res["psB_free"]), eb)
            psA_free = ep["ev"]
            psB_free = ep["ev"]
            ctx_free = ep["mm"]
            fins.append(ln_phase(c, H, 2, zscr, x1, ep["st"], lb))
        chain = merge(*fins)

        ttok = transpose_in(c, x1, A, chain, "t", xf, xb)
        ftok = ffn_core_f(c, A, M, wg0, wu0, wd0, None, None, cfg.NO0,
                          x1, zscr, x2, merge(ttok),
                          eb, lb, ffnb, psA_free, psB_free,
                          gtk={"g": gtk["wg0"], "u": gtk["wu0"], "d": gtk["wd0"]})
        chain = ftok["done"]
        psA_free, psB_free = ftok["psA_free"], ftok["psB_free"]

        ttok = transpose_in(c, x2, A, chain, "t", xf, xb)
        bvb_tok = c.S.dma("sync", bias_t, b1v.unsqueeze(0).broadcast_to((P, D)),
                          waits=merge(chain, [bias_tok]), sem="d_bias")
        res = l1_attn_f(c, A, M, {"wq": w1["q"], "wk": w1["k"], "wv": w1["v"]},
                        bt1, bias_t, merge(ttok, [bvb_tok]), l1b,
                        psA_free, psB_free)
        bias_tok = c.S.dma("sync", bias_t, b1o.unsqueeze(0).broadcast_to((P, D)),
                           waits=res["done"], sem="d_bias")
        fins = []
        ps_free = merge(res["psA_free"], res["psB_free"])
        for H in range(2):
            NTp = cfg.NT // 2

            def midT(ct, tt, H=H, NTp=NTp):
                t0 = (H * NTp + tt) * P
                return M[:, ct * SQ + t0: ct * SQ + t0 + P]

            def wp_blk(ct0, ng, e2c, w=w1["o"]):
                return w[ct0 * P:(ct0 + ng) * P,
                         e2c * cfg.WD:(e2c + 1) * cfg.WD].rearrange(
                    "(g p) w -> p g w", p=P)
            ep = epilogue(c, H, 2, midT, D, wp_blk, bias_view, x2, zscr,
                          merge(res["done"], [bias_tok]), "ep", "gpsimd",
                          ps_free, eb)
            ps_free = ep["ev"]
            fins.append(ln_phase(c, H, 2, zscr, x3, ep["st"], lb))
        chain = merge(*fins)
        psA_free = ps_free
        psB_free = ps_free

        ttok = transpose_in(c, x3, A, chain, "t", xf, xb)
        ftok = ffn_core_f(c, A, M, wg1, wu1, wd1, sgt, sut, cfg.NO1,
                          x3, zscr, xout, merge(ttok),
                          eb, lb, ffnb, psA_free, psB_free,
                          gtk={"g": gtk["wg1"], "u": gtk["wu1"], "d": gtk["wd1"]})
        c.S.op("sync", lambda e: None, waits=ftok["done"])
        c.S.emit(c.stack)
    return c.nc


def ffn_core_f(c, arena, M, wg_hbm, wu_hbm, wd_hbm, sg_t, su_t, NO,
               resid_hbm, z_hbm, out_hbm, deps, eb, lb, ffnb,
               psA_free0, psB_free0, gtk=None):
    cfg, S = c.cfg, c.S
    D, ND = cfg.D, cfg.ND
    SQ = cfg.S
    NP = 4 if cfg.NT >= 8 else 2
    SP_ = SQ // NP
    WH = min(cfg.W, SP_)
    nW2 = SP_ // WH
    hT = M[:, 0:NO * SP_]
    gw, uw, sgb = ffnb["gw"], ffnb["uw"], ffnb["sgb"]
    gtok = ffnb.setdefault("gtok", [None, None])
    utok = ffnb.setdefault("utok", [None, None])
    out_toks = []
    ep_prev = None
    hT_readers = ffnb.get("hT_readers")
    psg_free0 = psA_free0
    psu_free0 = psB_free0
    for PI in range(NP):
        psg_free = ep_prev["ev"] if ep_prev else psg_free0
        psu_free = ep_prev["ev"] if ep_prev else psu_free0
        sg_free = ffnb.get("sg_free")
        h_toks = []
        for ot in range(NO):
            b = ot % 2
            srcg = wg_hbm.rearrange("(nj p) o -> p nj o", p=P)[
                :, :, ot * P:(ot + 1) * P]
            srcu = wu_hbm.rearrange("(nj p) o -> p nj o", p=P)[
                :, :, ot * P:(ot + 1) * P]
            gw_rdy = ready(c, gtk["g"], (ot + 1) * P) if gtk else []
            gu_rdy = ready(c, gtk["u"], (ot + 1) * P) if gtk else []
            lg = S.dma("gpsimd", gw[b].rearrange("p (nj o) -> p nj o", nj=ND),
                       srcg, waits=merge(deps, gtok[b], gw_rdy), sem=f"d_gg{b}")
            lu = S.dma("gpsimd", uw[b].rearrange("p (nj o) -> p nj o", nj=ND),
                       srcu, waits=merge(deps, utok[b], gu_rdy), sem=f"d_gu{b}")
            c.pump(2)

            def mk(wl, ps, b, PI=PI):
                def em(e):
                    inst = None
                    for j in range(ND):
                        for cc in range(nW2):
                            c0 = PI * SP_ + cc * WH
                            inst = e.matmul(ps[:, cc * WH:(cc + 1) * WH],
                                            wl[b][:, j * P:(j + 1) * P],
                                            arena[:, j * SQ + c0: j * SQ + c0 + WH],
                                            start=(j == 0), stop=(j == ND - 1))
                    return inst
                return em
            mmg = S.op("tensor", mk(gw, c.psA, b), waits=merge([lg], deps, psg_free))
            gtok[b] = mmg
            mmu = S.op("tensor", mk(uw, c.psB, b), waits=merge([lu], deps, psu_free))
            utok[b] = mmu
            sg_sc = sg_t[:, ot:ot + 1] if sg_t is not None else 1.0
            s1 = S.op("scalar", lambda e, sg_sc=sg_sc: e.activation(
                sgb[:, 0:SP_], c.psA[:, 0:SP_], AF.Silu, bias=0.0, scale=sg_sc),
                waits=merge([mmg], sg_free))
            psg_free = s1
            dsth = hT[:, ot * SP_:(ot + 1) * SP_]
            hw = merge([mmu], [s1], hT_readers)
            if su_t is None:
                v1 = S.op("vector", lambda e, dsth=dsth: e.tensor_tensor(
                    dsth, c.psB[:, 0:SP_], sgb[:, 0:SP_], ALU.mult), waits=hw)
            else:
                su_sc = su_t[:, ot:ot + 1]
                v1 = S.op("vector", lambda e, dsth=dsth, su_sc=su_sc:
                          e.scalar_tensor_tensor(dsth, c.psB[:, 0:SP_], su_sc,
                                                 sgb[:, 0:SP_], ALU.mult, ALU.mult),
                          waits=hw)
            psu_free = v1
            sg_free = v1
            h_toks.append(v1)

        def midT(ct, tt):
            return hT[:, ct * SP_ + tt * P: ct * SP_ + (tt + 1) * P]

        def wp_blk(ct0, ng, e2c):
            return wd_hbm[ct0 * P:(ct0 + ng) * P,
                          e2c * cfg.WD:(e2c + 1) * cfg.WD].rearrange(
                "(g p) w -> p g w", p=P)
        ep = epilogue(c, PI, NP, midT, NO * P, wp_blk, None, resid_hbm, z_hbm,
                      merge(deps, *h_toks), "fe", "sync",
                      ps_free=merge(*h_toks, psg_free, psu_free), eb=eb,
                      wready=(lambda e2c: ready(c, gtk["d"], (e2c + 1) * cfg.WD))
                      if gtk else None)
        hT_readers = ep["mm"]
        ffnb["hT_readers"] = hT_readers
        ffnb["sg_free"] = sg_free
        ep_prev = ep
        out_toks.append(ln_phase(c, PI, NP, z_hbm, out_hbm, ep["st"], lb))
    return dict(done=merge(*out_toks), psA_free=ep_prev["ev"],
                psB_free=ep_prev["ev"])


def l1_attn_f(c, arena, oT, w_aps, bt, bv_bcast, deps, l1b,
              psA_free0, psB_free0):
    cfg, S = c.cfg, c.S
    D, ND, NT, NH = cfg.D, cfg.ND, cfg.NT, cfg.NH
    SQ = cfg.S
    HPG = min(cfg.NH, 2)
    VW = HPG * cfg.HD
    nW = SQ // cfg.W
    NQ = SQ // cfg.W
    perA = c.psA.shape[1] // cfg.W
    vg, wvb, qh, kh = l1b["vg"], l1b["wvb"], l1b["qh"], l1b["kh"]
    wqb, wkb, pT, rr = l1b["wqb"], l1b["wkb"], l1b["pT"], l1b["rr"]
    wq_tok = [None, None]
    wk_tok = [None, None]
    wv_tok = None
    psA_free = psA_free0
    psB_free = psB_free0
    sslot_free = [psB_free0, psB_free0]
    rslot_free = [psB_free0, psB_free0]
    pT_free = [None, None]
    rr_free = None
    qh_free = None
    kh_free = None
    o_toks = []
    for g in range(NH // HPG):
        srcv = w_aps["wv"].rearrange("(nj p) d -> p nj d", p=P)[
            :, :, g * VW:(g + 1) * VW]
        lv = S.dma("gpsimd", wvb.rearrange("p (nj d) -> p nj d", nj=ND), srcv,
                   waits=merge(deps, wv_tok), sem="d_wv0")
        vg_evs = []
        for tt in range(NT):
            ps = c.ps512(tt % perA)

            def em(e, tt=tt, ps=ps):
                inst = None
                for j in range(ND):
                    inst = e.matmul(ps[:, 0:VW],
                                    arena[:, j * SQ + tt * P: j * SQ + (tt + 1) * P],
                                    wvb[:, j * VW:(j + 1) * VW],
                                    start=(j == 0), stop=(j == ND - 1))
                return inst
            mmv = S.op("tensor", em, waits=merge([lv], deps,
                                                 psA_free if tt < perA else None,
                                                 vg_evs[tt - perA] if tt >= perA else None))
            ev = S.op("vector", lambda e, tt=tt, ps=ps, g=g: e.tensor_tensor(
                vg[:, tt * VW:(tt + 1) * VW], ps[:, 0:VW],
                bv_bcast[:, g * VW:(g + 1) * VW], ALU.add),
                waits=merge([mmv], wv_tok, deps))
            vg_evs.append(ev)
        wv_tok = merge(*vg_evs)
        psA_free = wv_tok
        head_evs = []
        for hh in range(HPG):
            h = g * HPG + hh
            b = h % 2
            srcq = w_aps["wq"].rearrange("(nj p) d -> p nj d", p=P)[
                :, :, h * P:(h + 1) * P]
            srck = w_aps["wk"].rearrange("(nj p) d -> p nj d", p=P)[
                :, :, h * P:(h + 1) * P]
            lq = S.dma("gpsimd", wqb[b].rearrange("p (nj d) -> p nj d", nj=ND),
                       srcq, waits=merge(deps, wq_tok[b]), sem=f"d_wq{b}")
            lk = S.dma("gpsimd", wkb[b].rearrange("p (nj d) -> p nj d", nj=ND),
                       srck, waits=merge(deps, wk_tok[b]), sem=f"d_wk{b}")

            def mkqk(wl, ps, b):
                def em(e):
                    inst = None
                    for j in range(ND):
                        for cc in range(nW):
                            inst = e.matmul(ps[:, cc * cfg.W:(cc + 1) * cfg.W],
                                            wl[b][:, j * P:(j + 1) * P],
                                            arena[:, j * SQ + cc * cfg.W:
                                                  j * SQ + (cc + 1) * cfg.W],
                                            start=(j == 0), stop=(j == ND - 1))
                    return inst
                return em
            mmq = S.op("tensor", mkqk(wqb, c.psA, b),
                       waits=merge([lq], deps, psA_free, qh_free))
            wq_tok[b] = mmq
            sq = S.op("scalar", lambda e, h=h: e.activation(
                qh[:, 0:SQ], c.psA[:, 0:SQ], AF.Identity, bias=bt["bq"][:, h:h + 1],
                scale=1.0),
                waits=merge([mmq], qh_free))
            mmk = S.op("tensor", mkqk(wkb, c.psB, b),
                       waits=merge([lk], deps, psB_free, kh_free,
                                   sslot_free[0], sslot_free[1],
                                   rslot_free[0], rslot_free[1]))
            wk_tok[b] = mmk
            sk = S.op("scalar", lambda e, h=h: e.activation(
                kh[:, 0:SQ], c.psB[:, 0:SQ], AF.Identity, bias=bt["bk"][:, h:h + 1],
                scale=1.0),
                waits=merge([mmk], kh_free))
            psB_free = sk
            evs = []
            for qg in range(NQ):
                c.pump(2)
                q0 = qg * cfg.W
                jmax = min(NT - 1, (q0 + cfg.W - 1) // P)
                rs_i = 2 + (qg % 2)
                rs = c.psB[:, rs_i * cfg.W:(rs_i + 1) * cfg.W]
                last_p = None
                for j in range(jmax + 1):
                    sl = j % 2
                    sT = c.psB[:, sl * cfg.W:(sl + 1) * cfg.W]
                    mm1 = S.op("tensor", lambda e, j=j, q0=q0, sT=sT: e.matmul(
                        sT[:, 0:cfg.W], kh[:, j * P:(j + 1) * P], qh[:, q0:q0 + cfg.W],
                        start=True, stop=True),
                        waits=merge([sq], [sk], sslot_free[sl]))
                    pb = pT[sl]
                    se = S.op("scalar", lambda e, pb=pb, sT=sT: e.activation(
                        pb[:, 0:cfg.W], sT[:, 0:cfg.W], AF.Exp, bias=0.0,
                        scale=cfg.SCALE),
                        waits=merge([mm1], pT_free[sl]))
                    sslot_free[sl] = se
                    off = j * P - q0
                    mtoks = [se]
                    if off > 0:
                        z1 = S.op("vector", lambda e, pb=pb, off=off:
                                  e.memset(pb[:, 0:off], 0.0), waits=[se])
                        mtoks.append(z1)
                    if 0 <= off <= cfg.W - P:
                        z2 = S.op("vector", lambda e, pb=pb, off=off: e.tensor_tensor(
                            pb[:, off:off + P], pb[:, off:off + P], c.tri[:], ALU.mult),
                            waits=merge([se], c.const_tok))
                        mtoks.append(z2)
                    ptok = merge(*mtoks)
                    mm2 = S.op("tensor", lambda e, pb=pb, rs=rs, j=j, jmax=jmax:
                               e.matmul(rs[:, 0:cfg.W], c.ones[:], pb[:, 0:cfg.W],
                                        start=(j == 0), stop=(j == jmax)),
                               waits=merge(ptok, c.const_tok,
                                           rslot_free[rs_i - 2] if j == 0 else None))
                    vsl = vg[:, j * VW + hh * cfg.HD: j * VW + hh * cfg.HD + P]
                    mm3 = S.op("tensor", lambda e, vsl=vsl, pb=pb, q0=q0, j=j, jmax=jmax:
                               e.matmul(c.psA[:, q0:q0 + cfg.W], vsl, pb[:, 0:cfg.W],
                                        start=(j == 0), stop=(j == jmax)),
                               waits=merge(ptok, wv_tok, [sq] if j == 0 else None))
                    last_p = merge([mm2], [mm3])
                    pT_free[sl] = last_p
                v1 = S.op("vector", lambda e, rs=rs: e.reciprocal(rr, rs[:, 0:cfg.W]),
                          waits=merge(last_p, rr_free))
                rslot_free[rs_i - 2] = v1
                dst = oT[:, h * SQ + q0: h * SQ + q0 + cfg.W]
                ev = S.op("vector", lambda e, dst=dst, q0=q0:
                          e.scalar_tensor_tensor(dst, c.psA[:, q0:q0 + cfg.W], 1.0,
                                                 rr, ALU.mult, ALU.mult),
                          waits=merge([v1], last_p))
                rr_free = ev
                evs.append(ev)
            psA_free = merge(*evs)
            qh_free = psA_free
            kh_free = psA_free
            head_evs.extend(evs)
            o_toks.extend(evs)
        wv_tok = merge(wv_tok, *head_evs)
    return dict(done=merge(*o_toks), psA_free=psA_free, psB_free=merge(
        sslot_free[0], sslot_free[1], rslot_free[0], rslot_free[1], psB_free))



def _colmajor(v, nd):
    return np.ascontiguousarray(np.asarray(v, np.float32).reshape(nd, P).T)


def kernel(L0_attn_x, L0_attn_Wq, L0_attn_bq, L0_attn_Wk, L0_attn_bk,
           L0_attn_Wv, L0_attn_bv, L0_attn_Wo, L0_attn_bo,
           L0_ffn_lut, L0_ffn_gate_idx, L0_ffn_up_idx, L0_ffn_down_idx,
           L1_attn_wq, L1_attn_bq, L1_attn_wk, L1_attn_bk, L1_attn_wv,
           L1_attn_bv, L1_attn_wo, L1_attn_bo,
           L1_ffn_wq_gate, L1_ffn_s_gate, L1_ffn_wq_up, L1_ffn_s_up,
           L1_ffn_wq_down, L1_ffn_s_down):
    cfg = FULL
    B = NCORES
    ident = np.eye(P, dtype=np.float32)
    ones = np.ones((P, P), dtype=np.float32)
    tri = np.triu(np.ones((P, P), dtype=np.float32))
    cv = [1e-5, 1.0 / cfg.D, -0.5, cfg.SCALE, 128.0]
    for j in range(1, 11):
        cv.append(-128.0 * (j - 0.5))
    cvals = np.array(cv, dtype=np.float32)

    def T(a):
        return np.ascontiguousarray(np.asarray(a).T)

    x = np.ascontiguousarray(np.asarray(L0_attn_x, np.float32))
    nc = build_fused(cfg)
    common = {
        "w0q": T(L0_attn_Wq), "w0k": T(L0_attn_Wk), "w0v": T(L0_attn_Wv),
        "w0o": T(L0_attn_Wo),
        "b0q": _colmajor(L0_attn_bq, cfg.ND), "b0k": _colmajor(L0_attn_bk, cfg.ND),
        "b0v": _colmajor(L0_attn_bv, cfg.ND),
        "b0o": np.asarray(L0_attn_bo, np.float32),
        "lut": np.asarray(L0_ffn_lut, np.float32),
        "g0g_i": np.ascontiguousarray(T(L0_ffn_gate_idx), dtype=np.int32),
        "g0u_i": np.ascontiguousarray(T(L0_ffn_up_idx), dtype=np.int32),
        "g0d_i": np.ascontiguousarray(T(L0_ffn_down_idx), dtype=np.int32),
        "w1q": T(L1_attn_wq), "w1k": T(L1_attn_wk), "w1v": T(L1_attn_wv),
        "w1o": T(L1_attn_wo),
        "b1q": _colmajor(L1_attn_bq, cfg.NH), "b1k": _colmajor(L1_attn_bk, cfg.NH),
        "b1v": np.asarray(L1_attn_bv, np.float32),
        "b1o": np.asarray(L1_attn_bo, np.float32),
        "g1g_i": np.ascontiguousarray(T(L1_ffn_wq_gate), dtype=np.int32),
        "g1u_i": np.ascontiguousarray(T(L1_ffn_wq_up), dtype=np.int32),
        "g1d_i": np.ascontiguousarray(T(L1_ffn_wq_down), dtype=np.int32),
        "sg": _colmajor(L1_ffn_s_gate, cfg.NO1),
        "su": _colmajor(L1_ffn_s_up, cfg.NO1),
        "sdown": np.asarray(L1_ffn_s_down, np.float32),
        "cident": ident, "cones": ones, "ctri": tri, "cvals": cvals,
    }
    in_maps = [dict(common, xin=np.ascontiguousarray(x[core]))
               for core in range(B)]
    res = run_bass_kernel_spmd(nc, in_maps, core_ids=list(range(B)))
    out = np.stack([res.results[i]["xout"] for i in range(B)], axis=0)
    return np.ascontiguousarray(out.astype(np.float32))


def kernel_unfused(L0_attn_x, L0_attn_Wq, L0_attn_bq, L0_attn_Wk, L0_attn_bk,
           L0_attn_Wv, L0_attn_bv, L0_attn_Wo, L0_attn_bo,
           L0_ffn_lut, L0_ffn_gate_idx, L0_ffn_up_idx, L0_ffn_down_idx,
           L1_attn_wq, L1_attn_bq, L1_attn_wk, L1_attn_bk, L1_attn_wv,
           L1_attn_bv, L1_attn_wo, L1_attn_bo,
           L1_ffn_wq_gate, L1_ffn_s_gate, L1_ffn_wq_up, L1_ffn_s_up,
           L1_ffn_wq_down, L1_ffn_s_down):
    cfg = FULL
    B = NCORES
    ident = np.eye(P, dtype=np.float32)
    ones = np.ones((P, P), dtype=np.float32)
    tri = np.triu(np.ones((P, P), dtype=np.float32))
    cv = [1e-5, 1.0 / cfg.D, -0.5, cfg.SCALE, 128.0]
    for j in range(1, 11):
        cv.append(-128.0 * (j - 0.5))
    cvals = np.array(cv, dtype=np.float32)

    def T(a):
        return np.ascontiguousarray(np.asarray(a).T)

    x = np.ascontiguousarray(np.asarray(L0_attn_x, np.float32))

    g_specs = [
        dict(name="g0g", rows=cfg.D // B, cols=cfg.I0, mode="lut"),
        dict(name="g0u", rows=cfg.D // B, cols=cfg.I0, mode="lut"),
        dict(name="g0d", rows=cfg.I0 // B, cols=cfg.D, mode="lut"),
        dict(name="g1g", rows=cfg.D // B, cols=cfg.I1, mode="deq"),
        dict(name="g1u", rows=cfg.D // B, cols=cfg.I1, mode="deq"),
        dict(name="g1d", rows=cfg.I1 // B, cols=cfg.D, mode="deqscale"),
    ]
    gT = {"g0g": T(L0_ffn_gate_idx), "g0u": T(L0_ffn_up_idx),
          "g0d": T(L0_ffn_down_idx),
          "g1g": T(L1_ffn_wq_gate), "g1u": T(L1_ffn_wq_up),
          "g1d": T(L1_ffn_wq_down)}
    nc_g = build_gather(cfg, g_specs)
    in_maps = []
    for core in range(B):
        m = {"lut": np.asarray(L0_ffn_lut, np.float32),
             "sdown": np.asarray(L1_ffn_s_down, np.float32)}
        for spec in g_specs:
            nm, R = spec["name"], spec["rows"]
            m[nm + "_i"] = np.ascontiguousarray(gT[nm][core * R:(core + 1) * R, :],
                                                dtype=np.int32)
        in_maps.append(m)
    res = run_bass_kernel_spmd(nc_g, in_maps, core_ids=list(range(B)))
    gath = {s["name"]: np.concatenate([res.results[i][s["name"] + "_o"]
                                       for i in range(B)], axis=0)
            for s in g_specs}

    nc_a = build_l0a(cfg)
    wqT, wkT, wvT, woT = T(L0_attn_Wq), T(L0_attn_Wk), T(L0_attn_Wv), T(L0_attn_Wo)
    in_maps = [{
        "xin": np.ascontiguousarray(x[core]),
        "wq": wqT, "wk": wkT, "wv": wvT, "wo": woT,
        "bq": _colmajor(L0_attn_bq, cfg.ND),
        "bk": _colmajor(L0_attn_bk, cfg.ND),
        "bv": _colmajor(L0_attn_bv, cfg.ND),
        "bo": np.asarray(L0_attn_bo, np.float32),
        "cident": ident, "cvals": cvals,
    } for core in range(B)]
    res = run_bass_kernel_spmd(nc_a, in_maps, core_ids=list(range(B)))
    x1 = [res.results[i]["xout"] for i in range(B)]

    nc_b = build_ffn(cfg, cfg.NO0, with_scales=False)
    in_maps = [{
        "xin": x1[core], "wg": gath["g0g"], "wu": gath["g0u"], "wd": gath["g0d"],
        "cident": ident, "cvals": cvals,
    } for core in range(B)]
    res = run_bass_kernel_spmd(nc_b, in_maps, core_ids=list(range(B)))
    x2 = [res.results[i]["xout"] for i in range(B)]

    nc_c = build_l1a(cfg)
    wqT, wkT, wvT, woT = T(L1_attn_wq), T(L1_attn_wk), T(L1_attn_wv), T(L1_attn_wo)
    in_maps = [{
        "xin": x2[core],
        "wq": wqT, "wk": wkT, "wv": wvT, "wo": woT,
        "bq": _colmajor(L1_attn_bq, cfg.NH),
        "bk": _colmajor(L1_attn_bk, cfg.NH),
        "bv": np.asarray(L1_attn_bv, np.float32),
        "bo": np.asarray(L1_attn_bo, np.float32),
        "cident": ident, "cones": ones, "ctri": tri, "cvals": cvals,
    } for core in range(B)]
    res = run_bass_kernel_spmd(nc_c, in_maps, core_ids=list(range(B)))
    x3 = [res.results[i]["xout"] for i in range(B)]

    nc_d = build_ffn(cfg, cfg.NO1, with_scales=True)
    in_maps = [{
        "xin": x3[core], "wg": gath["g1g"], "wu": gath["g1u"], "wd": gath["g1d"],
        "sg": _colmajor(L1_ffn_s_gate, cfg.NO1),
        "su": _colmajor(L1_ffn_s_up, cfg.NO1),
        "cident": ident, "cvals": cvals,
    } for core in range(B)]
    res = run_bass_kernel_spmd(nc_d, in_maps, core_ids=list(range(B)))
    out = np.stack([res.results[i]["xout"] for i in range(B)], axis=0)
    return np.ascontiguousarray(out.astype(np.float32))
